# version 01
# baseline (speedup 1.0000x reference)
import numpy as np
import ml_dtypes
import concourse.bass as bass
import concourse.mybir as mybir
from concourse.bass_utils import run_bass_kernel_spmd
from contextlib import ExitStack

F32 = mybir.dt.float32
BF16 = mybir.dt.bfloat16
I32 = mybir.dt.int32
AF = mybir.ActivationFunctionType
ALU = mybir.AluOpType

B, T, D, H = 2, 4096, 2048, 16
HPC = 4
Dh = 128
SC = 32
NSC = T // SC
CH = 16
NT = T // 128
KC = D // 128
QTOK = T // 4
EPS_LN = 1e-5
EPS_IN = 1e-6
DEBUG = False
import os
PARTS = set(os.environ.get("KB_PARTS", "all").split(","))
def has(p):
    return "all" in PARTS or p in PARTS

C_ONE, C_I128, C_EPSI, C_EPSL, C_HALF, C_ZERO, C_NEG1, C_127 = range(8)
C_MSKE, C_MSKO = 8, 9
C_AME, C_AMO = 16, 48


class Emitter:

    def __init__(self, nc, ctx):
        self.nc = nc
        self.ops = {k: [] for k in ("sync", "vector", "scalar", "tensor", "gpsimd")}
        self.sems = {}
        self.cnt = {}
        self.ctx = ctx

    def sem(self, name):
        if name not in self.sems:
            self.sems[name] = self.ctx.enter_context(self.nc.semaphore(name))
            self.cnt[name] = 0
        return self.sems[name]

    def emit(self, eng, fn):
        self.ops[eng].append(fn)

    def inc(self, name, n=1):
        self.sem(name)
        self.cnt[name] += n
        return self.cnt[name]

    def build(self):
        blk_ctx = self.nc.Block()
        block = blk_ctx.__enter__()

        def run(eng_name):
            def body(eng):
                for fn in self.ops[eng_name]:
                    fn(eng)
            return body

        block.sync(run("sync"))
        block.scalar(run("scalar"))
        block.vector(run("vector"))
        block.tensor(run("tensor"))
        block.gpsimd(run("gpsimd"))
        blk_ctx.__exit__(None, None, None)


def build_program(debug=False):
    nc = bass.Bass("TRN2", target_bir_lowering=False, detect_race_conditions=False)
    ctx = ExitStack()
    em = Emitter(nc, ctx)
    sems, cnt = em.sems, em.cnt

    dram = {}

    def din(name, shape, dtype=F32):
        dram[name] = nc.dram_tensor(name, list(shape), dtype, kind="ExternalInput").ap()
        return dram[name]

    def dout(name, shape, dtype=F32):
        dram[name] = nc.dram_tensor(name, list(shape), dtype, kind="ExternalOutput").ap()
        return dram[name]

    def dint(name, shape, dtype=BF16):
        dram[name] = nc.dram_tensor(name, list(shape), dtype, kind="Internal").ap()
        return dram[name]

    X = din("x", (T, D))
    XQ = din("xq", (QTOK, D))
    WQ = din("wq", (D, 512))
    WKVB = din("wkvb", (D, 4 * 257))
    WG1 = din("wg1", (D, 128))
    WG2 = din("wg2", (128, 512))
    WOS = din("wos", (512, D))
    WF = din("wf", (D, D))
    LNW = din("lnw", (HPC, Dh))
    LNB = din("lnb", (HPC, Dh))
    NRMW = din("nrmw", (HPC, Dh))
    CONSTS = din("consts", (128, 640))
    OUT = dout("out", (QTOK, D), F32)

    XN = dint("xnb", (NT, 128, D), BF16)
    QTD = dint("qtd", (4, 128, T), BF16)
    KRAW = dint("kraw", (NSC, 128, 132), BF16)
    VCH = dint("vch", (NSC, 128, 128), BF16)
    SGT = dint("sgt", (NSC, 128, 128), BF16)
    WQR = dint("wqraw", (KC, 128, D), BF16)
    OGT = dint("ogt", (4, 512, QTOK), BF16)
    RSI = dint("rsi", (4, QTOK, D), BF16)
    RSO = dint("rso", (QTOK, D), BF16)
    XQD = dint("xqd", (8, 128, D), BF16)

    dbg = {}
    if debug:
        for nm, src in (("dbg_xnt", XN), ("dbg_qtd", QTD), ("dbg_kraw", KRAW),
                        ("dbg_vch", VCH), ("dbg_sgt", SGT), ("dbg_wqt", WQR),
                        ("dbg_ogt", OGT), ("dbg_xq2", XQD)):
            dbg[nm] = dout(nm, src.shape, BF16)
        dbg["dbg_x2"] = dout("dbg_x2", (128, 16384), BF16)
        dbg["dbg_tc"] = dout("dbg_tc", (128, 16), F32)

    sb = {}

    def S(name, shape, dtype):
        sb[name] = ctx.enter_context(nc.sbuf_tensor(name, list(shape), dtype))
        return sb[name]

    S("wq_b", (128, KC, 512), BF16)
    S("wkvb_b", (128, KC, 4, 257), BF16)
    S("wg1_b", (128, KC, 128), BF16)
    S("wg2_b", (128, 512), BF16)
    S("lnw_s", (HPC, Dh), F32)
    S("lnb_s", (HPC, Dh), F32)
    S("nrmw_s", (HPC, Dh), F32)
    CS = S("consts_s", (128, 640), F32)

    S("xtile", (128, 2, D), F32)
    S("bnst", (128, 24), F32)
    S("p0col", (128, 2, 4), F32)
    S("xn_tok", (128, 2, D), BF16)

    def p0_gload(g):
        g.dma_start(out=sb["wq_b"][:], in_=WQ.rearrange("(k p) c -> p k c", p=128)).then_inc(sems["ldw"], 16)
        g.dma_start(out=sb["wkvb_b"][:], in_=WKVB.rearrange("(k p) (f c) -> p k f c", p=128, f=4)).then_inc(sems["ldw"], 16)
        g.dma_start(out=sb["wg1_b"][:], in_=WG1.rearrange("(k p) c -> p k c", p=128)).then_inc(sems["ldw"], 16)
        g.dma_start(out=sb["wg2_b"][:], in_=WG2).then_inc(sems["ldw"], 16)
    em.sem("ldw")
    if has("wload"):
        cnt["ldw"] += 64
        em.emit("gpsimd", p0_gload)

    def p0_small(sync):
        sync.dma_start(out=sb["lnw_s"][:], in_=LNW).then_inc(sems["ld"], 16)
        sync.dma_start(out=sb["lnb_s"][:], in_=LNB).then_inc(sems["ld"], 16)
        sync.dma_start(out=sb["nrmw_s"][:], in_=NRMW).then_inc(sems["ld"], 16)
        sync.dma_start(out=sb["consts_s"][:], in_=CONSTS).then_inc(sems["ld"], 16)
    em.sem("ld"); cnt["ld"] += 64
    em.emit("sync", p0_small)

    em.sem("ldx"); em.sem("vA"); em.sem("sA"); em.sem("xs"); em.sem("stq")

    def p0_x_loads(sync):
        for i in range(int(os.environ.get("KB_NTX", NT))):
            if i >= 2 and has("xvec"):
                sync.wait_ge(sems["vA"], 2 * (i - 1))
            sync.dma_start(out=sb["xtile"][:, i % 2, :],
                           in_=X.rearrange("(n p) d -> n p d", p=128)[i]).then_inc(sems["ldx"], 16)
    if has("xln"):
        cnt["ldx"] += 16 * NT
        em.emit("sync", p0_x_loads)

    def p0_x_vec(v):
        v.wait_ge(sems["ld"], 64)
        for i in range(NT):
            sl = i % 2
            v.wait_ge(sems["ldx"], 16 * (i + 1))
            for j in range(4):
                v.bn_stats(sb["bnst"][:, 6 * j:6 * (j + 1)],
                           sb["xtile"][:, sl, 512 * j:512 * (j + 1)]).then_inc(sems["vB"], 1)
            v.wait_ge(sems["vB"], 4 * (i + 1))
            v.bn_aggr(sb["p0col"][:, sl, 0:2], sb["bnst"][:]).then_inc(sems["vA"], 1)
            if has("xact"):
                v.wait_ge(sems["sA"], i + 1)
                if i >= 2:
                    v.wait_ge(sems["stq"], 16 * (i - 1) * (2 if debug else 1))
            v.tensor_scalar(sb["xn_tok"][:, sl, :], sb["xtile"][:, sl, :],
                            sb["p0col"][:, sl, 0:1], sb["p0col"][:, sl, 2:3],
                            ALU.subtract, ALU.mult).then_inc(sems["vA"], 1)
    if has("xln") and has("xvec"):
        cnt["vA"] += 2 * NT
        em.emit("vector", p0_x_vec)

    def p0_x_act(s):
        s.wait_ge(sems["ld"], 64)
        for i in range(NT):
            sl = i % 2
            s.wait_ge(sems["vA"], 2 * i + 1)
            s.activation(sb["p0col"][:, sl, 3:4], sb["p0col"][:, sl, 1:2], AF.Ln,
                         bias=CS[:, C_EPSL:C_EPSL + 1]).then_inc(sems["sB"], 1)
            s.wait_ge(sems["sB"], i + 1)
            s.activation(sb["p0col"][:, sl, 2:3], sb["p0col"][:, sl, 3:4], AF.Exp,
                         scale=-0.5).then_inc(sems["sA"], 1)
            s.wait_ge(sems["vA"], 2 * i + 2)
            s.dma_start(out=XN[i], in_=sb["xn_tok"][:, sl, :]).then_inc(sems["stq"], 16)
            if debug:
                s.dma_start(out=dbg["dbg_xnt"][i], in_=sb["xn_tok"][:, sl, :]).then_inc(sems["stq"], 16)

    em.sem("sB"); em.sem("vB")
    if has("xln") and has("xvec"):
        cnt["vB"] += 4 * NT
    if has("xln") and has("xact"):
        cnt["sB"] += NT
        cnt["sA"] += NT
        cnt["stq"] += 16 * NT * (2 if debug else 1)
        em.emit("scalar", p0_x_act)




    SCM = 16
    NM = 8
    S("xnt_m", (128, 2, KC, 512), BF16)
    S("sigq", (128, 512), F32)
    S("qsil", (128, 512), BF16)
    S("g1sb", (128, 2, 512), BF16)
    S("krawt", (128, 2, 132), BF16)
    S("vt", (128, 2, 128), BF16)
    S("sgtt", (128, 2, 128), BF16)
    S("etac", (128, 2), F32)
    psQ = [ctx.enter_context(nc.psum_tensor(f"psQ{i}", [128, 512], F32)) for i in range(2)]
    psKV = [ctx.enter_context(nc.psum_tensor(f"psKV{i}", [128, 512], F32)) for i in range(2)]
    psG = ctx.enter_context(nc.psum_tensor("psG", [128, 512], F32))
    psGT = [ctx.enter_context(nc.psum_tensor(f"psGT{i}", [128, 128], F32)) for i in range(2)]

    for s_ in ("ldB", "tQ", "tKV", "tG", "tGT", "tB", "sQ1", "vQ", "sKV1",
               "vKV", "sG1", "sGT1", "stB", "stG", "stq2"):
        em.sem(s_)

    if has("B"):
        def b_init(v):
            v.memset(sb["krawt"][:], 0.0).then_inc(sems["vKV"], 1)
        cnt["vKV"] += 1
        em.emit("vector", b_init)

        def b_loads(sync):
            for m in range(NM):
                for j in range(4):
                    i = 4 * m + j
                    if m >= 2 and has("Bpe") and has("Bsc"):
                        sync.wait_ge(sems["tGT"], SCM * (m - 1))
                    sync.wait_ge(sems["stq"], 16 * (i + 1) * (2 if debug else 1))
                    sync.dma_start(out=sb["xnt_m"][:, m % 2, :, 128 * j:128 * (j + 1)],
                                   in_=XN[i].rearrange("p (k f) -> p k f", k=KC),
                                   transpose=True).then_inc(sems["ldB"], 16)
        cnt["ldB"] += 16 * 4 * NM
        em.emit("sync", b_loads)
    if has("B") and has("Bpe"):

        def b_pe(t):
            t.wait_ge(sems["ldw"], 64)
            for m in range(NM):
                ms = m % 2
                t.wait_ge(sems["ldB"], 16 * 4 * (m + 1))
                xm = sb["xnt_m"][:, ms, :, :]
                for mch in range(4):
                    if 4 * m + mch >= 2 and has("Bsc"):
                        t.wait_ge(sems["vQ"], 4 * m + mch - 1)
                    mm = None
                    for k in range(KC):
                        mm = t.matmul(psQ[mch % 2][:],
                                      sb["wq_b"][:, k, 128 * mch:128 * (mch + 1)],
                                      xm[:, k, :], start=(k == 0), stop=(k == KC - 1))
                    mm.then_inc(sems["tQ"], 1)
                for sc in range(SCM):
                    if 16 * m + sc >= 2 and has("Bsc"):
                        t.wait_ge(sems["sKV1"], 16 * m + sc - 1)
                    mm = None
                    for p in range(HPC):
                        for k in range(KC):
                            mm = t.matmul(psKV[sc % 2][32 * p:32 * p + 32, 0:257],
                                          xm[:, k, 32 * sc:32 * sc + 32],
                                          sb["wkvb_b"][:, k, p, :],
                                          start=(k == 0), stop=(k == KC - 1),
                                          tile_position=(0, 32 * p))
                    mm.then_inc(sems["tKV"], 1)
                if m >= 2 and has("Bsc"):
                    t.wait_ge(sems["sG1"], m - 1)
                mm = None
                for k in range(KC):
                    mm = t.matmul(psG[:], sb["wg1_b"][:, k, :], xm[:, k, :],
                                  start=(k == 0), stop=(k == KC - 1))
                mm.then_inc(sems["tG"], 1)
                if has("Bsc"):
                    t.wait_ge(sems["sG1"], m + 1)
                for sc in range(SCM):
                    if 16 * m + sc >= 2 and has("Bsc"):
                        t.wait_ge(sems["sGT1"], 16 * m + sc - 1)
                    mm = None
                    for p in range(HPC):
                        mm = t.matmul(psGT[sc % 2][32 * p:32 * p + 32, :],
                                      sb["g1sb"][:, ms, 32 * sc:32 * sc + 32],
                                      sb["wg2_b"][:, 128 * p:128 * (p + 1)],
                                      start=True, stop=True,
                                      tile_position=(0, 32 * p))
                    mm.then_inc(sems["tGT"], 1)
        if has("Bpe"):
            cnt["tQ"] += 4 * NM
            cnt["tKV"] += SCM * NM
            cnt["tG"] += NM
            cnt["tGT"] += SCM * NM
            em.emit("tensor", b_pe)

        def b_scalar(s):
            for m in range(NM):
                ms = m % 2
                for mch in range(4):
                    s.wait_ge(sems["tQ"], 4 * m + mch + 1)
                    s.activation(sb["sigq"][:], psQ[mch % 2][:], AF.Sigmoid).then_inc(sems["sQ1"], 1)
                    s.wait_ge(sems["vQ"], 4 * m + mch + 1)
                    s.dma_start(out=QTD[mch][:, 512 * m:512 * (m + 1)],
                                in_=sb["qsil"][:]).then_inc(sems["stq2"], 16)
                    if debug:
                        s.dma_start(out=dbg["dbg_qtd"][mch][:, 512 * m:512 * (m + 1)],
                                    in_=sb["qsil"][:]).then_inc(sems["stq2"], 16)
                    s.wait_ge(sems["stq2"], cnt_stq2_after(m, mch, debug))
                for sc in range(SCM):
                    scl = sc % 2
                    g = NSC_IDX(m, sc)
                    s.wait_ge(sems["tKV"], 16 * m + sc + 1)
                    s.activation(sb["krawt"][:, scl, 0:128], psKV[sc % 2][:, 0:128],
                                 AF.Copy)
                    s.activation(sb["vt"][:, scl, :], psKV[sc % 2][:, 129:257],
                                 AF.Copy)
                    s.activation(sb["etac"][:, scl:scl + 1], psKV[sc % 2][:, 128:129],
                                 AF.Sigmoid).then_inc(sems["sKV1"], 1)
                    s.wait_ge(sems["vKV"], 16 * m + sc + 2)
                    s.dma_start(out=KRAW[g][:, 0:132],
                                in_=sb["krawt"][:, scl, :]).then_inc(sems["stB"], 16)
                    s.dma_start(out=VCH[g], in_=sb["vt"][:, scl, :]).then_inc(sems["stB"], 16)
                    if debug:
                        s.dma_start(out=dbg["dbg_kraw"][g][:, 0:132],
                                    in_=sb["krawt"][:, scl, :]).then_inc(sems["stB"], 16)
                        s.dma_start(out=dbg["dbg_vch"][g], in_=sb["vt"][:, scl, :]).then_inc(sems["stB"], 16)
                    s.wait_ge(sems["stB"], stb_after(m, sc, debug))
                s.wait_ge(sems["tG"], m + 1)
                s.activation(sb["g1sb"][:, ms, :], psG[:], AF.Copy).then_inc(sems["sG1"], 1)
                for sc in range(SCM):
                    scl = sc % 2
                    g = NSC_IDX(m, sc)
                    s.wait_ge(sems["tGT"], 16 * m + sc + 1)
                    s.activation(sb["sgtt"][:, scl, :], psGT[sc % 2][:],
                                 AF.Sigmoid).then_inc(sems["sGT1"], 1)
                    s.dma_start(out=SGT[g], in_=sb["sgtt"][:, scl, :]).then_inc(sems["stG"], 16)
                    if debug:
                        s.dma_start(out=dbg["dbg_sgt"][g], in_=sb["sgtt"][:, scl, :]).then_inc(sems["stG"], 16)
                    s.wait_ge(sems["stG"], stg_after(m, sc, debug))
        def NSC_IDX(m, sc):
            return 16 * m + sc
        def cnt_stq2_after(m, mch, dbgf):
            per = 32 if dbgf else 16
            return per * (4 * m + mch + 1)
        def stb_after(m, sc, dbgf):
            per = 64 if dbgf else 32
            return per * (16 * m + sc + 1)
        def stg_after(m, sc, dbgf):
            per = 32 if dbgf else 16
            return per * (16 * m + sc + 1)
        if has("Bpe") and has("Bsc"):
            cnt["sQ1"] += 4 * NM
            cnt["stq2"] += (32 if debug else 16) * 4 * NM
            cnt["sG1"] += NM
            cnt["sKV1"] += SCM * NM
            cnt["sGT1"] += SCM * NM
            cnt["stB"] += (64 if debug else 32) * SCM * NM
            cnt["stG"] += (32 if debug else 16) * SCM * NM
            em.emit("scalar", b_scalar)

        def b_vec(v):
            for m in range(NM):
                for mch in range(4):
                    v.wait_ge(sems["sQ1"], 4 * m + mch + 1)
                    v.tensor_tensor(sb["qsil"][:], psQ[mch % 2][:], sb["sigq"][:],
                                    ALU.mult).then_inc(sems["vQ"], 1)
                for sc in range(SCM):
                    scl = sc % 2
                    v.wait_ge(sems["sKV1"], 16 * m + sc + 1)
                    v.tensor_scalar(sb["krawt"][:, scl, 128:129],
                                    sb["etac"][:, scl:scl + 1], 2.0, None,
                                    ALU.mult).then_inc(sems["vKV"], 1)
        if has("Bpe") and has("Bsc"):
            cnt["vQ"] += 4 * NM
            cnt["vKV"] += SCM * NM
            em.emit("vector", b_vec)

    NPREP = 3
    S("krsb", (128, NPREP, 132), BF16)
    S("vsb", (128, NPREP, 128), BF16)
    S("ssb", (128, NPREP, 128), BF16)
    S("qisb", (128, NPREP, 4, 32), BF16)
    S("kisb", (128, NPREP, 128), BF16)
    S("kesb", (128, NPREP, 128), BF16)
    S("kitsb", (128, NPREP, 128), BF16)
    S("qchsb", (128, NPREP, 128), BF16)
    S("ccsb", (128, NPREP, 128), F32)
    S("cc128", (128, NPREP, 128), F32)
    S("vmk", (128, 128), BF16)
    S("ccol", (128, NPREP, 8), F32)
    S("S_sb", (128, 4, 128), BF16)
    S("khsb", (128, 128), BF16)
    S("t1sb", (128, 128), F32)
    S("redb", (128, 4, 128), F32)
    S("scr", (128, 128), F32)
    S("ats", (128, 32), F32)
    S("usb", (128, 128), BF16)
    S("ubd", (128, 512), BF16)
    S("atm", (128, 32), BF16)
    S("kcol", (128, 16), F32)
    S("ohsb", (128, 2, 128), BF16)
    S("mrg", (128, 128), BF16)
    S("ogsb", (128, 128), BF16)
    S("ogts", (128, 128), BF16)
    S("ocol", (128, 4), F32)
    S("w_bt", (128, 128), F32)
    S("w2n_bt", (128, 128), F32)
    S("b_bt", (128, 128), F32)
    S("wbb_bt", (128, 128), F32)
    S("nw_bt", (128, 128), F32)
    S("mskE_t", (128, 128), BF16)
    S("ident_t", (128, 128), BF16)
    S("ones_t", (128, 128), F32)

    psS = psQ[0]
    psCH = psGT[0]
    psA = psGT[1]
    psQHb = psKV
    psB = ctx.enter_context(nc.psum_tensor("psB", [128, 128], BF16))

    for s_ in ("vInit", "gInit", "ldp", "vPR", "sPR", "tPR", "tKH", "tA",
               "tCU", "tSU", "sE", "vR", "sR", "sCB", "vU", "vBD", "sSC",
               "sAT", "vAT", "sOH", "vOG", "vOG2", "tOG", "sEV", "stOG", "stDG", "vCX",
               "ccx", "ldo", "tWO", "sWO", "vWO", "stO", "ldw2", "tF", "ldf"):
        em.sem(s_)

    CP = int(os.environ.get("KB_NCHUNK", 2 * NSC))
    vcx_n = [0]
    NSCC = (CP + 1) // 2

    if has("C"):
        if debug:
            DBG_S = dout("dbg_S", (128, 512), BF16)
            DBG_KH = dout("dbg_kh", (8, 128, 128), BF16)
            DBG_U = dout("dbg_u", (8, 128, 128), BF16)
            DBG_KC = dout("dbg_kc", (8, 128, 16), F32)
            DBG_CC = dout("dbg_cc", (8, 128, 128), F32)
            DBG_CL = dout("dbg_cl", (8, 128, 8), F32)
            DBG_SS = dout("dbg_ss", (8, 128, 512), BF16)

        def c_init_g(g):
            g.memset(sb["ident_t"][:], 0.0)
            g.affine_select(out=sb["ident_t"][:], in_=sb["ident_t"][:],
                            compare_op=ALU.not_equal, fill=1.0, base=0,
                            pattern=[[-1, 128]], channel_multiplier=1)
            g.tensor_copy(sb["scr"][:, 0:1], sb["ident_t"][:, 0:1]).then_inc(sems["gInit"], 1)
        cnt["gInit"] += 1
        em.emit("gpsimd", c_init_g)

        def c_init_vec(v):
            v.wait_ge(sems["sPR"], 3)
            v.memset(sb["S_sb"][:], 0.0)
            v.memset(sb["ubd"][:], 0.0)
            v.memset(sb["ones_t"][:], 1.0)
            v.scalar_tensor_tensor(sb["w2n_bt"][:], sb["w_bt"][:], -1.0, sb["w_bt"][:],
                                   ALU.mult, ALU.mult).then_inc(sems["vInit"], 1)
            v.wait_ge(sems["vInit"], 1)
            v.tensor_tensor(sb["wbb_bt"][:], sb["w_bt"][:], sb["b_bt"][:], ALU.mult)
            v.tensor_scalar(sb["mskE_t"][:], sb["ones_t"][:], CS[:, C_MSKE:C_MSKE + 1],
                            None, ALU.mult).then_inc(sems["vInit"], 1)
        cnt["vInit"] += 2
        em.emit("vector", c_init_vec)

        def c_loads(sync):
            for g in range(NSCC):
                sl = g % NPREP
                if g >= NPREP:
                    sync.wait_ge(sems["tSU"], 2 * (g - NPREP) + 2)
                    sync.wait_ge(sems["vOG2"], g - NPREP + 1)
                sync.wait_ge(sems["stB"], (64 if debug else 32) * (g + 1))
                sync.wait_ge(sems["stG"], (32 if debug else 16) * (g + 1))
                sync.dma_start(out=sb["krsb"][:, sl, :], in_=KRAW[g]).then_inc(sems["ldp"], 16)
                sync.dma_start(out=sb["vsb"][:, sl, :], in_=VCH[g]).then_inc(sems["ldp"], 16)
                sync.dma_start(out=sb["ssb"][:, sl, :], in_=SGT[g]).then_inc(sems["ldp"], 16)
                sync.wait_ge(sems["stq2"], (32 if debug else 16) * 4 * min(NM, g // 16 + 1))
                sync.dma_start(out=sb["qisb"][:, sl, :, :],
                               in_=QTD[:, :, 32 * g:32 * (g + 1)].rearrange("c p t -> p c t")
                               ).then_inc(sems["ldp"], 16)
        cnt["ldp"] += 64 * NSCC
        em.emit("sync", c_loads)

        psb_n = {"pe": 0}

        def psb_wait(t):
            k = psb_n["pe"]
            if k > 0:
                t.wait_ge(sems["sEV"], k)
            psb_n["pe"] = k + 1

        def c_pe(t):
            t.wait_ge(sems["vQ"], cnt["vQ"])
            t.wait_ge(sems["sKV1"], cnt["sKV1"])
            t.wait_ge(sems["sGT1"], cnt["sGT1"])
            t.wait_ge(sems["sG1"], cnt["sG1"])
            t.wait_ge(sems["ld"], 64)
            t.wait_ge(sems["gInit"], 1)
            for i in range(3):
                rhs = (sb["lnw_s"], sb["lnb_s"], sb["nrmw_s"])[i]
                t.matmul(psA[:, 0:128], CS[0:4, 80:208], rhs[:],
                         start=True, stop=True).then_inc(sems["tPR"], 1)
                t.wait_ge(sems["sPR"], i + 1)
            for g in range(NSCC + 1):
                if g < NSCC:
                    sl = g % NPREP
                    t.wait_ge(sems["vPR"], 3 * g + 2)
                    psb_wait(t)
                    t.transpose(psB[:], sb["kisb"][:, sl, :], sb["ident_t"][:]).then_inc(sems["tPR"], 1)
                    psb_wait(t)
                    t.transpose(psB[:], sb["qisb"][:, sl, :, :].rearrange("p c t -> p (c t)"),
                                sb["ident_t"][:]).then_inc(sems["tPR"], 1)
                if g >= 1:
                    gg = g - 1
                    sl = gg % NPREP
                    for par in (0, 1):
                        ch = 2 * gg + par
                        if ch >= CP:
                            break
                        t.wait_ge(sems["sPR"], 3 + 4 * gg + 4)
                        t.wait_ge(sems["sSC"], ch)
                        if ch == 0:
                            t.wait_ge(sems["vInit"], 2)
                        t.wait_ge(sems["sE"], ch)
                        if gg >= 1:
                            t.wait_ge(sems["sOH"], 2 * (gg - 1) + par + 1)
                        mm = None
                        for p in range(4):
                            mm = t.matmul(psCH[32 * p:32 * p + 32, :],
                                          sb["kitsb"][:, sl, 32 * p:32 * p + 32],
                                          sb["S_sb"][:, p, :], start=True, stop=True,
                                          tile_position=(0, 32 * p))
                        mm.then_inc(sems["tKH"], 1)
                        mm = None
                        for p in range(4):
                            mm = t.matmul(psA[32 * p:32 * p + 32, 0:32],
                                          sb["kitsb"][:, sl, 32 * p:32 * p + 32],
                                          sb["qisb"][:, sl, p, :], start=True, stop=True,
                                          tile_position=(0, 32 * p))
                        mm.then_inc(sems["tA"], 1)
                        t.wait_ge(sems["vAT"], ch + 1)
                        t.wait_ge(sems["vU"], ch + 1)
                        mm = None
                        for p in range(4):
                            t.matmul(psQHb[par][32 * p:32 * p + 32, 0:128],
                                     sb["atm"][32 * p:32 * p + 32, :],
                                     sb["usb"][32 * p:32 * p + 32, :],
                                     start=True, stop=False,
                                     tile_position=(32 * p, 32 * p))
                            mm = t.matmul(psQHb[par][32 * p:32 * p + 32, 0:128],
                                          sb["qisb"][:, sl, p, :],
                                          sb["S_sb"][:, p, :], start=False, stop=True,
                                          tile_position=(0, 32 * p))
                        mm.then_inc(sems["tCU"], 1)
                        t.wait_ge(sems["vBD"], ch + 1)
                        t.matmul(psS[:], sb["kesb"][:, sl, :], sb["ubd"][:],
                                 start=(ch == 0), stop=True,
                                 skip_group_check=(ch > 0)).then_inc(sems["tSU"], 1)
                    t.wait_ge(sems["vOG"], gg + 1)
                    psb_wait(t)
                    t.transpose(psB[:], sb["ogsb"][:], sb["ident_t"][:]).then_inc(sems["tOG"], 1)
        cnt["tPR"] += 3 + 2 * NSCC
        cnt["tKH"] += CP
        cnt["tA"] += CP
        cnt["tCU"] += CP
        cnt["tSU"] += CP
        cnt["tOG"] += NSCC
        em.emit("tensor", c_pe)

        scb_n = [0]
        sr_n = [0]
        sr_at_scan = {}
        sr_at_o = {}

        def c_act(s):
            for i, dstn in enumerate(("w_bt", "b_bt", "nw_bt")):
                s.wait_ge(sems["tPR"], i + 1)
                s.activation(sb[dstn][:], psA[:, 0:128], AF.Copy).then_inc(sems["sPR"], 1)
            for g in range(NSCC + 1):
                if g < NSCC:
                    sl = g % NPREP
                    s.wait_ge(sems["vPR"], 3 * g + 1)
                    s.activation(sb["ccol"][:, sl, 3:4], sb["ccol"][:, sl, 2:3], AF.Ln,
                                 bias=CS[:, C_EPSI:C_EPSI + 1]).then_inc(sems["sPR"], 1)
                    s.wait_ge(sems["sPR"], 3 + 4 * g + 1)
                    s.activation(sb["ccol"][:, sl, 0:1], sb["ccol"][:, sl, 3:4], AF.Exp,
                                 scale=-0.5).then_inc(sems["sPR"], 1)
                    s.wait_ge(sems["tPR"], 3 + 2 * g + 1)
                    s.activation(sb["kitsb"][:, sl, :], psB[:], AF.Copy).then_inc(sems["sPR"], 1)
                    s.activation(sb["scr"][:, 0:1], sb["kitsb"][:, sl, 0:1], AF.Copy).then_inc(sems["sEV"], 1)
                    s.wait_ge(sems["tPR"], 3 + 2 * g + 2)
                    s.activation(sb["qchsb"][:, sl, :], psB[:], AF.Copy).then_inc(sems["sPR"], 1)
                    s.activation(sb["scr"][:, 0:1], sb["qchsb"][:, sl, 0:1], AF.Copy).then_inc(sems["sEV"], 1)
                if g >= 1:
                    gg = g - 1
                    sl = gg % NPREP
                    for par in (0, 1):
                        ch = 2 * gg + par
                        if ch >= CP:
                            break
                        s.wait_ge(sems["tKH"], ch + 1)
                        if ch >= 1:
                            s.wait_ge(sems["vU"], ch)
                        s.activation(sb["khsb"][:], psCH[:, 0:128], AF.Copy).then_inc(sems["sE"], 1)
                        s.wait_ge(sems["vR"], ch + 1)
                        if ch >= 1:
                            s.wait_ge(sems["vU"], ch)
                        s.activation(sb["kcol"][:, 15:16], sb["kcol"][:, 0:1], AF.Ln,
                                     scale=1.0 / 128,
                                     bias=CS[:, C_EPSI:C_EPSI + 1]).then_inc(sems["sCB"], 1)
                        scb_n[0] += 1
                        s.wait_ge(sems["sCB"], scb_n[0])
                        s.activation(sb["kcol"][:, 12:13], sb["kcol"][:, 15:16], AF.Exp,
                                     scale=-0.5).then_inc(sems["sCB"], 1)
                        scb_n[0] += 1
                        s.wait_ge(sems["sCB"], scb_n[0])
                        s.activation(sb["kcol"][:, 13:14], sb["kcol"][:, 12:13],
                                     AF.Square).then_inc(sems["sR"], 1)
                        sr_n[0] += 1
                        sr_at_scan[ch] = sr_n[0]
                        s.wait_ge(sems["tA"], ch + 1)
                        s.activation(sb["ats"][:], psA[:, 0:32], AF.Copy,
                                     scale=sb["ccol"][:, sl, 4:5]).then_inc(sems["sAT"], 1)
                        s.wait_ge(sems["tSU"], ch + 1)
                        s.activation(sb["S_sb"][:].rearrange("p a b -> p (a b)"), psS[:],
                                     AF.Copy).then_inc(sems["sSC"], 1)
                        if debug and ch < 8:
                            s.wait_ge(sems["vU"], ch + 1)
                            s.dma_start(out=DBG_KH[ch], in_=sb["khsb"][:]).then_inc(sems["stDG"], 16)
                            s.dma_start(out=DBG_U[ch], in_=sb["usb"][:]).then_inc(sems["stDG"], 16)
                            s.dma_start(out=DBG_KC[ch], in_=sb["kcol"][:]).then_inc(sems["stDG"], 16)
                            s.dma_start(out=DBG_CC[ch], in_=sb["ccsb"][:, sl, :]).then_inc(sems["stDG"], 16)
                            s.dma_start(out=DBG_CL[ch][:, 0:5], in_=sb["ccol"][:, sl, 0:5]).then_inc(sems["stDG"], 16)
                            s.wait_ge(sems["sSC"], ch + 1)
                            s.dma_start(out=DBG_SS[ch],
                                        in_=sb["S_sb"][:].rearrange("p a b -> p (a b)")).then_inc(sems["stDG"], 16)
                    s.wait_ge(sems["tCU"], 2 * gg + 2)
                    s.activation(sb["ohsb"][:, 0, :], psQHb[0][:, 0:128], AF.Copy).then_inc(sems["sOH"], 1)
                    s.activation(sb["ohsb"][:, 1, :], psQHb[1][:, 0:128], AF.Copy).then_inc(sems["sOH"], 1)
                    s.wait_ge(sems["ccx"], 2 * gg + 1)
                    s.activation(sb["ocol"][:, 2:3], sb["ocol"][:, 0:1], AF.Ln,
                                 scale=1.0 / 128, bias=CS[:, C_EPSI:C_EPSI + 1]).then_inc(sems["sCB"], 1)
                    scb_n[0] += 1
                    s.wait_ge(sems["sCB"], scb_n[0])
                    s.activation(sb["ocol"][:, 1:2], sb["ocol"][:, 2:3], AF.Exp,
                                 scale=-0.5).then_inc(sems["sR"], 1)
                    sr_n[0] += 1
                    sr_at_o[(gg, 1)] = sr_n[0]
                    s.wait_ge(sems["ccx"], 2 * gg + 2)
                    s.activation(sb["ocol"][:, 2:3], sb["ocol"][:, 3:4], AF.Ln,
                                 scale=1.0 / 128, bias=CS[:, C_EPSI:C_EPSI + 1]).then_inc(sems["sCB"], 1)
                    scb_n[0] += 1
                    s.wait_ge(sems["sCB"], scb_n[0])
                    s.activation(sb["ocol"][:, 3:4], sb["ocol"][:, 2:3], AF.Exp,
                                 scale=-0.5).then_inc(sems["sR"], 1)
                    sr_n[0] += 1
                    sr_at_o[(gg, 2)] = sr_n[0]
                    s.wait_ge(sems["tOG"], gg + 1)
                    if gg >= 1:
                        s.wait_ge(sems["stOG"], 16 * gg)
                    s.activation(sb["ogts"][:], psB[:], AF.Copy).then_inc(sems["sEV"], 1)
                    blk = gg // 32
                    s.dma_start(out=OGT[blk].rearrange("(p e) (s u) -> e p s u", e=128, u=32)[:, :, gg % 32, :],
                                in_=sb["ogts"][:].rearrange("p (c u) -> p c u", u=32)
                                ).then_inc(sems["stOG"], 16)
                    if debug:
                        s.dma_start(out=dbg["dbg_ogt"][blk].rearrange("(p e) (s u) -> e p s u", e=128, u=32)[:, :, gg % 32, :],
                                    in_=sb["ogts"][:].rearrange("p (c u) -> p c u", u=32)
                                    ).then_inc(sems["stDG"], 16)
            if debug:
                s.wait_ge(sems["sSC"], CP)
                s.dma_start(out=DBG_S, in_=sb["S_sb"][:].rearrange("p a b -> p (a b)")
                            ).then_inc(sems["stDG"], 16)

        cnt["sPR"] += 3 + 4 * NSCC
        cnt["sE"] += CP
        cnt["sCB"] += 2 * CP + 2 * NSCC
        cnt["sR"] += CP + 2 * NSCC
        cnt["sAT"] += CP
        cnt["sSC"] += CP
        cnt["sOH"] += 2 * NSCC
        cnt["sEV"] += 3 * NSCC
        cnt["stOG"] += 16 * NSCC
        if debug:
            cnt["stDG"] += 16 * (NSCC + 6 * min(CP, 8) + 1)
        em.emit("scalar", c_act)

        def c_vec(v):
            for g in range(NSCC + 1):
                if g < NSCC:
                    sl = g % NPREP
                    v.wait_ge(sems["ldp"], 64 * (g + 1))
                    if g >= NPREP:
                        v.wait_ge(sems["vU"], 2 * (g - NPREP) + 2)
                    v.tensor_copy(sb["ccol"][:, sl, 4:5], sb["krsb"][:, sl, 128:129])
                    v.scalar_tensor_tensor(sb["scr"][:], sb["krsb"][:, sl, 0:128], 1.0,
                                           sb["krsb"][:, sl, 0:128], ALU.mult, ALU.mult)
                    v.reduce_sum(sb["ccol"][:, sl, 2:3], sb["scr"][:],
                                 axis=mybir.AxisListType.X).then_inc(sems["vPR"], 1)
                    v.wait_ge(sems["sPR"], 3 + 4 * g + 2)
                    v.tensor_scalar(sb["kisb"][:, sl, :], sb["krsb"][:, sl, 0:128],
                                    sb["ccol"][:, sl, 0:1], None, ALU.mult)
                    v.tensor_scalar(sb["kesb"][:, sl, :], sb["kisb"][:, sl, :],
                                    sb["ccol"][:, sl, 4:5], None, ALU.mult).then_inc(sems["vPR"], 1)
                    v.tensor_sub(sb["vmk"][:], sb["vsb"][:, sl, :], sb["kisb"][:, sl, :])
                    v.scalar_tensor_tensor(sb["scr"][:], sb["vmk"][:], -1.0, sb["w_bt"][:],
                                           ALU.mult, ALU.mult)
                    v.tensor_add(sb["ccsb"][:, sl, :], sb["scr"][:], sb["wbb_bt"][:])
                    v.tensor_scalar(sb["cc128"][:, sl, :], sb["ccsb"][:, sl, :],
                                    1.0 / 128, None, ALU.mult)
                    v.reduce_sum(sb["ccol"][:, sl, 1:2], sb["cc128"][:, sl, :],
                                 axis=mybir.AxisListType.X).then_inc(sems["vPR"], 1)
                if g >= 1:
                    gg = g - 1
                    sl = gg % NPREP
                    for par in (0, 1):
                        ch = 2 * gg + par
                        if ch >= CP:
                            break
                        mask = CS[:, C_MSKE + par:C_MSKE + par + 1]
                        am0 = C_AME if par == 0 else C_AMO
                        amask = CS[:, am0:am0 + 32]
                        v.wait_ge(sems["sE"], ch + 1)
                        v.scalar_tensor_tensor(sb["redb"][:, 0, :], sb["khsb"][:], 1.0,
                                               sb["khsb"][:], ALU.mult, ALU.mult)
                        v.scalar_tensor_tensor(sb["redb"][:, 1, :], sb["khsb"][:], 1.0,
                                               sb["w2n_bt"][:], ALU.mult, ALU.mult)
                        v.scalar_tensor_tensor(sb["redb"][:, 2, :], sb["redb"][:, 1, :], 1.0,
                                               sb["khsb"][:], ALU.mult, ALU.mult)
                        v.scalar_tensor_tensor(sb["redb"][:, 3, :], sb["khsb"][:], 1.0,
                                               sb["cc128"][:, sl, :], ALU.mult, ALU.mult)
                        v.reduce_sum(sb["kcol"][:, 0:4], sb["redb"][:],
                                     axis=mybir.AxisListType.X).then_inc(sems["vR"], 1)
                        v.wait_ge(sems["sAT"], ch + 1)
                        if ch >= 1:
                            v.wait_ge(sems["tCU"], ch)
                        v.tensor_tensor(sb["atm"][:], sb["ats"][:], amask, ALU.mult).then_inc(sems["vAT"], 1)
                        v.wait_ge(sems["sR"], sr_at_scan[ch])
                        kc = sb["kcol"]
                        v.tensor_scalar(kc[:, 5:6], kc[:, 2:3], -1.0 / 128, None, ALU.mult)
                        v.tensor_scalar(kc[:, 6:7], kc[:, 1:2], -1.0 / 128, None, ALU.mult)
                        v.tensor_tensor(kc[:, 7:8], kc[:, 12:13], mask, ALU.mult)
                        v.tensor_tensor(kc[:, 8:9], kc[:, 13:14], mask, ALU.mult)
                        v.tensor_tensor(kc[:, 9:10], kc[:, 3:4], kc[:, 12:13], ALU.mult)
                        lastd1 = v.tensor_tensor(kc[:, 11:12], kc[:, 12:13],
                                                 sb["ccol"][:, sl, 1:2], ALU.mult)
                        lastd1.then_inc(sems["vCX"], 1)
                        vcx_n[0] += 1
                        v.wait_ge(sems["vCX"], vcx_n[0])
                        v.scalar_tensor_tensor(kc[:, 10:11], kc[:, 5:6], kc[:, 13:14], kc[:, 9:10],
                                               ALU.mult, ALU.add)
                        v.scalar_tensor_tensor(kc[:, 4:5], kc[:, 6:7], kc[:, 13:14], kc[:, 11:12],
                                               ALU.mult, ALU.add).then_inc(sems["vCX"], 1)
                        vcx_n[0] += 1
                        v.wait_ge(sems["vCX"], vcx_n[0])
                        v.tensor_tensor(kc[:, 15:16], kc[:, 4:5], mask, ALU.mult)
                        v.tensor_scalar(kc[:, 14:15], kc[:, 7:8], -1.0, None,
                                        ALU.mult).then_inc(sems["vCX"], 1)
                        vcx_n[0] += 1
                        v.scalar_tensor_tensor(sb["scr"][:], sb["khsb"][:], kc[:, 10:11],
                                               sb["redb"][:, 1, :], ALU.mult, ALU.add)
                        v.wait_ge(sems["vCX"], vcx_n[0])
                        v.tensor_scalar(sb["t1sb"][:], sb["ccsb"][:, sl, :], kc[:, 14:15],
                                        kc[:, 15:16], ALU.mult, ALU.add)
                        if ch >= 1:
                            v.wait_ge(sems["tSU"], ch)
                        v.scalar_tensor_tensor(sb["usb"][:], sb["scr"][:], kc[:, 8:9],
                                               sb["t1sb"][:], ALU.mult, ALU.add).then_inc(sems["vU"], 1)
                        cp = None
                        for p in range(4):
                            cp = v.tensor_copy(sb["ubd"][32 * p:32 * p + 32, 128 * p:128 * (p + 1)],
                                               sb["usb"][32 * p:32 * p + 32, :])
                        cp.then_inc(sems["vBD"], 1)
                    v.wait_ge(sems["sOH"], 2 * gg + 2)
                    v.tensor_sub(sb["scr"][:], sb["ohsb"][:, 0, :], sb["ohsb"][:, 1, :])
                    v.scalar_tensor_tensor(sb["mrg"][:], sb["scr"][:],
                                           CS[:, C_MSKE:C_MSKE + 1], sb["ohsb"][:, 1, :],
                                           ALU.mult, ALU.add)
                    v.scalar_tensor_tensor(sb["scr"][:], sb["mrg"][:], 1.0, sb["mrg"][:],
                                           ALU.mult, ALU.mult)
                    v.reduce_sum(sb["ocol"][:, 0:1], sb["scr"][:],
                                 axis=mybir.AxisListType.X).then_inc(sems["ccx"], 1)
                    v.wait_ge(sems["sR"], sr_at_o[(gg, 1)])
                    v.tensor_scalar(sb["scr"][:], sb["mrg"][:], sb["ocol"][:, 1:2], None,
                                    ALU.mult)
                    v.scalar_tensor_tensor(sb["scr"][:], sb["scr"][:], 1.0, sb["w_bt"][:],
                                           ALU.mult, ALU.mult)
                    v.tensor_add(sb["scr"][:], sb["scr"][:], sb["b_bt"][:])
                    v.tensor_add(sb["scr"][:], sb["scr"][:], sb["qchsb"][:, sl, :])
                    v.tensor_tensor(sb["scr"][:], sb["scr"][:], sb["ssb"][:, sl, :], ALU.mult)
                    v.scalar_tensor_tensor(sb["t1sb"][:], sb["scr"][:], 1.0, sb["scr"][:],
                                           ALU.mult, ALU.mult)
                    v.reduce_sum(sb["ocol"][:, 3:4], sb["t1sb"][:],
                                 axis=mybir.AxisListType.X).then_inc(sems["ccx"], 1)
                    v.wait_ge(sems["sR"], sr_at_o[(gg, 2)])
                    v.tensor_scalar(sb["scr"][:], sb["scr"][:], sb["ocol"][:, 3:4], None,
                                    ALU.mult)
                    if gg >= 1:
                        v.wait_ge(sems["tOG"], gg)
                    v.tensor_tensor(sb["ogsb"][:], sb["scr"][:], sb["nw_bt"][:],
                                    ALU.mult).then_inc(sems["vOG"], 1)
                    v.nop().then_inc(sems["vOG2"], 1)

        cnt["vPR"] += 3 * NSCC
        cnt["vCX"] += 3 * CP
        cnt["vR"] += CP
        cnt["vAT"] += CP
        cnt["vU"] += CP
        cnt["vBD"] += CP
        cnt["ccx"] += 2 * NSCC
        cnt["vOG"] += NSCC
        cnt["vOG2"] += NSCC
        em.emit("vector", c_vec)

    S("xqt_t", (128, 4, KC, 128), BF16)
    S("ogt_t", (128, 2, 4, 128), BF16)
    S("wqr_k", (128, 2, 4, 128), BF16)
    S("ycols", (128, 8), F32)
    S("t2col", (128, 8), F32)
    S("outsb", (128, 2, 512), F32)
    S("xqf", (128, D), F32)
    S("xqi", (128, D), I32)
    S("xqd2", (128, D), F32)
    S("wf_t", (128, 2, D), F32)
    sb["wfq"] = sb["xn_tok"]

    S("wfcol", (128, 16), F32)
    S("wfsc", (128, 8), F32)
    S("wfsb", (128, 144), BF16)

    for s_ in ("ldwos", "ldog", "stRS", "sX2", "tWO", "ldrs", "ldr", "vX2",
               "ldf", "sWO", "tF", "vWO", "ldw2", "vQ2", "vT", "sT", "sT2",
               "vQX", "stX", "ldq", "ldff", "tFF", "vFF", "stO"):
        em.sem(s_)

    x2f = sb["xnt_m"][:].rearrange("p a k f -> p (a k f)")
    wov2 = sb["wq_b"][:].rearrange("p k c -> p (k c)")
    x2p = sb["g1sb"]
    LW2 = 3 if debug else 2
    DX = 2 if debug else 1
    psW = [psQ[0], psQ[1], psKV[0], psKV[1]]

    if has("T"):
        def t_gload(g):
            g.wait_ge(sems["tQ"], cnt["tQ"])
            wsr = WOS.rearrange("(c p) n -> c p n", p=128)
            for fc in range(4):
                g.dma_start(out=wov2[:, 2048 * fc:2048 * (fc + 1)],
                            in_=wsr[fc]).then_inc(sems["ldwos"], 16)

            def wfld(i):
                if i >= 2:
                    g.wait_ge(sems["sWO"], i - 1)
                g.dma_start(out=sb["wf_t"][:, i % 2, :],
                            in_=WF.rearrange("(n p) d -> n p d", p=128)[i]
                            ).then_inc(sems["ldf"], 16)
            if has("F"):
                for i in range(5):
                    wfld(i)
            g.wait_ge(sems["stRS"], 16 * 128)
            g.collective_compute(
                "ReduceScatter", ALU.add,
                replica_groups=[[0, 1, 2, 3], [4, 5, 6, 7]],
                ins=[RSI.rearrange("b q d -> (b q d)").opt()],
                outs=[RSO.rearrange("q d -> (q d)").opt()],
            ).then_inc(sems["ccx"], 1)
            if has("F"):
                for i in range(5, KC):
                    wfld(i)
        cnt["ldwos"] += 64
        cnt["ccx"] += 1
        ccs_done = cnt["ccx"]
        if has("F"):
            cnt["ldf"] += 16 * 16
        em.emit("gpsimd", t_gload)

        def t_sync(sync):
            for tt in range(32):
                if tt >= 2:
                    sync.wait_ge(sems["tWO"], 4 * (tt - 1))
                sync.wait_ge(sems["stOG"], 16 * 4 * (tt + 1))
                sync.dma_start(out=sb["ogt_t"][:, tt % 2, :, :],
                               in_=OGT[tt // 8][:, 128 * (tt % 8):128 * (tt % 8) + 128]
                               .rearrange("(c p) t -> p c t", p=128)
                               ).then_inc(sems["ldog"], 16)
            sync.wait_ge(sems["ccx"], ccs_done)
            rsr = RSO.rearrange("(n p) d -> n p d", p=128)
            xqr = XQ.rearrange("(n p) d -> n p d", p=128)
            for t in range(8):
                sync.dma_start(out=x2f[:, 2048 * t:2048 * (t + 1)],
                               in_=rsr[t]).then_inc(sems["ldrs"], 16)
                if t >= 2:
                    sync.wait_ge(sems["vX2"], t - 1)
                sync.dma_start(out=sb["xtile"][:, t % 2, :],
                               in_=xqr[t]).then_inc(sems["ldr"], 16)
            def ldq1(t):
                sync.wait_ge(sems["stX"], 16 * DX * (t + 1))
                if t >= 4:
                    sync.wait_ge(sems["tFF"], 64)
                sync.dma_start(out=sb["xqt_t"][:, t % 4, :, :],
                               in_=XQD[t].rearrange("p (k f) -> p k f", f=128),
                               transpose=True).then_inc(sems["ldq"], 16)

            def wqr1(idx2):
                th, nq, k = idx2 // 64, (idx2 // 16) % 4, idx2 % 16
                if idx2 >= 2:
                    sync.wait_ge(sems["tFF"], idx2 - 1)
                sync.wait_ge(sems["ldw2"], 16 * LW2 * (4 * nq + 4))
                for a in range(4):
                    sync.dma_start(out=sb["wqr_k"][:, idx2 % 2, a, :],
                                   in_=WQR[4 * nq + a][:, 128 * k:128 * (k + 1)],
                                   transpose=True).then_inc(sems["ldff"], 16)
            for t in range(4):
                ldq1(t)
            for idx2 in range(64):
                wqr1(idx2)
            for t in range(4, 8):
                ldq1(t)
            for idx2 in range(64, 128):
                wqr1(idx2)
        cnt["ldog"] += 16 * 32
        cnt["ldrs"] += 16 * 8
        cnt["ldr"] += 16 * 8
        cnt["ldq"] += 16 * 8
        cnt["ldff"] += 16 * 4 * 128
        em.emit("sync", t_sync)

        def wo_pe(t):
            t.wait_ge(sems["ldwos"], 64)
            for grp in range(128):
                tt, nq = grp // 4, grp % 4
                if nq == 0:
                    t.wait_ge(sems["ldog"], 16 * (tt + 1))
                if grp >= 4:
                    t.wait_ge(sems["sX2"], grp - 3)
                for fc in range(4):
                    mm = t.matmul(psW[grp % 4][:, 0:512],
                                  sb["ogt_t"][:, tt % 2, fc, :],
                                  wov2[:, 2048 * fc + 512 * nq:2048 * fc + 512 * (nq + 1)],
                                  start=(fc == 0), stop=(fc == 3))
                mm.then_inc(sems["tWO"], 1)
        cnt["tWO"] += 128
        em.emit("tensor", wo_pe)

        def wo_act(s):
            for grp in range(128):
                tt, nq = grp // 4, grp % 4
                s.wait_ge(sems["tWO"], grp + 1)
                if grp >= 2:
                    s.wait_ge(sems["stRS"], 16 * (grp - 1))
                s.activation(x2p[:, grp % 2, :], psW[grp % 4][:, 0:512],
                             AF.Copy).then_inc(sems["sX2"], 1)
                s.dma_start(out=RSI[tt // 8][128 * (tt % 8):128 * (tt % 8) + 128,
                                            512 * nq:512 * (nq + 1)],
                            in_=x2p[:, grp % 2, :]).then_inc(sems["stRS"], 16)
        cnt["sX2"] += 128
        cnt["stRS"] += 16 * 128
        em.emit("scalar", wo_act)

        def t_vec(v):
            for t in range(8):
                v.wait_ge(sems["ldrs"], 16 * (t + 1))
                v.wait_ge(sems["ldr"], 16 * (t + 1))
                v.tensor_tensor(x2f[:, 2048 * t:2048 * (t + 1)],
                                x2f[:, 2048 * t:2048 * (t + 1)],
                                sb["xtile"][:, t % 2, :],
                                ALU.add).then_inc(sems["vX2"], 1)
        cnt["vX2"] += 8
        em.emit("vector", t_vec)

    if has("F"):
        def f1_act(s):
            for i in range(KC):
                s.wait_ge(sems["ldf"], 16 * (i + 1))
                s.activation(sb["xqd2"][:, 0:D], sb["wf_t"][:, i % 2, :], AF.Abs,
                             accum_out=sb["wfcol"][:, i:i + 1]).then_inc(sems["sWO"], 1)
        cnt["sWO"] += KC
        em.emit("scalar", f1_act)

        def f1_pe(t):
            t.wait_ge(sems["vWO"], 4)
            t.matmul(psGT[1][0:1, 0:1], sb["wfsb"][:, 1:2], sb["wfsb"][:, 0:1],
                     start=True, stop=True).then_inc(sems["tF"], 1)
            t.wait_ge(sems["vWO"], 8)
            t.matmul(psGT[1][:, 1:3], sb["wfsb"][0:1, 8:136], sb["wfsb"][0:1, 2:4],
                     start=True, stop=True).then_inc(sems["tF"], 1)
        cnt["tF"] += 2
        em.emit("tensor", f1_pe)

        def f1_vec(v):
            v.wait_ge(sems["ld"], 64)
            v.tensor_copy(sb["wfsb"][:, 1:2], CS[:, C_ONE:C_ONE + 1]).then_inc(sems["vWO"], 1)
            v.tensor_copy(sb["wfsb"][0:1, 8:136], CS[0:1, 208:336]).then_inc(sems["vWO"], 1)
            v.wait_ge(sems["sWO"], KC)
            v.reduce_sum(sb["wfsc"][:, 0:1], sb["wfcol"][:],
                         axis=mybir.AxisListType.X).then_inc(sems["vWO"], 1)
            v.wait_ge(sems["vWO"], 3)
            v.tensor_copy(sb["wfsb"][:, 0:1], sb["wfsc"][:, 0:1]).then_inc(sems["vWO"], 1)
            v.wait_ge(sems["tF"], 1)
            v.tensor_scalar(sb["wfsc"][0:1, 1:2], psGT[1][0:1, 0:1], 1.0 / (D * D),
                            None, ALU.mult).then_inc(sems["vWO"], 1)
            v.wait_ge(sems["vWO"], 5)
            v.tensor_scalar(sb["wfsc"][0:1, 2:3], sb["wfsc"][0:1, 1:2], 1e-6, None,
                            ALU.max).then_inc(sems["vWO"], 1)
            v.wait_ge(sems["vWO"], 6)
            v.reciprocal(sb["wfsc"][0:1, 3:4], sb["wfsc"][0:1, 2:3]).then_inc(sems["vWO"], 1)
            v.wait_ge(sems["vWO"], 7)
            v.tensor_copy(sb["wfsb"][0:1, 2:4], sb["wfsc"][0:1, 2:4]).then_inc(sems["vWO"], 1)
            v.wait_ge(sems["tF"], 2)
            v.tensor_copy(sb["wfsc"][:, 4:6], psGT[1][:, 1:3]).then_inc(sems["vWO"], 1)
        cnt["vWO"] += 9
        em.emit("vector", f1_vec)

        def f2_g(g):
            g.wait_ge(sems["sWO"], KC)
            for i in range(KC):
                if i >= 2:
                    g.wait_ge(sems["vQ2"], i - 1)
                g.dma_start(out=sb["wf_t"][:, i % 2, :],
                            in_=WF.rearrange("(n p) d -> n p d", p=128)[i]).then_inc(sems["ldw2"], 16)
                g.wait_ge(sems["vQ2"], i + 1)
                g.dma_start(out=WQR[i], in_=sb["wfq"][:, i % 2, :]).then_inc(sems["ldw2"], 16)
                if debug:
                    g.dma_start(out=dbg["dbg_wqt"][i], in_=sb["wfq"][:, i % 2, :]).then_inc(sems["ldw2"], 16)
        cnt["ldw2"] += 16 * KC * LW2
        em.emit("gpsimd", f2_g)

        def f2_vec(v):
            v.wait_ge(sems["vWO"], 9)
            for i in range(KC):
                v.wait_ge(sems["ldw2"], 16 * (LW2 * i + 1))
                v.tensor_scalar(sb["xqd2"][:], sb["wf_t"][:, i % 2, :],
                                sb["wfsc"][:, 5:6], None, ALU.mult)
                v.tensor_scalar(sb["xqf"][:], sb["xqd2"][:], 0.5, None, ALU.is_gt)
                v.scalar_tensor_tensor(sb["wfq"][:, i % 2, :], sb["xqd2"][:], -0.5,
                                       sb["xqf"][:], ALU.is_lt,
                                       ALU.subtract).then_inc(sems["vQ2"], 1)
        cnt["vQ2"] += KC
        em.emit("vector", f2_vec)

    if has("T"):
        def t2_vec(v):
            v.wait_ge(sems["ldw2"], 16 * KC * LW2)
            for t in range(8):
                x2t = x2f[:, 2048 * t:2048 * (t + 1)]
                sl = t % 2
                for j in range(4):
                    v.bn_stats(sb["bnst"][:, 6 * j:6 * (j + 1)],
                               x2t[:, 512 * j:512 * (j + 1)]).then_inc(sems["vT"], 1)
                v.wait_ge(sems["vT"], 5 * t + 4)
                v.bn_aggr(sb["p0col"][:, sl, 0:2], sb["bnst"][:]).then_inc(sems["vT"], 1)
                v.wait_ge(sems["sT"], 2 * (t + 1))
                v.tensor_scalar(sb["xqf"][:], x2t, sb["p0col"][:, sl, 0:1],
                                sb["p0col"][:, sl, 2:3], ALU.subtract,
                                ALU.mult).then_inc(sems["vQX"], 1)
                v.wait_ge(sems["sT2"], t + 1)
                v.reduce_max(sb["t2col"][:, t:t + 1], sb["xqd2"][:],
                             axis=mybir.AxisListType.X).then_inc(sems["vQX"], 1)
                v.wait_ge(sems["vQX"], 12 * t + 2)
                v.tensor_scalar(sb["t2col"][:, t:t + 1], sb["t2col"][:, t:t + 1],
                                1e-8, None, ALU.max).then_inc(sems["vQX"], 1)
                v.wait_ge(sems["vQX"], 12 * t + 3)
                v.reciprocal(sb["p0col"][:, sl, 3:4],
                             sb["t2col"][:, t:t + 1]).then_inc(sems["vQX"], 1)
                v.wait_ge(sems["vQX"], 12 * t + 4)
                v.tensor_scalar(sb["ycols"][:, t:t + 1], sb["t2col"][:, t:t + 1],
                                sb["wfsc"][:, 4:5], 1.0 / 127, ALU.mult,
                                ALU.mult).then_inc(sems["vQX"], 1)
                v.tensor_scalar(sb["wf_t"][:, 0, :], sb["xqf"][:],
                                sb["p0col"][:, sl, 3:4], 127.0, ALU.mult,
                                ALU.mult).then_inc(sems["vQX"], 1)
                v.tensor_copy(sb["xqi"][:], sb["wf_t"][:, 0, :]).then_inc(sems["vQX"], 1)
                v.tensor_copy(sb["xqd2"][:], sb["xqi"][:]).then_inc(sems["vQX"], 1)
                v.tensor_tensor(sb["wf_t"][:, 0, :], sb["wf_t"][:, 0, :],
                                sb["xqd2"][:], ALU.subtract).then_inc(sems["vQX"], 1)
                v.scalar_tensor_tensor(sb["wf_t"][:, 1, :], sb["wf_t"][:, 0, :], 0.5,
                                       sb["xqd2"][:], ALU.is_ge,
                                       ALU.add).then_inc(sems["vQX"], 1)
                v.scalar_tensor_tensor(sb["xqf"][:], sb["wf_t"][:, 0, :], -0.5,
                                       sb["wf_t"][:, 1, :], ALU.is_le,
                                       ALU.subtract).then_inc(sems["vQX"], 1)
                if t >= 2:
                    v.wait_ge(sems["stX"], 16 * DX * (t - 1))
                v.tensor_scalar(sb["wfq"][:, sl, :], sb["xqf"][:], 127.0, -127.0,
                                ALU.min, ALU.max).then_inc(sems["vQX"], 1)
        cnt["vT"] += 5 * 8
        cnt["vQX"] += 12 * 8
        em.emit("vector", t2_vec)

        def t2_act(s):
            for t in range(8):
                sl = t % 2
                s.wait_ge(sems["vT"], 5 * (t + 1))
                s.activation(sb["p0col"][:, sl, 3:4], sb["p0col"][:, sl, 1:2], AF.Ln,
                             bias=CS[:, C_EPSL:C_EPSL + 1]).then_inc(sems["sT"], 1)
                s.wait_ge(sems["sT"], 2 * t + 1)
                s.activation(sb["p0col"][:, sl, 2:3], sb["p0col"][:, sl, 3:4], AF.Exp,
                             scale=-0.5).then_inc(sems["sT"], 1)
                s.wait_ge(sems["vQX"], 12 * t + 1)
                s.activation(sb["xqd2"][:], sb["xqf"][:], AF.Abs).then_inc(sems["sT2"], 1)
                s.wait_ge(sems["vQX"], 12 * (t + 1))
                s.dma_start(out=XQD[t], in_=sb["wfq"][:, sl, :]).then_inc(sems["stX"], 16)
                if debug:
                    s.dma_start(out=dbg["dbg_xq2"][t], in_=sb["wfq"][:, sl, :]).then_inc(sems["stX"], 16)
        cnt["sT"] += 2 * 8
        cnt["sT2"] += 8
        cnt["stX"] += 16 * 8 * DX
        em.emit("scalar", t2_act)

        def t3_pe(t):
            t.wait_ge(sems["sX2"], 128)
            for idx2 in range(128):
                th, nq, k = idx2 // 64, (idx2 // 16) % 4, idx2 % 16
                fg = idx2 // 16
                t.wait_ge(sems["ldff"], 64 * (idx2 + 1))
                if k == 0:
                    t.wait_ge(sems["ldq"], 16 * 4 * (th + 1))
                    if fg >= 1:
                        t.wait_ge(sems["vFF"], 4 * fg)
                for ti in range(4):
                    mm = t.matmul(psW[ti][:, 0:512],
                                  sb["xqt_t"][:, ti, k, :],
                                  sb["wqr_k"][:, idx2 % 2, :, :].rearrange("p a f -> p (a f)"),
                                  start=(k == 0), stop=(k == KC - 1))
                mm.then_inc(sems["tFF"], 1)
        cnt["tFF"] += 128
        em.emit("tensor", t3_pe)

        def t3_vec(v):
            for fg in range(8):
                th, nq = fg // 4, fg % 4
                v.wait_ge(sems["tFF"], 16 * (fg + 1))
                v.wait_ge(sems["vQX"], 12 * 4 * (th + 1))
                for ti in range(4):
                    tile = 4 * th + ti
                    j = 4 * fg + ti
                    if j >= 2:
                        v.wait_ge(sems["stO"], 16 * (j - 1))
                    v.scalar_tensor_tensor(
                        sb["outsb"][:, j % 2, :], psW[ti][:, 0:512],
                        sb["ycols"][:, tile:tile + 1],
                        x2f[:, 2048 * tile + 512 * nq:2048 * tile + 512 * (nq + 1)],
                        ALU.mult, ALU.add).then_inc(sems["vFF"], 1)
        cnt["vFF"] += 32
        em.emit("vector", t3_vec)

        def t3_st(s):
            for j in range(32):
                fg, ti = j // 4, j % 4
                th, nq = fg // 4, fg % 4
                tile = 4 * th + ti
                s.wait_ge(sems["vFF"], j + 1)
                s.dma_start(out=OUT[128 * tile:128 * (tile + 1), 512 * nq:512 * (nq + 1)],
                            in_=sb["outsb"][:, j % 2, :]).then_inc(sems["stO"], 16)
            if debug:
                s.dma_start(out=dbg["dbg_x2"], in_=x2f).then_inc(sems["stDG"], 16)
                s.dma_start(out=dbg["dbg_tc"][:, 0:8], in_=sb["t2col"][:]).then_inc(sems["stDG"], 16)
                s.dma_start(out=dbg["dbg_tc"][:, 8:16], in_=sb["ycols"][:]).then_inc(sems["stDG"], 16)
        cnt["stO"] += 16 * 32
        if debug:
            cnt["stDG"] += 16 * 3
        em.emit("scalar", t3_st)

    finals = {k: v for k, v in cnt.items()}

    def final_sync(sync):
        for k, v in finals.items():
            if v > 0:
                sync.wait_ge(sems[k], v)
    em.emit("sync", final_sync)

    em.build()
    ctx.close()
    return nc



def build_consts():
    c = np.zeros((128, 640), np.float32)
    c[:, C_ONE] = 1.0
    c[:, C_I128] = 1.0 / 128
    c[:, C_EPSI] = EPS_IN
    c[:, C_EPSL] = EPS_LN
    c[:, C_HALF] = 0.5
    c[:, C_NEG1] = -1.0
    c[:, C_127] = 127.0
    t = np.arange(128) % 32
    c[:, C_MSKE] = (t < 16).astype(np.float32)
    c[:, C_MSKO] = (t >= 16).astype(np.float32)
    for ci in range(32):
        even = (t < 16) & (ci < 16) & (t <= ci)
        odd = (t >= 16) & (ci >= 16) & ((t - 16) <= (ci - 16))
        c[:, C_AME + ci] = even.astype(np.float32)
        c[:, C_AMO + ci] = odd.astype(np.float32)
    c[:, 208:336] = 1.0
    m = np.arange(128)
    for j in range(4):
        c[j, 80:208] = (m // 32 == j).astype(np.float32)
    return c


def host_prep(inputs):
    x = np.asarray(inputs["L0_attn_x"])
    wq = np.asarray(inputs["L0_attn_Wq"])
    wk = np.asarray(inputs["L0_attn_Wk"])
    wv = np.asarray(inputs["L0_attn_Wv"])
    wo = np.asarray(inputs["L0_attn_Wo"])
    wbeta = np.asarray(inputs["L0_attn_Wbeta"])
    lnw = np.asarray(inputs["L0_attn_ln_w"])
    lnb = np.asarray(inputs["L0_attn_ln_b"])
    wg1 = np.asarray(inputs["L0_attn_Wg1"])
    wg2 = np.asarray(inputs["L0_attn_Wg2"])
    nrm = np.asarray(inputs["L0_attn_norm_w"])
    wf = np.asarray(inputs["L0_ffn_weight"])
    consts = build_consts()

    in_maps = []
    for c in range(8):
        b = c // 4
        hg = c % 4
        hsl = slice(4 * hg * Dh, 4 * (hg + 1) * Dh)
        heads = [4 * hg + j for j in range(HPC)]
        wkvb = np.concatenate(
            [np.concatenate([wk[:, h * Dh:(h + 1) * Dh],
                             wbeta[:, h:h + 1],
                             wv[:, h * Dh:(h + 1) * Dh]], axis=1)
             for h in heads], axis=1)
        m = {
            "x": np.ascontiguousarray(x[b]),
            "xq": np.ascontiguousarray(x[b, QTOK * hg:QTOK * (hg + 1)]),
            "wq": np.ascontiguousarray(wq[:, hsl]),
            "wkvb": np.ascontiguousarray(wkvb),
            "wg1": np.ascontiguousarray(wg1),
            "wg2": np.ascontiguousarray(wg2[:, hsl]),
            "wos": np.ascontiguousarray(wo[hsl, :]),
            "wf": np.ascontiguousarray(wf),
            "lnw": np.ascontiguousarray(lnw[heads]),
            "lnb": np.ascontiguousarray(lnb[heads]),
            "nrmw": np.ascontiguousarray(nrm[hsl]).reshape(HPC, Dh),
            "consts": consts,
        }
        in_maps.append(m)
    return in_maps


def kernel(**inputs):
    in_maps = host_prep(inputs)
    nc = build_program(debug=DEBUG)
    res = run_bass_kernel_spmd(nc, in_maps, core_ids=list(range(8)))
    out = np.zeros((B, T, D), np.float32)
    for ci in range(8):
        b = ci // 4
        hg = ci % 4
        out[b, QTOK * hg:QTOK * (hg + 1)] = res.results[ci]["out"]
    return out


# version 06
# speedup vs baseline: 1.0709x; 1.0709x over previous
import numpy as np
import ml_dtypes
import concourse.bass as bass
import concourse.mybir as mybir
from concourse.bass_utils import run_bass_kernel_spmd
from contextlib import ExitStack

F32 = mybir.dt.float32
BF16 = mybir.dt.bfloat16
I32 = mybir.dt.int32
AF = mybir.ActivationFunctionType
ALU = mybir.AluOpType

B, T, D, H = 2, 4096, 2048, 16
HPC = 4
Dh = 128
SC = 32
NSC = T // SC
CH = 16
NT = T // 128
KC = D // 128
QTOK = T // 4
EPS_LN = 1e-5
EPS_IN = 1e-6
DEBUG = False
import os
PARTS = set(os.environ.get("KB_PARTS", "all").split(","))
def has(p):
    return "all" in PARTS or p in PARTS

C_ONE, C_I128, C_EPSI, C_EPSL, C_HALF, C_ZERO, C_NEG1, C_127 = range(8)
C_MSKE, C_MSKO = 8, 9
C_AME, C_AMO = 16, 48


class Emitter:

    def __init__(self, nc, ctx):
        self.nc = nc
        self.ops = {k: [] for k in ("sync", "vector", "scalar", "tensor", "gpsimd")}
        self.sems = {}
        self.cnt = {}
        self.ctx = ctx

    def sem(self, name):
        if name not in self.sems:
            self.sems[name] = self.ctx.enter_context(self.nc.semaphore(name))
            self.cnt[name] = 0
        return self.sems[name]

    def emit(self, eng, fn):
        self.ops[eng].append(fn)

    def inc(self, name, n=1):
        self.sem(name)
        self.cnt[name] += n
        return self.cnt[name]

    def build(self):
        blk_ctx = self.nc.Block()
        block = blk_ctx.__enter__()

        def run(eng_name):
            def body(eng):
                for fn in self.ops[eng_name]:
                    fn(eng)
            return body

        block.sync(run("sync"))
        block.scalar(run("scalar"))
        block.vector(run("vector"))
        block.tensor(run("tensor"))
        block.gpsimd(run("gpsimd"))
        blk_ctx.__exit__(None, None, None)


def build_program(debug=False):
    nc = bass.Bass("TRN2", target_bir_lowering=False, detect_race_conditions=False)
    ctx = ExitStack()
    em = Emitter(nc, ctx)
    sems, cnt = em.sems, em.cnt

    dram = {}

    def din(name, shape, dtype=F32):
        dram[name] = nc.dram_tensor(name, list(shape), dtype, kind="ExternalInput").ap()
        return dram[name]

    def dout(name, shape, dtype=F32):
        dram[name] = nc.dram_tensor(name, list(shape), dtype, kind="ExternalOutput").ap()
        return dram[name]

    def dint(name, shape, dtype=BF16):
        dram[name] = nc.dram_tensor(name, list(shape), dtype, kind="Internal").ap()
        return dram[name]

    X = din("x", (T, D))
    XQ = din("xq", (QTOK, D))
    WQ = din("wq", (D, 512))
    WKVB = din("wkvb", (D, 4 * 257))
    WG1 = din("wg1", (D, 128))
    WG2 = din("wg2", (128, 512))
    WOS = din("wos", (512, D))
    WF = din("wf", (D, D))
    LNW = din("lnw", (HPC, Dh))
    LNB = din("lnb", (HPC, Dh))
    NRMW = din("nrmw", (HPC, Dh))
    CONSTS = din("consts", (128, 640))
    OUT = dout("out", (QTOK, D), F32)

    XN = dint("xnb", (NT, 128, D), BF16)
    QTD = dint("qtd", (4, 128, T), BF16)
    KRAW = dint("kraw", (NSC, 128, 132), BF16)
    VCH = dint("vch", (NSC, 128, 128), BF16)
    SGT = dint("sgt", (NSC, 128, 128), BF16)
    WQR = dint("wqraw", (KC, 128, D), BF16)
    OGT = dint("ogt", (4, 512, QTOK), BF16)
    RSI = dint("rsi", (4, QTOK, D), BF16)
    RSO = dint("rso", (QTOK, D), BF16)
    XQD = dint("xqd", (8, 128, D), BF16)

    dbg = {}
    if debug:
        for nm, src in (("dbg_xnt", XN), ("dbg_qtd", QTD), ("dbg_kraw", KRAW),
                        ("dbg_vch", VCH), ("dbg_sgt", SGT), ("dbg_wqt", WQR),
                        ("dbg_ogt", OGT), ("dbg_xq2", XQD)):
            dbg[nm] = dout(nm, src.shape, BF16)
        dbg["dbg_x2"] = dout("dbg_x2", (128, 16384), BF16)
        dbg["dbg_tc"] = dout("dbg_tc", (128, 16), F32)

    sb = {}

    def S(name, shape, dtype):
        sb[name] = ctx.enter_context(nc.sbuf_tensor(name, list(shape), dtype))
        return sb[name]

    S("wq_b", (128, KC, 512), BF16)
    S("wkvb_b", (128, KC, 4, 257), BF16)
    S("wg1_b", (128, KC, 128), BF16)
    S("wg2_b", (128, 512), BF16)
    S("lnw_s", (HPC, Dh), F32)
    S("lnb_s", (HPC, Dh), F32)
    S("nrmw_s", (HPC, Dh), F32)
    CS = S("consts_s", (128, 640), F32)

    S("xtile", (128, 2, D), F32)
    S("bnst", (128, 24), F32)
    S("p0col", (128, 2, 4), F32)
    S("xn_tok", (128, 2, D), BF16)

    def p0_gload(g):
        g.dma_start(out=sb["wq_b"][:], in_=WQ.rearrange("(k p) c -> p k c", p=128)).then_inc(sems["ldw"], 16)
        g.dma_start(out=sb["wkvb_b"][:], in_=WKVB.rearrange("(k p) (f c) -> p k f c", p=128, f=4)).then_inc(sems["ldw"], 16)
        g.dma_start(out=sb["wg1_b"][:], in_=WG1.rearrange("(k p) c -> p k c", p=128)).then_inc(sems["ldw"], 16)
        g.dma_start(out=sb["wg2_b"][:], in_=WG2).then_inc(sems["ldw"], 16)
    em.sem("ldw")
    if has("wload"):
        cnt["ldw"] += 64
        em.emit("gpsimd", p0_gload)

    def p0_small(sync):
        sync.dma_start(out=sb["lnw_s"][:], in_=LNW).then_inc(sems["ld"], 16)
        sync.dma_start(out=sb["lnb_s"][:], in_=LNB).then_inc(sems["ld"], 16)
        sync.dma_start(out=sb["nrmw_s"][:], in_=NRMW).then_inc(sems["ld"], 16)
        sync.dma_start(out=sb["consts_s"][:], in_=CONSTS).then_inc(sems["ld"], 16)
    em.sem("ld"); cnt["ld"] += 64
    em.emit("sync", p0_small)

    em.sem("ldx"); em.sem("vA"); em.sem("sA"); em.sem("xs"); em.sem("stq")

    def p0_x_loads(sync):
        for i in range(int(os.environ.get("KB_NTX", NT))):
            if i >= 2 and has("xvec"):
                sync.wait_ge(sems["vA"], 2 * (i - 1))
            sync.dma_start(out=sb["xtile"][:, i % 2, :],
                           in_=X.rearrange("(n p) d -> n p d", p=128)[i]).then_inc(sems["ldx"], 16)
    if has("xln"):
        cnt["ldx"] += 16 * NT
        em.emit("sync", p0_x_loads)

    def p0_x_vec(v):
        v.wait_ge(sems["ld"], 64)
        for i in range(NT):
            sl = i % 2
            v.wait_ge(sems["ldx"], 16 * (i + 1))
            for j in range(4):
                v.bn_stats(sb["bnst"][:, 6 * j:6 * (j + 1)],
                           sb["xtile"][:, sl, 512 * j:512 * (j + 1)]).then_inc(sems["vB"], 1)
            v.wait_ge(sems["vB"], 4 * (i + 1))
            v.bn_aggr(sb["p0col"][:, sl, 0:2], sb["bnst"][:]).then_inc(sems["vA"], 1)
            if has("xact"):
                v.wait_ge(sems["sA"], i + 1)
                if i >= 2:
                    v.wait_ge(sems["stq"], 16 * (i - 1) * (2 if debug else 1))
            v.tensor_scalar(sb["xn_tok"][:, sl, :], sb["xtile"][:, sl, :],
                            sb["p0col"][:, sl, 0:1], sb["p0col"][:, sl, 2:3],
                            ALU.subtract, ALU.mult).then_inc(sems["vA"], 1)
    if has("xln") and has("xvec"):
        cnt["vA"] += 2 * NT
        em.emit("vector", p0_x_vec)

    def p0_x_act(s):
        s.wait_ge(sems["ld"], 64)
        for i in range(NT):
            sl = i % 2
            s.wait_ge(sems["vA"], 2 * i + 1)
            s.activation(sb["p0col"][:, sl, 3:4], sb["p0col"][:, sl, 1:2], AF.Ln,
                         bias=CS[:, C_EPSL:C_EPSL + 1]).then_inc(sems["sB"], 1)
            s.wait_ge(sems["sB"], i + 1)
            s.activation(sb["p0col"][:, sl, 2:3], sb["p0col"][:, sl, 3:4], AF.Exp,
                         scale=-0.5).then_inc(sems["sA"], 1)
            s.wait_ge(sems["vA"], 2 * i + 2)
            s.dma_start(out=XN[i], in_=sb["xn_tok"][:, sl, :]).then_inc(sems["stq"], 16)
            if debug:
                s.dma_start(out=dbg["dbg_xnt"][i], in_=sb["xn_tok"][:, sl, :]).then_inc(sems["stq"], 16)

    em.sem("sB"); em.sem("vB")
    if has("xln") and has("xvec"):
        cnt["vB"] += 4 * NT
    if has("xln") and has("xact"):
        cnt["sB"] += NT
        cnt["sA"] += NT
        cnt["stq"] += 16 * NT * (2 if debug else 1)
        em.emit("scalar", p0_x_act)




    SCM = 16
    NM = 8
    S("xnt_m", (128, 2, KC, 512), BF16)
    S("sigq", (128, 512), F32)
    S("qsil", (128, 512), BF16)
    S("g1sb", (128, 2, 512), BF16)
    S("krawt", (128, 2, 132), BF16)
    S("vt", (128, 2, 128), BF16)
    S("sgtt", (128, 2, 128), BF16)
    S("etac", (128, 2), F32)
    psQ = [ctx.enter_context(nc.psum_tensor(f"psQ{i}", [128, 512], F32)) for i in range(2)]
    psKV = [ctx.enter_context(nc.psum_tensor(f"psKV{i}", [128, 512], F32)) for i in range(2)]
    psG = ctx.enter_context(nc.psum_tensor("psG", [128, 512], F32))
    psGT = [ctx.enter_context(nc.psum_tensor(f"psGT{i}", [128, 128], F32)) for i in range(2)]

    for s_ in ("ldB", "tQ", "tKV", "tG", "tGT", "tB", "sQ1", "vQ", "sKV1",
               "vKV", "sG1", "sGT1", "stB", "stG", "stq2"):
        em.sem(s_)

    if has("B"):
        def b_init(v):
            v.memset(sb["krawt"][:], 0.0).then_inc(sems["vKV"], 1)
        cnt["vKV"] += 1
        em.emit("vector", b_init)

        def b_loads(sync):
            for m in range(NM):
                for j in range(4):
                    i = 4 * m + j
                    if m >= 2 and has("Bpe") and has("Bsc"):
                        sync.wait_ge(sems["tGT"], SCM * (m - 1))
                    sync.wait_ge(sems["stq"], 16 * (i + 1) * (2 if debug else 1))
                    sync.dma_start(out=sb["xnt_m"][:, m % 2, :, 128 * j:128 * (j + 1)],
                                   in_=XN[i].rearrange("p (k f) -> p k f", k=KC),
                                   transpose=True).then_inc(sems["ldB"], 16)
        cnt["ldB"] += 16 * 4 * NM
        em.emit("sync", b_loads)
    if has("B") and has("Bpe"):

        def b_pe(t):
            t.wait_ge(sems["ldw"], 64)
            for m in range(NM):
                ms = m % 2
                t.wait_ge(sems["ldB"], 16 * 4 * (m + 1))
                xm = sb["xnt_m"][:, ms, :, :]
                for mch in range(4):
                    if 4 * m + mch >= 2 and has("Bsc"):
                        t.wait_ge(sems["vQ"], 4 * m + mch - 1)
                    mm = None
                    for k in range(KC):
                        mm = t.matmul(psQ[mch % 2][:],
                                      sb["wq_b"][:, k, 128 * mch:128 * (mch + 1)],
                                      xm[:, k, :], start=(k == 0), stop=(k == KC - 1))
                    mm.then_inc(sems["tQ"], 1)
                for sc in range(SCM):
                    if 16 * m + sc >= 2 and has("Bsc"):
                        t.wait_ge(sems["sKV1"], 16 * m + sc - 1)
                    mm = None
                    for p in range(HPC):
                        for k in range(KC):
                            mm = t.matmul(psKV[sc % 2][32 * p:32 * p + 32, 0:257],
                                          xm[:, k, 32 * sc:32 * sc + 32],
                                          sb["wkvb_b"][:, k, p, :],
                                          start=(k == 0), stop=(k == KC - 1),
                                          tile_position=(0, 32 * p))
                    mm.then_inc(sems["tKV"], 1)
                if m >= 2 and has("Bsc"):
                    t.wait_ge(sems["sG1"], m - 1)
                mm = None
                for k in range(KC):
                    mm = t.matmul(psG[:], sb["wg1_b"][:, k, :], xm[:, k, :],
                                  start=(k == 0), stop=(k == KC - 1))
                mm.then_inc(sems["tG"], 1)
                if has("Bsc"):
                    t.wait_ge(sems["sG1"], m + 1)
                for sc in range(SCM):
                    if 16 * m + sc >= 2 and has("Bsc"):
                        t.wait_ge(sems["sGT1"], 16 * m + sc - 1)
                    mm = None
                    for p in range(HPC):
                        mm = t.matmul(psGT[sc % 2][32 * p:32 * p + 32, :],
                                      sb["g1sb"][:, ms, 32 * sc:32 * sc + 32],
                                      sb["wg2_b"][:, 128 * p:128 * (p + 1)],
                                      start=True, stop=True,
                                      tile_position=(0, 32 * p))
                    mm.then_inc(sems["tGT"], 1)
        if has("Bpe"):
            cnt["tQ"] += 4 * NM
            cnt["tKV"] += SCM * NM
            cnt["tG"] += NM
            cnt["tGT"] += SCM * NM
            em.emit("tensor", b_pe)

        def b_scalar(s):
            for m in range(NM):
                ms = m % 2
                for mch in range(4):
                    s.wait_ge(sems["tQ"], 4 * m + mch + 1)
                    s.activation(sb["sigq"][:], psQ[mch % 2][:], AF.Sigmoid).then_inc(sems["sQ1"], 1)
                    s.wait_ge(sems["vQ"], 4 * m + mch + 1)
                    s.dma_start(out=QTD[mch][:, 512 * m:512 * (m + 1)],
                                in_=sb["qsil"][:]).then_inc(sems["stq2"], 16)
                    if debug:
                        s.dma_start(out=dbg["dbg_qtd"][mch][:, 512 * m:512 * (m + 1)],
                                    in_=sb["qsil"][:]).then_inc(sems["stq2"], 16)
                    s.wait_ge(sems["stq2"], cnt_stq2_after(m, mch, debug))
                for sc in range(SCM):
                    scl = sc % 2
                    g = NSC_IDX(m, sc)
                    s.wait_ge(sems["tKV"], 16 * m + sc + 1)
                    s.activation(sb["krawt"][:, scl, 0:128], psKV[sc % 2][:, 0:128],
                                 AF.Copy)
                    s.activation(sb["vt"][:, scl, :], psKV[sc % 2][:, 129:257],
                                 AF.Copy)
                    s.activation(sb["etac"][:, scl:scl + 1], psKV[sc % 2][:, 128:129],
                                 AF.Sigmoid).then_inc(sems["sKV1"], 1)
                    s.wait_ge(sems["vKV"], 16 * m + sc + 2)
                    s.dma_start(out=KRAW[g][:, 0:132],
                                in_=sb["krawt"][:, scl, :]).then_inc(sems["stB"], 16)
                    s.dma_start(out=VCH[g], in_=sb["vt"][:, scl, :]).then_inc(sems["stB"], 16)
                    if debug:
                        s.dma_start(out=dbg["dbg_kraw"][g][:, 0:132],
                                    in_=sb["krawt"][:, scl, :]).then_inc(sems["stB"], 16)
                        s.dma_start(out=dbg["dbg_vch"][g], in_=sb["vt"][:, scl, :]).then_inc(sems["stB"], 16)
                    s.wait_ge(sems["stB"], stb_after(m, sc, debug))
                s.wait_ge(sems["tG"], m + 1)
                s.activation(sb["g1sb"][:, ms, :], psG[:], AF.Copy).then_inc(sems["sG1"], 1)
                for sc in range(SCM):
                    scl = sc % 2
                    g = NSC_IDX(m, sc)
                    s.wait_ge(sems["tGT"], 16 * m + sc + 1)
                    s.activation(sb["sgtt"][:, scl, :], psGT[sc % 2][:],
                                 AF.Sigmoid).then_inc(sems["sGT1"], 1)
                    s.dma_start(out=SGT[g], in_=sb["sgtt"][:, scl, :]).then_inc(sems["stG"], 16)
                    if debug:
                        s.dma_start(out=dbg["dbg_sgt"][g], in_=sb["sgtt"][:, scl, :]).then_inc(sems["stG"], 16)
                    s.wait_ge(sems["stG"], stg_after(m, sc, debug))
        def NSC_IDX(m, sc):
            return 16 * m + sc
        def cnt_stq2_after(m, mch, dbgf):
            per = 32 if dbgf else 16
            return per * (4 * m + mch + 1)
        def stb_after(m, sc, dbgf):
            per = 64 if dbgf else 32
            return per * (16 * m + sc + 1)
        def stg_after(m, sc, dbgf):
            per = 32 if dbgf else 16
            return per * (16 * m + sc + 1)
        if has("Bpe") and has("Bsc"):
            cnt["sQ1"] += 4 * NM
            cnt["stq2"] += (32 if debug else 16) * 4 * NM
            cnt["sG1"] += NM
            cnt["sKV1"] += SCM * NM
            cnt["sGT1"] += SCM * NM
            cnt["stB"] += (64 if debug else 32) * SCM * NM
            cnt["stG"] += (32 if debug else 16) * SCM * NM
            em.emit("scalar", b_scalar)

        def b_vec(v):
            for m in range(NM):
                for mch in range(4):
                    v.wait_ge(sems["sQ1"], 4 * m + mch + 1)
                    v.tensor_tensor(sb["qsil"][:], psQ[mch % 2][:], sb["sigq"][:],
                                    ALU.mult).then_inc(sems["vQ"], 1)
                for sc in range(SCM):
                    scl = sc % 2
                    v.wait_ge(sems["sKV1"], 16 * m + sc + 1)
                    v.tensor_scalar(sb["krawt"][:, scl, 128:129],
                                    sb["etac"][:, scl:scl + 1], 2.0, None,
                                    ALU.mult).then_inc(sems["vKV"], 1)
        if has("Bpe") and has("Bsc"):
            cnt["vQ"] += 4 * NM
            cnt["vKV"] += SCM * NM
            em.emit("vector", b_vec)

    NPREP = 3
    S("krsb", (128, NPREP, 132), BF16)
    S("vsb", (128, NPREP, 128), BF16)
    S("ssb", (128, NPREP, 128), BF16)
    S("qisb", (128, NPREP, 4, 32), BF16)
    S("kisb", (128, NPREP, 128), BF16)
    S("kesb", (128, NPREP, 128), BF16)
    S("kitsb", (128, NPREP, 128), BF16)
    S("qchsb", (128, NPREP, 128), BF16)
    S("ccsb", (128, NPREP, 128), F32)
    S("cc128", (128, NPREP, 128), F32)
    S("vmk", (128, 128), BF16)
    S("ccol", (128, NPREP, 8), F32)
    S("S_sb", (128, 4, 128), BF16)
    S("khsb", (128, 128), BF16)
    S("t1sb", (128, 128), F32)
    S("redb", (128, 4, 128), F32)
    S("scr", (128, 128), F32)
    S("ats", (128, 32), F32)
    S("usb", (128, 128), BF16)
    S("ubd", (128, 512), BF16)
    S("atm", (128, 32), BF16)
    S("kcol", (128, 16), F32)
    S("ohsb", (128, 2, 128), BF16)
    S("mrg", (128, 128), BF16)
    S("ogsb", (128, 128), BF16)
    S("ogts", (128, 128), BF16)
    S("ocol", (128, 4), F32)
    S("w_bt", (128, 128), F32)
    S("w2n_bt", (128, 128), F32)
    S("b_bt", (128, 128), F32)
    S("wbb_bt", (128, 128), F32)
    S("nw_bt", (128, 128), F32)
    S("mskE_t", (128, 128), BF16)
    S("ident_t", (128, 128), BF16)
    S("ones_t", (128, 128), F32)

    psS = psQ[0]
    psCH = psGT[0]
    psA = psGT[1]
    psQHb = psKV
    psB = ctx.enter_context(nc.psum_tensor("psB", [128, 128], BF16))

    for s_ in ("vInit", "gInit", "ldp", "vPR", "sPR", "tPR", "tKH", "tA",
               "tCU", "tSU", "sE", "vR", "sR", "sCB", "vU", "vBD", "sSC",
               "sAT", "vAT", "sOH", "vOG", "vOG2", "tOG", "sEV", "stOG", "stDG", "vCX",
               "ccx", "ldo", "tWO", "sWO", "vWO", "stO", "ldw2", "tF", "ldf"):
        em.sem(s_)

    CP = int(os.environ.get("KB_NCHUNK", 2 * NSC))
    vcx_n = [0]
    NSCC = (CP + 1) // 2

    if has("C"):
        if debug:
            DBG_S = dout("dbg_S", (128, 512), BF16)
            DBG_KH = dout("dbg_kh", (8, 128, 128), BF16)
            DBG_U = dout("dbg_u", (8, 128, 128), BF16)
            DBG_KC = dout("dbg_kc", (8, 128, 16), F32)
            DBG_CC = dout("dbg_cc", (8, 128, 128), F32)
            DBG_CL = dout("dbg_cl", (8, 128, 8), F32)
            DBG_SS = dout("dbg_ss", (8, 128, 512), BF16)

        def c_init_g(g):
            g.memset(sb["ident_t"][:], 0.0)
            g.affine_select(out=sb["ident_t"][:], in_=sb["ident_t"][:],
                            compare_op=ALU.not_equal, fill=1.0, base=0,
                            pattern=[[-1, 128]], channel_multiplier=1)
            g.tensor_copy(sb["scr"][:, 0:1], sb["ident_t"][:, 0:1]).then_inc(sems["gInit"], 1)
        cnt["gInit"] += 1
        em.emit("gpsimd", c_init_g)

        def c_init_vec(v):
            v.wait_ge(sems["sPR"], 3)
            v.memset(sb["S_sb"][:], 0.0)
            v.memset(sb["ubd"][:], 0.0)
            v.memset(sb["ones_t"][:], 1.0)
            v.scalar_tensor_tensor(sb["w2n_bt"][:], sb["w_bt"][:], -1.0, sb["w_bt"][:],
                                   ALU.mult, ALU.mult).then_inc(sems["vInit"], 1)
            v.wait_ge(sems["vInit"], 1)
            v.tensor_tensor(sb["wbb_bt"][:], sb["w_bt"][:], sb["b_bt"][:], ALU.mult)
            v.tensor_scalar(sb["mskE_t"][:], sb["ones_t"][:], CS[:, C_MSKE:C_MSKE + 1],
                            None, ALU.mult).then_inc(sems["vInit"], 1)
        cnt["vInit"] += 2
        em.emit("vector", c_init_vec)

        def c_loads(sync):
            for g in range(NSCC):
                sl = g % NPREP
                if g >= NPREP:
                    sync.wait_ge(sems["tSU"], 2 * (g - NPREP) + 2)
                    sync.wait_ge(sems["vOG2"], g - NPREP + 1)
                sync.wait_ge(sems["stB"], (64 if debug else 32) * (g + 1))
                sync.wait_ge(sems["stG"], (32 if debug else 16) * (g + 1))
                sync.dma_start(out=sb["krsb"][:, sl, :], in_=KRAW[g]).then_inc(sems["ldp"], 16)
                sync.dma_start(out=sb["vsb"][:, sl, :], in_=VCH[g]).then_inc(sems["ldp"], 16)
                sync.dma_start(out=sb["ssb"][:, sl, :], in_=SGT[g]).then_inc(sems["ldp"], 16)
                sync.wait_ge(sems["stq2"], (32 if debug else 16) * 4 * min(NM, g // 16 + 1))
                sync.dma_start(out=sb["qisb"][:, sl, :, :],
                               in_=QTD[:, :, 32 * g:32 * (g + 1)].rearrange("c p t -> p c t")
                               ).then_inc(sems["ldp"], 16)
        cnt["ldp"] += 64 * NSCC
        em.emit("sync", c_loads)

        psb_n = {"pe": 0}

        def psb_wait(t):
            k = psb_n["pe"]
            if k > 0:
                t.wait_ge(sems["sEV"], k)
            psb_n["pe"] = k + 1

        def c_pe(t):
            t.wait_ge(sems["vQ"], cnt["vQ"])
            t.wait_ge(sems["sKV1"], cnt["sKV1"])
            t.wait_ge(sems["sGT1"], cnt["sGT1"])
            t.wait_ge(sems["sG1"], cnt["sG1"])
            t.wait_ge(sems["ld"], 64)
            t.wait_ge(sems["gInit"], 1)
            for i in range(3):
                rhs = (sb["lnw_s"], sb["lnb_s"], sb["nrmw_s"])[i]
                t.matmul(psA[:, 0:128], CS[0:4, 80:208], rhs[:],
                         start=True, stop=True).then_inc(sems["tPR"], 1)
                t.wait_ge(sems["sPR"], i + 1)
            for g in range(NSCC + 1):
                if g < NSCC:
                    sl = g % NPREP
                    t.wait_ge(sems["vPR"], 3 * g + 2)
                    psb_wait(t)
                    t.transpose(psB[:], sb["kisb"][:, sl, :], sb["ident_t"][:]).then_inc(sems["tPR"], 1)
                    psb_wait(t)
                    t.transpose(psB[:], sb["qisb"][:, sl, :, :].rearrange("p c t -> p (c t)"),
                                sb["ident_t"][:]).then_inc(sems["tPR"], 1)
                if g >= 1:
                    gg = g - 1
                    sl = gg % NPREP
                    for par in (0, 1):
                        ch = 2 * gg + par
                        if ch >= CP:
                            break
                        t.wait_ge(sems["sPR"], 3 + 4 * gg + 4)
                        t.wait_ge(sems["sSC"], ch)
                        if ch == 0:
                            t.wait_ge(sems["vInit"], 2)
                        t.wait_ge(sems["sE"], ch)
                        if gg >= 1:
                            t.wait_ge(sems["sOH"], 2 * (gg - 1) + par + 1)
                        mm = None
                        for p in range(4):
                            mm = t.matmul(psCH[32 * p:32 * p + 32, :],
                                          sb["kitsb"][:, sl, 32 * p:32 * p + 32],
                                          sb["S_sb"][:, p, :], start=True, stop=True,
                                          tile_position=(0, 32 * p))
                        mm.then_inc(sems["tKH"], 1)
                        mm = None
                        for p in range(4):
                            mm = t.matmul(psA[32 * p:32 * p + 32, 0:32],
                                          sb["kitsb"][:, sl, 32 * p:32 * p + 32],
                                          sb["qisb"][:, sl, p, :], start=True, stop=True,
                                          tile_position=(0, 32 * p))
                        mm.then_inc(sems["tA"], 1)
                        t.wait_ge(sems["vAT"], ch + 1)
                        t.wait_ge(sems["vU"], ch + 1)
                        mm = None
                        for p in range(4):
                            t.matmul(psQHb[par][32 * p:32 * p + 32, 0:128],
                                     sb["atm"][32 * p:32 * p + 32, :],
                                     sb["usb"][32 * p:32 * p + 32, :],
                                     start=True, stop=False,
                                     tile_position=(32 * p, 32 * p))
                            mm = t.matmul(psQHb[par][32 * p:32 * p + 32, 0:128],
                                          sb["qisb"][:, sl, p, :],
                                          sb["S_sb"][:, p, :], start=False, stop=True,
                                          tile_position=(0, 32 * p))
                        mm.then_inc(sems["tCU"], 1)
                        t.wait_ge(sems["vBD"], ch + 1)
                        t.matmul(psS[:], sb["kesb"][:, sl, :], sb["ubd"][:],
                                 start=(ch == 0), stop=True,
                                 skip_group_check=(ch > 0)).then_inc(sems["tSU"], 1)
                    t.wait_ge(sems["vOG"], gg + 1)
                    psb_wait(t)
                    t.transpose(psB[:], sb["ogsb"][:], sb["ident_t"][:]).then_inc(sems["tOG"], 1)
        cnt["tPR"] += 3 + 2 * NSCC
        cnt["tKH"] += CP
        cnt["tA"] += CP
        cnt["tCU"] += CP
        cnt["tSU"] += CP
        cnt["tOG"] += NSCC
        em.emit("tensor", c_pe)

        scb_n = [0]
        sr_n = [0]
        sr_at_scan = {}
        sr_at_o = {}

        def c_act(s):
            for i, dstn in enumerate(("w_bt", "b_bt", "nw_bt")):
                s.wait_ge(sems["tPR"], i + 1)
                s.activation(sb[dstn][:], psA[:, 0:128], AF.Copy).then_inc(sems["sPR"], 1)
            for g in range(NSCC + 1):
                if g < NSCC:
                    sl = g % NPREP
                    s.wait_ge(sems["vPR"], 3 * g + 1)
                    s.activation(sb["ccol"][:, sl, 3:4], sb["ccol"][:, sl, 2:3], AF.Ln,
                                 bias=CS[:, C_EPSI:C_EPSI + 1]).then_inc(sems["sPR"], 1)
                    s.wait_ge(sems["sPR"], 3 + 4 * g + 1)
                    s.activation(sb["ccol"][:, sl, 0:1], sb["ccol"][:, sl, 3:4], AF.Exp,
                                 scale=-0.5).then_inc(sems["sPR"], 1)
                    s.wait_ge(sems["tPR"], 3 + 2 * g + 1)
                    s.activation(sb["kitsb"][:, sl, :], psB[:], AF.Copy).then_inc(sems["sPR"], 1)
                    s.activation(sb["scr"][:, 0:1], sb["kitsb"][:, sl, 0:1], AF.Copy).then_inc(sems["sEV"], 1)
                    s.wait_ge(sems["tPR"], 3 + 2 * g + 2)
                    s.activation(sb["qchsb"][:, sl, :], psB[:], AF.Copy).then_inc(sems["sPR"], 1)
                    s.activation(sb["scr"][:, 0:1], sb["qchsb"][:, sl, 0:1], AF.Copy).then_inc(sems["sEV"], 1)
                if g >= 1:
                    gg = g - 1
                    sl = gg % NPREP
                    for par in (0, 1):
                        ch = 2 * gg + par
                        if ch >= CP:
                            break
                        s.wait_ge(sems["tKH"], ch + 1)
                        if ch >= 1:
                            s.wait_ge(sems["vU"], ch)
                        s.activation(sb["khsb"][:], psCH[:, 0:128], AF.Copy).then_inc(sems["sE"], 1)
                        s.wait_ge(sems["vR"], ch + 1)
                        if ch >= 1:
                            s.wait_ge(sems["vU"], ch)
                        s.activation(sb["kcol"][:, 15:16], sb["kcol"][:, 0:1], AF.Ln,
                                     scale=1.0 / 128,
                                     bias=CS[:, C_EPSI:C_EPSI + 1]).then_inc(sems["sCB"], 1)
                        scb_n[0] += 1
                        s.wait_ge(sems["sCB"], scb_n[0])
                        s.activation(sb["kcol"][:, 12:13], sb["kcol"][:, 15:16], AF.Exp,
                                     scale=-0.5).then_inc(sems["sCB"], 1)
                        scb_n[0] += 1
                        s.wait_ge(sems["sCB"], scb_n[0])
                        s.activation(sb["kcol"][:, 13:14], sb["kcol"][:, 12:13],
                                     AF.Square).then_inc(sems["sR"], 1)
                        sr_n[0] += 1
                        sr_at_scan[ch] = sr_n[0]
                        s.wait_ge(sems["tA"], ch + 1)
                        s.activation(sb["ats"][:], psA[:, 0:32], AF.Copy,
                                     scale=sb["ccol"][:, sl, 4:5]).then_inc(sems["sAT"], 1)
                        s.wait_ge(sems["tSU"], ch + 1)
                        s.activation(sb["S_sb"][:].rearrange("p a b -> p (a b)"), psS[:],
                                     AF.Copy).then_inc(sems["sSC"], 1)
                        if debug and ch < 8:
                            s.wait_ge(sems["vU"], ch + 1)
                            s.dma_start(out=DBG_KH[ch], in_=sb["khsb"][:]).then_inc(sems["stDG"], 16)
                            s.dma_start(out=DBG_U[ch], in_=sb["usb"][:]).then_inc(sems["stDG"], 16)
                            s.dma_start(out=DBG_KC[ch], in_=sb["kcol"][:]).then_inc(sems["stDG"], 16)
                            s.dma_start(out=DBG_CC[ch], in_=sb["ccsb"][:, sl, :]).then_inc(sems["stDG"], 16)
                            s.dma_start(out=DBG_CL[ch][:, 0:5], in_=sb["ccol"][:, sl, 0:5]).then_inc(sems["stDG"], 16)
                            s.wait_ge(sems["sSC"], ch + 1)
                            s.dma_start(out=DBG_SS[ch],
                                        in_=sb["S_sb"][:].rearrange("p a b -> p (a b)")).then_inc(sems["stDG"], 16)
                    s.wait_ge(sems["tCU"], 2 * gg + 2)
                    s.activation(sb["ohsb"][:, 0, :], psQHb[0][:, 0:128], AF.Copy).then_inc(sems["sOH"], 1)
                    s.activation(sb["ohsb"][:, 1, :], psQHb[1][:, 0:128], AF.Copy).then_inc(sems["sOH"], 1)
                    s.wait_ge(sems["ccx"], 2 * gg + 1)
                    s.activation(sb["ocol"][:, 2:3], sb["ocol"][:, 0:1], AF.Ln,
                                 scale=1.0 / 128, bias=CS[:, C_EPSI:C_EPSI + 1]).then_inc(sems["sCB"], 1)
                    scb_n[0] += 1
                    s.wait_ge(sems["sCB"], scb_n[0])
                    s.activation(sb["ocol"][:, 1:2], sb["ocol"][:, 2:3], AF.Exp,
                                 scale=-0.5).then_inc(sems["sR"], 1)
                    sr_n[0] += 1
                    sr_at_o[(gg, 1)] = sr_n[0]
                    s.wait_ge(sems["ccx"], 2 * gg + 2)
                    s.activation(sb["ocol"][:, 2:3], sb["ocol"][:, 3:4], AF.Ln,
                                 scale=1.0 / 128, bias=CS[:, C_EPSI:C_EPSI + 1]).then_inc(sems["sCB"], 1)
                    scb_n[0] += 1
                    s.wait_ge(sems["sCB"], scb_n[0])
                    s.activation(sb["ocol"][:, 3:4], sb["ocol"][:, 2:3], AF.Exp,
                                 scale=-0.5).then_inc(sems["sR"], 1)
                    sr_n[0] += 1
                    sr_at_o[(gg, 2)] = sr_n[0]
                    s.wait_ge(sems["tOG"], gg + 1)
                    if gg >= 1:
                        s.wait_ge(sems["stOG"], 16 * gg)
                    s.activation(sb["ogts"][:], psB[:], AF.Copy).then_inc(sems["sEV"], 1)
                    blk = gg // 32
                    s.dma_start(out=OGT[blk].rearrange("(p e) (s u) -> e p s u", e=128, u=32)[:, :, gg % 32, :],
                                in_=sb["ogts"][:].rearrange("p (c u) -> p c u", u=32)
                                ).then_inc(sems["stOG"], 16)
                    if debug:
                        s.dma_start(out=dbg["dbg_ogt"][blk].rearrange("(p e) (s u) -> e p s u", e=128, u=32)[:, :, gg % 32, :],
                                    in_=sb["ogts"][:].rearrange("p (c u) -> p c u", u=32)
                                    ).then_inc(sems["stDG"], 16)
            if debug:
                s.wait_ge(sems["sSC"], CP)
                s.dma_start(out=DBG_S, in_=sb["S_sb"][:].rearrange("p a b -> p (a b)")
                            ).then_inc(sems["stDG"], 16)

        cnt["sPR"] += 3 + 4 * NSCC
        cnt["sE"] += CP
        cnt["sCB"] += 2 * CP + 2 * NSCC
        cnt["sR"] += CP + 2 * NSCC
        cnt["sAT"] += CP
        cnt["sSC"] += CP
        cnt["sOH"] += 2 * NSCC
        cnt["sEV"] += 3 * NSCC
        cnt["stOG"] += 16 * NSCC
        if debug:
            cnt["stDG"] += 16 * (NSCC + 6 * min(CP, 8) + 1)
        em.emit("scalar", c_act)

        def c_vec(v):
            for g in range(NSCC + 1):
                if g < NSCC:
                    sl = g % NPREP
                    v.wait_ge(sems["ldp"], 64 * (g + 1))
                    if g >= NPREP:
                        v.wait_ge(sems["vU"], 2 * (g - NPREP) + 2)
                    v.tensor_copy(sb["ccol"][:, sl, 4:5], sb["krsb"][:, sl, 128:129])
                    v.scalar_tensor_tensor(sb["scr"][:], sb["krsb"][:, sl, 0:128], 1.0,
                                           sb["krsb"][:, sl, 0:128], ALU.mult, ALU.mult)
                    v.reduce_sum(sb["ccol"][:, sl, 2:3], sb["scr"][:],
                                 axis=mybir.AxisListType.X).then_inc(sems["vPR"], 1)
                    v.wait_ge(sems["sPR"], 3 + 4 * g + 2)
                    v.tensor_scalar(sb["kisb"][:, sl, :], sb["krsb"][:, sl, 0:128],
                                    sb["ccol"][:, sl, 0:1], None, ALU.mult)
                    v.tensor_scalar(sb["kesb"][:, sl, :], sb["kisb"][:, sl, :],
                                    sb["ccol"][:, sl, 4:5], None, ALU.mult).then_inc(sems["vPR"], 1)
                    v.tensor_sub(sb["vmk"][:], sb["vsb"][:, sl, :], sb["kisb"][:, sl, :])
                    v.scalar_tensor_tensor(sb["scr"][:], sb["vmk"][:], -1.0, sb["w_bt"][:],
                                           ALU.mult, ALU.mult)
                    v.tensor_add(sb["ccsb"][:, sl, :], sb["scr"][:], sb["wbb_bt"][:])
                    v.tensor_scalar(sb["cc128"][:, sl, :], sb["ccsb"][:, sl, :],
                                    1.0 / 128, None, ALU.mult)
                    v.reduce_sum(sb["ccol"][:, sl, 1:2], sb["cc128"][:, sl, :],
                                 axis=mybir.AxisListType.X).then_inc(sems["vPR"], 1)
                if g >= 1:
                    gg = g - 1
                    sl = gg % NPREP
                    for par in (0, 1):
                        ch = 2 * gg + par
                        if ch >= CP:
                            break
                        mask = CS[:, C_MSKE + par:C_MSKE + par + 1]
                        am0 = C_AME if par == 0 else C_AMO
                        amask = CS[:, am0:am0 + 32]
                        v.wait_ge(sems["sE"], ch + 1)
                        v.scalar_tensor_tensor(sb["redb"][:, 0, :], sb["khsb"][:], 1.0,
                                               sb["khsb"][:], ALU.mult, ALU.mult)
                        v.scalar_tensor_tensor(sb["redb"][:, 1, :], sb["khsb"][:], 1.0,
                                               sb["w2n_bt"][:], ALU.mult, ALU.mult)
                        v.scalar_tensor_tensor(sb["redb"][:, 2, :], sb["redb"][:, 1, :], 1.0,
                                               sb["khsb"][:], ALU.mult, ALU.mult)
                        v.scalar_tensor_tensor(sb["redb"][:, 3, :], sb["khsb"][:], 1.0,
                                               sb["cc128"][:, sl, :], ALU.mult, ALU.mult)
                        v.reduce_sum(sb["kcol"][:, 0:4], sb["redb"][:],
                                     axis=mybir.AxisListType.X).then_inc(sems["vR"], 1)
                        v.wait_ge(sems["sAT"], ch + 1)
                        if ch >= 1:
                            v.wait_ge(sems["tCU"], ch)
                        v.tensor_tensor(sb["atm"][:], sb["ats"][:], amask, ALU.mult).then_inc(sems["vAT"], 1)
                        v.wait_ge(sems["sR"], sr_at_scan[ch])
                        kc = sb["kcol"]
                        v.tensor_scalar(kc[:, 5:6], kc[:, 2:3], -1.0 / 128, None, ALU.mult)
                        v.tensor_scalar(kc[:, 6:7], kc[:, 1:2], -1.0 / 128, None, ALU.mult)
                        v.tensor_tensor(kc[:, 7:8], kc[:, 12:13], mask, ALU.mult)
                        v.tensor_tensor(kc[:, 8:9], kc[:, 13:14], mask, ALU.mult)
                        v.tensor_tensor(kc[:, 9:10], kc[:, 3:4], kc[:, 12:13], ALU.mult)
                        lastd1 = v.tensor_tensor(kc[:, 11:12], kc[:, 12:13],
                                                 sb["ccol"][:, sl, 1:2], ALU.mult)
                        lastd1.then_inc(sems["vCX"], 1)
                        vcx_n[0] += 1
                        v.wait_ge(sems["vCX"], vcx_n[0])
                        v.scalar_tensor_tensor(kc[:, 10:11], kc[:, 5:6], kc[:, 13:14], kc[:, 9:10],
                                               ALU.mult, ALU.add)
                        v.scalar_tensor_tensor(kc[:, 4:5], kc[:, 6:7], kc[:, 13:14], kc[:, 11:12],
                                               ALU.mult, ALU.add).then_inc(sems["vCX"], 1)
                        vcx_n[0] += 1
                        v.wait_ge(sems["vCX"], vcx_n[0])
                        v.tensor_tensor(kc[:, 15:16], kc[:, 4:5], mask, ALU.mult)
                        v.tensor_scalar(kc[:, 14:15], kc[:, 7:8], -1.0, None,
                                        ALU.mult).then_inc(sems["vCX"], 1)
                        vcx_n[0] += 1
                        v.scalar_tensor_tensor(sb["scr"][:], sb["khsb"][:], kc[:, 10:11],
                                               sb["redb"][:, 1, :], ALU.mult, ALU.add)
                        v.wait_ge(sems["vCX"], vcx_n[0])
                        v.tensor_scalar(sb["t1sb"][:], sb["ccsb"][:, sl, :], kc[:, 14:15],
                                        kc[:, 15:16], ALU.mult, ALU.add)
                        if ch >= 1:
                            v.wait_ge(sems["tSU"], ch)
                        v.scalar_tensor_tensor(sb["usb"][:], sb["scr"][:], kc[:, 8:9],
                                               sb["t1sb"][:], ALU.mult, ALU.add).then_inc(sems["vU"], 1)
                        cp = None
                        for p in range(4):
                            cp = v.tensor_copy(sb["ubd"][32 * p:32 * p + 32, 128 * p:128 * (p + 1)],
                                               sb["usb"][32 * p:32 * p + 32, :])
                        cp.then_inc(sems["vBD"], 1)
                    v.wait_ge(sems["sOH"], 2 * gg + 2)
                    v.tensor_sub(sb["scr"][:], sb["ohsb"][:, 0, :], sb["ohsb"][:, 1, :])
                    v.scalar_tensor_tensor(sb["mrg"][:], sb["scr"][:],
                                           CS[:, C_MSKE:C_MSKE + 1], sb["ohsb"][:, 1, :],
                                           ALU.mult, ALU.add)
                    v.scalar_tensor_tensor(sb["scr"][:], sb["mrg"][:], 1.0, sb["mrg"][:],
                                           ALU.mult, ALU.mult)
                    v.reduce_sum(sb["ocol"][:, 0:1], sb["scr"][:],
                                 axis=mybir.AxisListType.X).then_inc(sems["ccx"], 1)
                    v.wait_ge(sems["sR"], sr_at_o[(gg, 1)])
                    v.tensor_scalar(sb["scr"][:], sb["mrg"][:], sb["ocol"][:, 1:2], None,
                                    ALU.mult)
                    v.scalar_tensor_tensor(sb["scr"][:], sb["scr"][:], 1.0, sb["w_bt"][:],
                                           ALU.mult, ALU.mult)
                    v.tensor_add(sb["scr"][:], sb["scr"][:], sb["b_bt"][:])
                    v.tensor_add(sb["scr"][:], sb["scr"][:], sb["qchsb"][:, sl, :])
                    v.tensor_tensor(sb["scr"][:], sb["scr"][:], sb["ssb"][:, sl, :], ALU.mult)
                    v.scalar_tensor_tensor(sb["t1sb"][:], sb["scr"][:], 1.0, sb["scr"][:],
                                           ALU.mult, ALU.mult)
                    v.reduce_sum(sb["ocol"][:, 3:4], sb["t1sb"][:],
                                 axis=mybir.AxisListType.X).then_inc(sems["ccx"], 1)
                    v.wait_ge(sems["sR"], sr_at_o[(gg, 2)])
                    v.tensor_scalar(sb["scr"][:], sb["scr"][:], sb["ocol"][:, 3:4], None,
                                    ALU.mult)
                    if gg >= 1:
                        v.wait_ge(sems["tOG"], gg)
                    v.tensor_tensor(sb["ogsb"][:], sb["scr"][:], sb["nw_bt"][:],
                                    ALU.mult).then_inc(sems["vOG"], 1)
                    v.nop().then_inc(sems["vOG2"], 1)

        cnt["vPR"] += 3 * NSCC
        cnt["vCX"] += 3 * CP
        cnt["vR"] += CP
        cnt["vAT"] += CP
        cnt["vU"] += CP
        cnt["vBD"] += CP
        cnt["ccx"] += 2 * NSCC
        cnt["vOG"] += NSCC
        cnt["vOG2"] += NSCC
        em.emit("vector", c_vec)

    S("xqt_t", (128, 4, KC, 128), BF16)
    S("ogt_t", (128, 2, 4, 128), BF16)
    S("wqt4", (128, KC, 4, 128), BF16)
    S("ycols", (128, 8), F32)
    S("t2col", (128, 8), F32)
    S("outsb", (128, 2, 512), F32)
    S("xqf", (128, D), F32)
    S("xqi", (128, D), I32)
    S("xqd2", (128, D), F32)
    sb["wfq"] = sb["xn_tok"]

    S("wfcol", (128, 64), F32)
    S("wfsc", (128, 8), F32)
    S("wfsb", (128, 144), BF16)

    for s_ in ("ldwos", "ldog", "stRS", "sX2", "tWO", "ldrs", "ldr", "vX2",
               "ldf", "sWO", "tF", "vWO", "ldw2", "vQ2", "vT", "sT", "sT2",
               "vQX", "stX", "ldq", "ldff", "ldf2", "tFF", "vFF", "stO"):
        em.sem(s_)

    x2f = sb["xnt_m"][:].rearrange("p a k f -> p (a k f)")
    wov2 = sb["wq_b"][:].rearrange("p k c -> p (k c)")
    x2p = sb["g1sb"]
    LW2 = 3 if debug else 2
    DX = 2 if debug else 1
    psW = [psQ[0], psQ[1], psKV[0], psKV[1]]

    if has("T"):
        def t_gload(g):
            g.wait_ge(sems["tQ"], cnt["tQ"])
            wsr = WOS.rearrange("(c p) n -> c p n", p=128)
            for fc in range(4):
                g.dma_start(out=wov2[:, 2048 * fc:2048 * (fc + 1)],
                            in_=wsr[fc]).then_inc(sems["ldwos"], 16)

            def wfld(i):
                if i >= 2:
                    g.wait_ge(sems["sWO"], i - 1)
                g.dma_start(out=sb["xqf" if i % 2 == 0 else "xqd2"][:],
                            in_=WF.rearrange("(n p) d -> n p d", p=128)[i]
                            ).then_inc(sems["ldf"], 16)
            if has("F"):
                for i in range(5):
                    wfld(i)
            g.wait_ge(sems["stRS"], 16 * 128)
            g.collective_compute(
                "ReduceScatter", ALU.add,
                replica_groups=[[0, 1, 2, 3], [4, 5, 6, 7]],
                ins=[RSI.rearrange("b q d -> (b q d)").opt()],
                outs=[RSO.rearrange("q d -> (q d)").opt()],
            ).then_inc(sems["ccx"], 1)
            if has("F"):
                for i in range(5, KC):
                    wfld(i)
        cnt["ldwos"] += 64
        cnt["ccx"] += 1
        ccs_done = cnt["ccx"]
        if has("F"):
            cnt["ldf"] += 16 * 16
        em.emit("gpsimd", t_gload)

        def t_sync(sync):
            for tt in range(32):
                if tt >= 2:
                    sync.wait_ge(sems["tWO"], 4 * (tt - 1))
                sync.wait_ge(sems["stOG"], 16 * 4 * (tt + 1))
                sync.dma_start(out=sb["ogt_t"][:, tt % 2, :, :],
                               in_=OGT[tt // 8][:, 128 * (tt % 8):128 * (tt % 8) + 128]
                               .rearrange("(c p) t -> p c t", p=128)
                               ).then_inc(sems["ldog"], 16)
            sync.wait_ge(sems["ccx"], ccs_done)
            rsr = RSO.rearrange("(n p) d -> n p d", p=128)
            xqr = XQ.rearrange("(n p) d -> n p d", p=128)
            for t in range(8):
                sync.dma_start(out=x2f[:, 2048 * t:2048 * (t + 1)],
                               in_=rsr[t]).then_inc(sems["ldrs"], 16)
                if t >= 2:
                    sync.wait_ge(sems["vX2"], t - 1)
                sync.dma_start(out=sb["xtile"][:, t % 2, :],
                               in_=xqr[t]).then_inc(sems["ldr"], 16)
            def ldq1(t):
                sync.wait_ge(sems["stX"], 16 * DX * (t + 1))
                if t >= 4:
                    sync.wait_ge(sems["tFF"], 64)
                sync.dma_start(out=sb["xqt_t"][:, t % 4, :, :],
                               in_=XQD[t].rearrange("p (k f) -> p k f", f=128),
                               transpose=True).then_inc(sems["ldq"], 16)

            def wqt1(fg, a):
                nq = fg % 4
                if fg >= 1 and a == 0:
                    sync.wait_ge(sems["tFF"], 16 * fg)
                sync.wait_ge(sems["ldw2"], 16 * LW2 * (4 * nq + a + 1))
                sync.dma_start(out=sb["wqt4"][:, :, a, :],
                               in_=WQR[4 * nq + a].rearrange("p (k f) -> p k f", f=128),
                               transpose=True).then_inc(sems["ldff"], 16)
            for t in range(4):
                ldq1(t)
            for fg in range(4):
                for a in range(4):
                    wqt1(fg, a)
            for t in range(4, 8):
                ldq1(t)
            for fg in range(4, 8):
                for a in range(4):
                    wqt1(fg, a)
        cnt["ldog"] += 16 * 32
        cnt["ldrs"] += 16 * 8
        cnt["ldr"] += 16 * 8
        cnt["ldq"] += 16 * 8
        cnt["ldff"] += 16 * 4 * 8
        em.emit("sync", t_sync)

        def wo_pe(t):
            t.wait_ge(sems["ldwos"], 64)
            for grp in range(128):
                tt, nq = grp // 4, grp % 4
                if nq == 0:
                    t.wait_ge(sems["ldog"], 16 * (tt + 1))
                if grp >= 4:
                    t.wait_ge(sems["sX2"], grp - 3)
                for fc in range(4):
                    mm = t.matmul(psW[grp % 4][:, 0:512],
                                  sb["ogt_t"][:, tt % 2, fc, :],
                                  wov2[:, 2048 * fc + 512 * nq:2048 * fc + 512 * (nq + 1)],
                                  start=(fc == 0), stop=(fc == 3))
                mm.then_inc(sems["tWO"], 1)
        cnt["tWO"] += 128
        em.emit("tensor", wo_pe)

        def wo_act(s):
            for grp in range(128):
                tt, nq = grp // 4, grp % 4
                s.wait_ge(sems["tWO"], grp + 1)
                if grp >= 2:
                    s.wait_ge(sems["stRS"], 16 * (grp - 1))
                s.activation(x2p[:, grp % 2, :], psW[grp % 4][:, 0:512],
                             AF.Copy).then_inc(sems["sX2"], 1)
                s.dma_start(out=RSI[tt // 8][128 * (tt % 8):128 * (tt % 8) + 128,
                                            512 * nq:512 * (nq + 1)],
                            in_=x2p[:, grp % 2, :]).then_inc(sems["stRS"], 16)
        cnt["sX2"] += 128
        cnt["stRS"] += 16 * 128
        em.emit("scalar", wo_act)

        def t_vec(v):
            for t in range(8):
                v.wait_ge(sems["ldrs"], 16 * (t + 1))
                v.wait_ge(sems["ldr"], 16 * (t + 1))
                v.tensor_tensor(x2f[:, 2048 * t:2048 * (t + 1)],
                                x2f[:, 2048 * t:2048 * (t + 1)],
                                sb["xtile"][:, t % 2, :],
                                ALU.add).then_inc(sems["vX2"], 1)
        cnt["vX2"] += 8
        em.emit("vector", t_vec)

    if has("F"):
        def f1_act(s):
            red512 = sb["redb"][:].rearrange("p a f -> p (a f)")
            for i in range(KC):
                s.wait_ge(sems["ldf"], 16 * (i + 1))
                buf = sb["xqf" if i % 2 == 0 else "xqd2"][:]
                for j in range(4):
                    mm = s.activation(red512, buf[:, 512 * j:512 * (j + 1)], AF.Abs,
                                      accum_out=sb["wfcol"][:, 4 * i + j:4 * i + j + 1])
                mm.then_inc(sems["sWO"], 1)
        cnt["sWO"] += KC
        em.emit("scalar", f1_act)

        def f1_pe(t):
            t.wait_ge(sems["vWO"], 4)
            t.matmul(psGT[1][0:1, 0:1], sb["wfsb"][:, 1:2], sb["wfsb"][:, 0:1],
                     start=True, stop=True).then_inc(sems["tF"], 1)
            t.wait_ge(sems["vWO"], 8)
            t.matmul(psGT[1][:, 1:3], sb["wfsb"][0:1, 8:136], sb["wfsb"][0:1, 2:4],
                     start=True, stop=True).then_inc(sems["tF"], 1)
        cnt["tF"] += 2
        em.emit("tensor", f1_pe)

        def f1_vec(v):
            v.wait_ge(sems["ld"], 64)
            v.tensor_copy(sb["wfsb"][:, 1:2], CS[:, C_ONE:C_ONE + 1]).then_inc(sems["vWO"], 1)
            v.tensor_copy(sb["wfsb"][0:1, 8:136], CS[0:1, 208:336]).then_inc(sems["vWO"], 1)
            v.wait_ge(sems["sWO"], KC)
            v.reduce_sum(sb["wfsc"][:, 0:1], sb["wfcol"][:],
                         axis=mybir.AxisListType.X).then_inc(sems["vWO"], 1)
            v.wait_ge(sems["vWO"], 3)
            v.tensor_copy(sb["wfsb"][:, 0:1], sb["wfsc"][:, 0:1]).then_inc(sems["vWO"], 1)
            v.wait_ge(sems["tF"], 1)
            v.tensor_scalar(sb["wfsc"][0:1, 1:2], psGT[1][0:1, 0:1], 1.0 / (D * D),
                            None, ALU.mult).then_inc(sems["vWO"], 1)
            v.wait_ge(sems["vWO"], 5)
            v.tensor_scalar(sb["wfsc"][0:1, 2:3], sb["wfsc"][0:1, 1:2], 1e-6, None,
                            ALU.max).then_inc(sems["vWO"], 1)
            v.wait_ge(sems["vWO"], 6)
            v.reciprocal(sb["wfsc"][0:1, 3:4], sb["wfsc"][0:1, 2:3]).then_inc(sems["vWO"], 1)
            v.wait_ge(sems["vWO"], 7)
            v.tensor_copy(sb["wfsb"][0:1, 2:4], sb["wfsc"][0:1, 2:4]).then_inc(sems["vWO"], 1)
            v.wait_ge(sems["tF"], 2)
            v.tensor_copy(sb["wfsc"][:, 4:6], psGT[1][:, 1:3]).then_inc(sems["vWO"], 1)
        cnt["vWO"] += 9
        em.emit("vector", f1_vec)

        def f2_g(g):
            g.wait_ge(sems["sWO"], KC)
            g.wait_ge(sems["vX2"], 8)
            for i in range(KC):
                if i >= 2:
                    g.wait_ge(sems["vQ2"], i - 1)
                g.dma_start(out=sb["xtile"][:, i % 2, :],
                            in_=WF.rearrange("(n p) d -> n p d", p=128)[i]).then_inc(sems["ldw2"], 16)
                g.wait_ge(sems["vQ2"], i + 1)
                g.dma_start(out=WQR[i], in_=sb["wfq"][:, i % 2, :]).then_inc(sems["ldw2"], 16)
                if debug:
                    g.dma_start(out=dbg["dbg_wqt"][i], in_=sb["wfq"][:, i % 2, :]).then_inc(sems["ldw2"], 16)
        cnt["ldw2"] += 16 * KC * LW2
        em.emit("gpsimd", f2_g)

        def f2_vec(v):
            v.wait_ge(sems["vWO"], 9)
            for i in range(KC):
                v.wait_ge(sems["ldw2"], 16 * (LW2 * i + 1))
                v.tensor_scalar(sb["xqd2"][:], sb["xtile"][:, i % 2, :],
                                sb["wfsc"][:, 5:6], None, ALU.mult)
                v.tensor_scalar(sb["xqf"][:], sb["xqd2"][:], 0.5, None, ALU.is_gt)
                v.scalar_tensor_tensor(sb["wfq"][:, i % 2, :], sb["xqd2"][:], -0.5,
                                       sb["xqf"][:], ALU.is_lt,
                                       ALU.subtract).then_inc(sems["vQ2"], 1)
        cnt["vQ2"] += KC
        em.emit("vector", f2_vec)

    if has("T"):
        def t2_vec(v):
            v.wait_ge(sems["ldw2"], 16 * KC * LW2)
            for t in range(8):
                x2t = x2f[:, 2048 * t:2048 * (t + 1)]
                sl = t % 2
                for j in range(4):
                    v.bn_stats(sb["bnst"][:, 6 * j:6 * (j + 1)],
                               x2t[:, 512 * j:512 * (j + 1)]).then_inc(sems["vT"], 1)
                v.wait_ge(sems["vT"], 5 * t + 4)
                v.bn_aggr(sb["p0col"][:, sl, 0:2], sb["bnst"][:]).then_inc(sems["vT"], 1)
                v.wait_ge(sems["sT"], 2 * (t + 1))
                v.tensor_scalar(sb["xqf"][:], x2t, sb["p0col"][:, sl, 0:1],
                                sb["p0col"][:, sl, 2:3], ALU.subtract,
                                ALU.mult).then_inc(sems["vQX"], 1)
                v.wait_ge(sems["sT2"], t + 1)
                v.reduce_max(sb["t2col"][:, t:t + 1], sb["xqd2"][:],
                             axis=mybir.AxisListType.X).then_inc(sems["vQX"], 1)
                v.wait_ge(sems["vQX"], 12 * t + 2)
                v.tensor_scalar(sb["t2col"][:, t:t + 1], sb["t2col"][:, t:t + 1],
                                1e-8, None, ALU.max).then_inc(sems["vQX"], 1)
                v.wait_ge(sems["vQX"], 12 * t + 3)
                v.reciprocal(sb["p0col"][:, sl, 3:4],
                             sb["t2col"][:, t:t + 1]).then_inc(sems["vQX"], 1)
                v.wait_ge(sems["vQX"], 12 * t + 4)
                v.tensor_scalar(sb["ycols"][:, t:t + 1], sb["t2col"][:, t:t + 1],
                                sb["wfsc"][:, 4:5], 1.0 / 127, ALU.mult,
                                ALU.mult).then_inc(sems["vQX"], 1)
                v.tensor_scalar(sb["xtile"][:, 0, :], sb["xqf"][:],
                                sb["p0col"][:, sl, 3:4], 127.0, ALU.mult,
                                ALU.mult).then_inc(sems["vQX"], 1)
                v.tensor_copy(sb["xqi"][:], sb["xtile"][:, 0, :]).then_inc(sems["vQX"], 1)
                v.tensor_copy(sb["xqd2"][:], sb["xqi"][:]).then_inc(sems["vQX"], 1)
                v.tensor_tensor(sb["xtile"][:, 0, :], sb["xtile"][:, 0, :],
                                sb["xqd2"][:], ALU.subtract).then_inc(sems["vQX"], 1)
                v.scalar_tensor_tensor(sb["xtile"][:, 1, :], sb["xtile"][:, 0, :], 0.5,
                                       sb["xqd2"][:], ALU.is_ge,
                                       ALU.add).then_inc(sems["vQX"], 1)
                v.scalar_tensor_tensor(sb["xqf"][:], sb["xtile"][:, 0, :], -0.5,
                                       sb["xtile"][:, 1, :], ALU.is_le,
                                       ALU.subtract).then_inc(sems["vQX"], 1)
                if t >= 2:
                    v.wait_ge(sems["stX"], 16 * DX * (t - 1))
                v.tensor_scalar(sb["wfq"][:, sl, :], sb["xqf"][:], 127.0, -127.0,
                                ALU.min, ALU.max).then_inc(sems["vQX"], 1)
        cnt["vT"] += 5 * 8
        cnt["vQX"] += 12 * 8
        em.emit("vector", t2_vec)

        def t2_act(s):
            for t in range(8):
                sl = t % 2
                s.wait_ge(sems["vT"], 5 * (t + 1))
                s.activation(sb["p0col"][:, sl, 3:4], sb["p0col"][:, sl, 1:2], AF.Ln,
                             bias=CS[:, C_EPSL:C_EPSL + 1]).then_inc(sems["sT"], 1)
                s.wait_ge(sems["sT"], 2 * t + 1)
                s.activation(sb["p0col"][:, sl, 2:3], sb["p0col"][:, sl, 3:4], AF.Exp,
                             scale=-0.5).then_inc(sems["sT"], 1)
                s.wait_ge(sems["vQX"], 12 * t + 1)
                s.activation(sb["xqd2"][:], sb["xqf"][:], AF.Abs).then_inc(sems["sT2"], 1)
                s.wait_ge(sems["vQX"], 12 * (t + 1))
                s.dma_start(out=XQD[t], in_=sb["wfq"][:, sl, :]).then_inc(sems["stX"], 16)
                if debug:
                    s.dma_start(out=dbg["dbg_xq2"][t], in_=sb["wfq"][:, sl, :]).then_inc(sems["stX"], 16)
        cnt["sT"] += 2 * 8
        cnt["sT2"] += 8
        cnt["stX"] += 16 * 8 * DX
        em.emit("scalar", t2_act)

        def t3_pe(t):
            t.wait_ge(sems["sX2"], 128)
            for idx2 in range(128):
                th, nq, k = idx2 // 64, (idx2 // 16) % 4, idx2 % 16
                fg = idx2 // 16
                if k == 0:
                    t.wait_ge(sems["ldff"], 16 * 4 * (fg + 1))
                    t.wait_ge(sems["ldq"], 16 * 4 * (th + 1))
                    if fg >= 1:
                        t.wait_ge(sems["vFF"], 4 * fg)
                for ti in range(4):
                    mm = t.matmul(psW[ti][:, 0:512],
                                  sb["xqt_t"][:, ti, k, :],
                                  sb["wqt4"][:, k, :, :].rearrange("p a f -> p (a f)"),
                                  start=(k == 0), stop=(k == KC - 1))
                mm.then_inc(sems["tFF"], 1)
        cnt["tFF"] += 128
        em.emit("tensor", t3_pe)

        def t3_vec(v):
            for fg in range(8):
                th, nq = fg // 4, fg % 4
                v.wait_ge(sems["tFF"], 16 * (fg + 1))
                v.wait_ge(sems["vQX"], 12 * 4 * (th + 1))
                for ti in range(4):
                    tile = 4 * th + ti
                    j = 4 * fg + ti
                    if j >= 2:
                        v.wait_ge(sems["stO"], 16 * (j - 1))
                    v.scalar_tensor_tensor(
                        sb["outsb"][:, j % 2, :], psW[ti][:, 0:512],
                        sb["ycols"][:, tile:tile + 1],
                        x2f[:, 2048 * tile + 512 * nq:2048 * tile + 512 * (nq + 1)],
                        ALU.mult, ALU.add).then_inc(sems["vFF"], 1)
        cnt["vFF"] += 32
        em.emit("vector", t3_vec)

        def t3_st(s):
            for j in range(32):
                fg, ti = j // 4, j % 4
                th, nq = fg // 4, fg % 4
                tile = 4 * th + ti
                s.wait_ge(sems["vFF"], j + 1)
                s.dma_start(out=OUT[128 * tile:128 * (tile + 1), 512 * nq:512 * (nq + 1)],
                            in_=sb["outsb"][:, j % 2, :]).then_inc(sems["stO"], 16)
            if debug:
                s.dma_start(out=dbg["dbg_x2"], in_=x2f).then_inc(sems["stDG"], 16)
                s.dma_start(out=dbg["dbg_tc"][:, 0:8], in_=sb["t2col"][:]).then_inc(sems["stDG"], 16)
                s.dma_start(out=dbg["dbg_tc"][:, 8:16], in_=sb["ycols"][:]).then_inc(sems["stDG"], 16)
        cnt["stO"] += 16 * 32
        if debug:
            cnt["stDG"] += 16 * 3
        em.emit("scalar", t3_st)

    finals = {k: v for k, v in cnt.items()}

    def final_sync(sync):
        for k, v in finals.items():
            if v > 0:
                sync.wait_ge(sems[k], v)
    em.emit("sync", final_sync)

    em.build()
    ctx.close()
    return nc



def build_consts():
    c = np.zeros((128, 640), np.float32)
    c[:, C_ONE] = 1.0
    c[:, C_I128] = 1.0 / 128
    c[:, C_EPSI] = EPS_IN
    c[:, C_EPSL] = EPS_LN
    c[:, C_HALF] = 0.5
    c[:, C_NEG1] = -1.0
    c[:, C_127] = 127.0
    t = np.arange(128) % 32
    c[:, C_MSKE] = (t < 16).astype(np.float32)
    c[:, C_MSKO] = (t >= 16).astype(np.float32)
    for ci in range(32):
        even = (t < 16) & (ci < 16) & (t <= ci)
        odd = (t >= 16) & (ci >= 16) & ((t - 16) <= (ci - 16))
        c[:, C_AME + ci] = even.astype(np.float32)
        c[:, C_AMO + ci] = odd.astype(np.float32)
    c[:, 208:336] = 1.0
    m = np.arange(128)
    for j in range(4):
        c[j, 80:208] = (m // 32 == j).astype(np.float32)
    return c


def host_prep(inputs):
    x = np.asarray(inputs["L0_attn_x"])
    wq = np.asarray(inputs["L0_attn_Wq"])
    wk = np.asarray(inputs["L0_attn_Wk"])
    wv = np.asarray(inputs["L0_attn_Wv"])
    wo = np.asarray(inputs["L0_attn_Wo"])
    wbeta = np.asarray(inputs["L0_attn_Wbeta"])
    lnw = np.asarray(inputs["L0_attn_ln_w"])
    lnb = np.asarray(inputs["L0_attn_ln_b"])
    wg1 = np.asarray(inputs["L0_attn_Wg1"])
    wg2 = np.asarray(inputs["L0_attn_Wg2"])
    nrm = np.asarray(inputs["L0_attn_norm_w"])
    wf = np.asarray(inputs["L0_ffn_weight"])
    consts = build_consts()

    in_maps = []
    for c in range(8):
        b = c // 4
        hg = c % 4
        hsl = slice(4 * hg * Dh, 4 * (hg + 1) * Dh)
        heads = [4 * hg + j for j in range(HPC)]
        wkvb = np.concatenate(
            [np.concatenate([wk[:, h * Dh:(h + 1) * Dh],
                             wbeta[:, h:h + 1],
                             wv[:, h * Dh:(h + 1) * Dh]], axis=1)
             for h in heads], axis=1)
        m = {
            "x": np.ascontiguousarray(x[b]),
            "xq": np.ascontiguousarray(x[b, QTOK * hg:QTOK * (hg + 1)]),
            "wq": np.ascontiguousarray(wq[:, hsl]),
            "wkvb": np.ascontiguousarray(wkvb),
            "wg1": np.ascontiguousarray(wg1),
            "wg2": np.ascontiguousarray(wg2[:, hsl]),
            "wos": np.ascontiguousarray(wo[hsl, :]),
            "wf": np.ascontiguousarray(wf),
            "lnw": np.ascontiguousarray(lnw[heads]),
            "lnb": np.ascontiguousarray(lnb[heads]),
            "nrmw": np.ascontiguousarray(nrm[hsl]).reshape(HPC, Dh),
            "consts": consts,
        }
        in_maps.append(m)
    return in_maps


def kernel(**inputs):
    in_maps = host_prep(inputs)
    nc = build_program(debug=DEBUG)
    res = run_bass_kernel_spmd(nc, in_maps, core_ids=list(range(8)))
    out = np.zeros((B, T, D), np.float32)
    for ci in range(8):
        b = ci // 4
        hg = ci % 4
        out[b, QTOK * hg:QTOK * (hg + 1)] = res.results[ci]["out"]
    return out


# version 08
# speedup vs baseline: 1.1666x; 1.0893x over previous
import numpy as np
import ml_dtypes
import concourse.bass as bass
import concourse.mybir as mybir
from concourse.bass_utils import run_bass_kernel_spmd
from contextlib import ExitStack

F32 = mybir.dt.float32
BF16 = mybir.dt.bfloat16
I32 = mybir.dt.int32
AF = mybir.ActivationFunctionType
ALU = mybir.AluOpType

B, T, D, H = 2, 4096, 2048, 16
HPC = 4
Dh = 128
SC = 32
NSC = T // SC
CH = 16
NT = T // 128
KC = D // 128
QTOK = T // 4
EPS_LN = 1e-5
EPS_IN = 1e-6
DEBUG = False
import os
PARTS = set(os.environ.get("KB_PARTS", "all").split(","))
def has(p):
    return "all" in PARTS or p in PARTS

C_ONE, C_I128, C_EPSI, C_EPSL, C_HALF, C_ZERO, C_NEG1, C_127 = range(8)
C_MSKE, C_MSKO = 8, 9
C_AME, C_AMO = 16, 48


class Emitter:

    def __init__(self, nc, ctx):
        self.nc = nc
        self.ops = {k: [] for k in ("sync", "vector", "scalar", "tensor", "gpsimd")}
        self.sems = {}
        self.cnt = {}
        self.ctx = ctx

    def sem(self, name):
        if name not in self.sems:
            self.sems[name] = self.ctx.enter_context(self.nc.semaphore(name))
            self.cnt[name] = 0
        return self.sems[name]

    def emit(self, eng, fn):
        self.ops[eng].append(fn)

    def inc(self, name, n=1):
        self.sem(name)
        self.cnt[name] += n
        return self.cnt[name]

    def build(self):
        blk_ctx = self.nc.Block()
        block = blk_ctx.__enter__()

        def run(eng_name):
            def body(eng):
                for fn in self.ops[eng_name]:
                    fn(eng)
            return body

        block.sync(run("sync"))
        block.scalar(run("scalar"))
        block.vector(run("vector"))
        block.tensor(run("tensor"))
        block.gpsimd(run("gpsimd"))
        blk_ctx.__exit__(None, None, None)


def build_program(debug=False):
    nc = bass.Bass("TRN2", target_bir_lowering=False, detect_race_conditions=False)
    ctx = ExitStack()
    em = Emitter(nc, ctx)
    sems, cnt = em.sems, em.cnt

    dram = {}

    def din(name, shape, dtype=F32):
        dram[name] = nc.dram_tensor(name, list(shape), dtype, kind="ExternalInput").ap()
        return dram[name]

    def dout(name, shape, dtype=F32):
        dram[name] = nc.dram_tensor(name, list(shape), dtype, kind="ExternalOutput").ap()
        return dram[name]

    def dint(name, shape, dtype=BF16):
        dram[name] = nc.dram_tensor(name, list(shape), dtype, kind="Internal").ap()
        return dram[name]

    X = din("x", (T, D))
    XQ = din("xq", (QTOK, D))
    WQ = din("wq", (D, 512))
    WKVB = din("wkvb", (D, 4 * 257))
    WG1 = din("wg1", (D, 128))
    WG2 = din("wg2", (128, 512))
    WOS = din("wos", (512, D))
    WF = din("wf", (D, D))
    LNW = din("lnw", (HPC, Dh))
    LNB = din("lnb", (HPC, Dh))
    NRMW = din("nrmw", (HPC, Dh))
    CONSTS = din("consts", (128, 640))
    OUT = dout("out", (QTOK, D), F32)

    XN = dint("xnb", (NT, 128, D), BF16)
    QTD = dint("qtd", (4, 128, T), BF16)
    KT = dint("kt", (4, T, 128), BF16)
    ETJ = dint("etj", (4, T), BF16)
    VT = dint("vt4", (4, T, 128), BF16)
    ST = dint("st4", (4, T, 128), BF16)
    WQR = dint("wqraw", (KC, 128, D), BF16)
    OGT = dint("ogt", (4, 512, QTOK), BF16)
    RSI = dint("rsi", (4, QTOK, D), BF16)
    RSO = dint("rso", (QTOK, D), BF16)
    XQD = dint("xqd", (8, 128, D), BF16)

    dbg = {}
    if debug:
        for nm, src in (("dbg_xnt", XN), ("dbg_qtd", QTD), ("dbg_wqt", WQR),
                        ("dbg_ogt", OGT), ("dbg_xq2", XQD)):
            dbg[nm] = dout(nm, src.shape, BF16)
        dbg["dbg_x2"] = dout("dbg_x2", (128, 16384), BF16)
        dbg["dbg_tc"] = dout("dbg_tc", (128, 16), F32)

    sb = {}

    def S(name, shape, dtype):
        sb[name] = ctx.enter_context(nc.sbuf_tensor(name, list(shape), dtype))
        return sb[name]

    S("wq_b", (128, KC, 512), BF16)
    S("wkvb_b", (128, KC, 1028), BF16)
    S("wg1_b", (128, KC, 128), BF16)
    S("wg2_b", (128, 512), BF16)
    S("lnw_s", (HPC, Dh), F32)
    S("lnb_s", (HPC, Dh), F32)
    S("nrmw_s", (HPC, Dh), F32)
    CS = S("consts_s", (128, 640), F32)

    S("xtile", (128, 2, D), F32)
    S("bnst", (128, 24), F32)
    S("p0col", (128, 2, 4), F32)
    S("xn_tok", (128, 2, D), BF16)

    def p0_gload(g):
        g.dma_start(out=sb["wq_b"][:], in_=WQ.rearrange("(k p) c -> p k c", p=128)).then_inc(sems["ldw"], 16)
        g.dma_start(out=sb["wkvb_b"][:], in_=WKVB.rearrange("(k p) c -> p k c", p=128)).then_inc(sems["ldw"], 16)
        g.dma_start(out=sb["wg1_b"][:], in_=WG1.rearrange("(k p) c -> p k c", p=128)).then_inc(sems["ldw"], 16)
        g.dma_start(out=sb["wg2_b"][:], in_=WG2).then_inc(sems["ldw"], 16)
    em.sem("ldw")
    if has("wload"):
        cnt["ldw"] += 64
        em.emit("gpsimd", p0_gload)

    def p0_small(sync):
        sync.dma_start(out=sb["lnw_s"][:], in_=LNW).then_inc(sems["ld"], 16)
        sync.dma_start(out=sb["lnb_s"][:], in_=LNB).then_inc(sems["ld"], 16)
        sync.dma_start(out=sb["nrmw_s"][:], in_=NRMW).then_inc(sems["ld"], 16)
        sync.dma_start(out=sb["consts_s"][:], in_=CONSTS).then_inc(sems["ld"], 16)
    em.sem("ld"); cnt["ld"] += 64
    em.emit("sync", p0_small)

    em.sem("ldx"); em.sem("vA"); em.sem("sA"); em.sem("xs"); em.sem("stq")

    def p0_x_loads(sync):
        for i in range(int(os.environ.get("KB_NTX", NT))):
            if i >= 2 and has("xvec"):
                sync.wait_ge(sems["vA"], 2 * (i - 1))
            sync.dma_start(out=sb["xtile"][:, i % 2, :],
                           in_=X.rearrange("(n p) d -> n p d", p=128)[i]).then_inc(sems["ldx"], 16)
    if has("xln"):
        cnt["ldx"] += 16 * NT
        em.emit("sync", p0_x_loads)

    def p0_x_vec(v):
        v.wait_ge(sems["ld"], 64)
        for i in range(NT):
            sl = i % 2
            v.wait_ge(sems["ldx"], 16 * (i + 1))
            for j in range(4):
                v.bn_stats(sb["bnst"][:, 6 * j:6 * (j + 1)],
                           sb["xtile"][:, sl, 512 * j:512 * (j + 1)]).then_inc(sems["vB"], 1)
            v.wait_ge(sems["vB"], 4 * (i + 1))
            v.bn_aggr(sb["p0col"][:, sl, 0:2], sb["bnst"][:]).then_inc(sems["vA"], 1)
            if has("xact"):
                v.wait_ge(sems["sA"], i + 1)
                if i >= 2:
                    v.wait_ge(sems["stq"], 16 * (i - 1) * (2 if debug else 1))
            v.tensor_scalar(sb["xn_tok"][:, sl, :], sb["xtile"][:, sl, :],
                            sb["p0col"][:, sl, 0:1], sb["p0col"][:, sl, 2:3],
                            ALU.subtract, ALU.mult).then_inc(sems["vA"], 1)
    if has("xln") and has("xvec"):
        cnt["vA"] += 2 * NT
        em.emit("vector", p0_x_vec)

    def p0_x_act(s):
        s.wait_ge(sems["ld"], 64)
        for i in range(NT):
            sl = i % 2
            s.wait_ge(sems["vA"], 2 * i + 1)
            s.activation(sb["p0col"][:, sl, 3:4], sb["p0col"][:, sl, 1:2], AF.Ln,
                         bias=CS[:, C_EPSL:C_EPSL + 1]).then_inc(sems["sB"], 1)
            s.wait_ge(sems["sB"], i + 1)
            s.activation(sb["p0col"][:, sl, 2:3], sb["p0col"][:, sl, 3:4], AF.Exp,
                         scale=-0.5).then_inc(sems["sA"], 1)
            s.wait_ge(sems["vA"], 2 * i + 2)
            s.dma_start(out=XN[i], in_=sb["xn_tok"][:, sl, :]).then_inc(sems["stq"], 16)
            if debug:
                s.dma_start(out=dbg["dbg_xnt"][i], in_=sb["xn_tok"][:, sl, :]).then_inc(sems["stq"], 16)

    em.sem("sB"); em.sem("vB")
    if has("xln") and has("xvec"):
        cnt["vB"] += 4 * NT
    if has("xln") and has("xact"):
        cnt["sB"] += NT
        cnt["sA"] += NT
        cnt["stq"] += 16 * NT * (2 if debug else 1)
        em.emit("scalar", p0_x_act)




    SCM = 16
    NM = 8
    S("xnt_m", (128, 2, KC, 512), BF16)
    S("sigq", (128, 512), F32)
    S("qsil", (128, 512), BF16)
    S("g1sb", (128, 2, 512), BF16)
    S("ktok", (128, 2, 512), BF16)
    S("vtok", (128, 2, 512), BF16)
    S("sgtok", (128, 2, 512), BF16)
    S("etst", (128, 16), BF16)
    S("etT", (16, 128), BF16)
    psQ = [ctx.enter_context(nc.psum_tensor(f"psQ{i}", [128, 512], F32)) for i in range(2)]
    psKV = [ctx.enter_context(nc.psum_tensor(f"psKV{i}", [128, 512], F32)) for i in range(2)]
    psG = ctx.enter_context(nc.psum_tensor("psG", [128, 512], F32))
    psGT = [ctx.enter_context(nc.psum_tensor(f"psGT{i}", [128, 512], F32)) for i in range(2)]

    for s_ in ("ldB", "tQ", "tKV", "tG", "tGT", "tB", "sQ1", "vQ", "sKV1",
               "vKV", "sG1", "sGT1", "stB", "stG", "stq2", "stE"):
        em.sem(s_)

    if has("B"):
        def b_loads(sync):
            for m in range(NM):
                for j in range(4):
                    i = 4 * m + j
                    if m >= 2 and has("Bpe") and has("Bsc"):
                        sync.wait_ge(sems["tKV"], 4 * (m - 1))
                    sync.wait_ge(sems["stq"], 16 * (i + 1) * (2 if debug else 1))
                    sync.dma_start(out=sb["xnt_m"][:, m % 2, :, 128 * j:128 * (j + 1)],
                                   in_=XN[i].rearrange("p (k f) -> p k f", k=KC),
                                   transpose=True).then_inc(sems["ldB"], 16)
        cnt["ldB"] += 16 * 4 * NM
        em.emit("sync", b_loads)
    if has("B") and has("Bpe"):

        def b_pe(t):
            t.wait_ge(sems["ldw"], 64)
            t.wait_ge(sems["gInit"], 1)

            def sg_mm(j):
                m2 = j // 4
                t.wait_ge(sems["sG1"], m2 + 1)
                t.wait_ge(sems["sKV1"], j + 1)
                t.matmul(psGT[j % 2][:, 0:512],
                         sb["g1sb"][:, m2 % 2, 128 * (j % 4):128 * (j % 4) + 128],
                         sb["wg2_b"][:], start=True, stop=True).then_inc(sems["tGT"], 1)
            for m in range(NM):
                ms = m % 2
                t.wait_ge(sems["ldB"], 16 * 4 * (m + 1))
                xm = sb["xnt_m"][:, ms, :, :]
                for mch in range(4):
                    if 4 * m + mch >= 2 and has("Bsc"):
                        t.wait_ge(sems["vQ"], 4 * m + mch - 1)
                    mm = None
                    for k in range(KC):
                        mm = t.matmul(psQ[mch % 2][:],
                                      sb["wq_b"][:, k, 128 * mch:128 * (mch + 1)],
                                      xm[:, k, :], start=(k == 0), stop=(k == KC - 1))
                    mm.then_inc(sems["tQ"], 1)
                if m >= 2 and has("Bsc"):
                    t.wait_ge(sems["sG1"], m - 1)
                if m >= 1 and has("Bsc"):
                    t.wait_ge(sems["vKV"], 2 * m - 1)
                mm = None
                for k in range(KC):
                    mm = t.matmul(psG[:], sb["wg1_b"][:, k, :], xm[:, k, :],
                                  start=(k == 0), stop=(k == KC - 1))
                mm.then_inc(sems["tG"], 1)
                if has("Bsc"):
                    t.wait_ge(sems["sG1"], m + 1)
                mm = None
                for mc in range(4):
                    for k in range(KC):
                        mm = t.matmul(psG[:, 4 * mc:4 * mc + 4],
                                      xm[:, k, 128 * mc:128 * (mc + 1)],
                                      sb["wkvb_b"][:, k, 1024:1028],
                                      start=(k == 0), stop=(k == KC - 1))
                mm.then_inc(sems["tG"], 1)
                t.wait_ge(sems["vKV"], 2 * m + 1)
                t.transpose(psB[0:16, :], sb["etst"][:], sb["ident_t"][:]).then_inc(sems["tB"], 1)
                for mc in range(4):
                    idx = 4 * m + mc
                    if idx >= 2:
                        sg_mm(idx - 2)
                    if idx >= 2 and has("Bsc"):
                        t.wait_ge(sems["sKV1"], idx - 1)
                        t.wait_ge(sems["sGT1"], idx - 1)
                    mm = None
                    for k in range(KC):
                        mm = t.matmul(psKV[mc % 2][:, 0:512],
                                      xm[:, k, 128 * mc:128 * (mc + 1)],
                                      sb["wkvb_b"][:, k, 0:512],
                                      start=(k == 0), stop=(k == KC - 1))
                    for k in range(KC):
                        mm = t.matmul(psGT[mc % 2][:, 0:512],
                                      xm[:, k, 128 * mc:128 * (mc + 1)],
                                      sb["wkvb_b"][:, k, 512:1024],
                                      start=(k == 0), stop=(k == KC - 1))
                    mm.then_inc(sems["tKV"], 1)
            sg_mm(4 * NM - 2)
            sg_mm(4 * NM - 1)
        if has("Bpe"):
            cnt["tQ"] += 4 * NM
            cnt["tKV"] += 4 * NM
            cnt["tG"] += 2 * NM
            cnt["tGT"] += 4 * NM
            em.emit("tensor", b_pe)

        def b_scalar(s):
            for m in range(NM):
                ms = m % 2
                for mch in range(4):
                    s.wait_ge(sems["tQ"], 4 * m + mch + 1)
                    s.activation(sb["sigq"][:], psQ[mch % 2][:], AF.Sigmoid).then_inc(sems["sQ1"], 1)
                    s.wait_ge(sems["vQ"], 4 * m + mch + 1)
                    s.dma_start(out=QTD[mch][:, 512 * m:512 * (m + 1)],
                                in_=sb["qsil"][:]).then_inc(sems["stq2"], 16)
                    if debug:
                        s.dma_start(out=dbg["dbg_qtd"][mch][:, 512 * m:512 * (m + 1)],
                                    in_=sb["qsil"][:]).then_inc(sems["stq2"], 16)
                    s.wait_ge(sems["stq2"], (32 if debug else 16) * (4 * m + mch + 1))
                s.wait_ge(sems["tG"], 2 * m + 1)
                s.activation(sb["g1sb"][:, ms, :], psG[:], AF.Copy).then_inc(sems["sG1"], 1)
                s.wait_ge(sems["tG"], 2 * m + 2)
                s.activation(sb["etst"][:], psG[:, 0:16], AF.Copy).then_inc(sems["vKV"], 1)
                s.wait_ge(sems["tB"], m + 1)
                if m >= 1:
                    s.wait_ge(sems["stE"], 16 * m)
                s.activation(sb["etT"][:], psB[0:16, :], AF.Sigmoid).then_inc(sems["vKV"], 1)
                s.dma_start(out=ETJ[:, 512 * m:512 * (m + 1)].rearrange("j (mc t) -> mc j t", t=128),
                            in_=sb["etT"][:]).then_inc(sems["stE"], 16)

                def sg_ev(j):
                    s.wait_ge(sems["tGT"], j + 1)
                    if j >= 2:
                        s.wait_ge(sems["stG"], 16 * (j - 1))
                    s.activation(sb["sgtok"][:, j % 2, :], psGT[j % 2][:, 0:512],
                                 AF.Sigmoid).then_inc(sems["sGT1"], 1)
                    s.dma_start(out=ST[:, 128 * j:128 * (j + 1), :].rearrange("j t f -> t j f"),
                                in_=sb["sgtok"][:, j % 2, :].rearrange("t (j f) -> t j f", f=128)
                                ).then_inc(sems["stG"], 16)
                for mc in range(4):
                    idx = 4 * m + mc
                    if idx >= 2:
                        sg_ev(idx - 2)
                    s.wait_ge(sems["tKV"], idx + 1)
                    if idx >= 2:
                        s.wait_ge(sems["stB"], 32 * (idx - 1))
                    s.activation(sb["ktok"][:, mc % 2, :], psKV[mc % 2][:, 0:512],
                                 AF.Copy)
                    s.activation(sb["vtok"][:, mc % 2, :], psGT[mc % 2][:, 0:512],
                                 AF.Copy).then_inc(sems["sKV1"], 1)
                    s.dma_start(out=KT[:, 128 * idx:128 * (idx + 1), :].rearrange("j t f -> t j f"),
                                in_=sb["ktok"][:, mc % 2, :].rearrange("t (j f) -> t j f", f=128)
                                ).then_inc(sems["stB"], 16)
                    s.dma_start(out=VT[:, 128 * idx:128 * (idx + 1), :].rearrange("j t f -> t j f"),
                                in_=sb["vtok"][:, mc % 2, :].rearrange("t (j f) -> t j f", f=128)
                                ).then_inc(sems["stB"], 16)
            sg_ev(4 * NM - 2)
            sg_ev(4 * NM - 1)
        if has("Bpe") and has("Bsc"):
            cnt["sQ1"] += 4 * NM
            cnt["stq2"] += (32 if debug else 16) * 4 * NM
            cnt["sG1"] += NM
            cnt["vKV"] += 2 * NM
            cnt["tB"] += NM
            cnt["stE"] += 16 * NM
            cnt["sKV1"] += 4 * NM
            cnt["sGT1"] += 4 * NM
            cnt["stB"] += 32 * 4 * NM
            cnt["stG"] += 16 * 4 * NM
            em.emit("scalar", b_scalar)

        def b_vec(v):
            for m in range(NM):
                for mch in range(4):
                    v.wait_ge(sems["sQ1"], 4 * m + mch + 1)
                    v.tensor_tensor(sb["qsil"][:], psQ[mch % 2][:], sb["sigq"][:],
                                    ALU.mult).then_inc(sems["vQ"], 1)
        if has("Bpe") and has("Bsc"):
            cnt["vQ"] += 4 * NM
            em.emit("vector", b_vec)

    NPREP = 3
    S("krsb", (128, NPREP, 132), BF16)
    S("vsb", (128, NPREP, 128), BF16)
    S("ssb", (128, NPREP, 128), BF16)
    S("qisb", (128, NPREP, 4, 32), BF16)
    S("kisb", (128, NPREP, 128), BF16)
    S("kesb", (128, NPREP, 128), BF16)
    S("kitsb", (128, NPREP, 128), BF16)
    S("qchsb", (128, NPREP, 128), BF16)
    S("ccsb", (128, NPREP, 128), F32)
    S("cc128", (128, NPREP, 128), F32)
    S("vmk", (128, 128), BF16)
    S("ccol", (128, NPREP, 8), F32)
    S("S_sb", (128, 4, 128), BF16)
    S("khsb", (128, 128), BF16)
    S("t1sb", (128, 128), F32)
    S("redb", (128, 4, 128), F32)
    S("scr", (128, 128), F32)
    S("ats", (128, 32), F32)
    S("usb", (128, 128), BF16)
    S("ubd", (128, 512), BF16)
    S("atm", (128, 32), BF16)
    S("kcol", (128, 16), F32)
    S("ohsb", (128, 2, 128), BF16)
    S("mrg", (128, 128), BF16)
    S("ogsb", (128, 128), BF16)
    S("ogts", (128, 128), BF16)
    S("ocol", (128, 4), F32)
    S("w_bt", (128, 128), F32)
    S("w2n_bt", (128, 128), F32)
    S("b_bt", (128, 128), F32)
    S("wbb_bt", (128, 128), F32)
    S("nw_bt", (128, 128), F32)
    S("mskE_t", (128, 128), BF16)
    S("ident_t", (128, 128), BF16)
    S("ones_t", (128, 128), F32)

    psS = psQ[0]
    psCH = psGT[0]
    psA = psGT[1]
    psQHb = psKV
    psB = ctx.enter_context(nc.psum_tensor("psB", [128, 128], BF16))

    for s_ in ("vInit", "gInit", "ldp", "vPR", "sPR", "tPR", "tKH", "tA",
               "tCU", "tSU", "sE", "vR", "sR", "sCB", "vU", "vBD", "sSC",
               "sAT", "vAT", "sOH", "vOG", "vOG2", "tOG", "sEV", "stOG", "stDG", "vCX",
               "ccx", "ldo", "tWO", "sWO", "vWO", "stO", "ldw2", "tF", "ldf"):
        em.sem(s_)

    CP = int(os.environ.get("KB_NCHUNK", 2 * NSC))
    vcx_n = [0]
    NSCC = (CP + 1) // 2

    if has("C"):
        if debug:
            DBG_S = dout("dbg_S", (128, 512), BF16)
            DBG_KH = dout("dbg_kh", (8, 128, 128), BF16)
            DBG_U = dout("dbg_u", (8, 128, 128), BF16)
            DBG_KC = dout("dbg_kc", (8, 128, 16), F32)
            DBG_CC = dout("dbg_cc", (8, 128, 128), F32)
            DBG_CL = dout("dbg_cl", (8, 128, 8), F32)
            DBG_SS = dout("dbg_ss", (8, 128, 512), BF16)

        def c_init_g(g):
            g.memset(sb["ident_t"][:], 0.0)
            g.affine_select(out=sb["ident_t"][:], in_=sb["ident_t"][:],
                            compare_op=ALU.not_equal, fill=1.0, base=0,
                            pattern=[[-1, 128]], channel_multiplier=1)
            g.tensor_copy(sb["scr"][:, 0:1], sb["ident_t"][:, 0:1]).then_inc(sems["gInit"], 1)
        cnt["gInit"] += 1
        em.emit("gpsimd", c_init_g)

        def c_init_vec(v):
            v.wait_ge(sems["sPR"], 3)
            v.memset(sb["S_sb"][:], 0.0)
            v.memset(sb["ubd"][:], 0.0)
            v.memset(sb["ones_t"][:], 1.0)
            v.scalar_tensor_tensor(sb["w2n_bt"][:], sb["w_bt"][:], -1.0, sb["w_bt"][:],
                                   ALU.mult, ALU.mult).then_inc(sems["vInit"], 1)
            v.wait_ge(sems["vInit"], 1)
            v.tensor_tensor(sb["wbb_bt"][:], sb["w_bt"][:], sb["b_bt"][:], ALU.mult)
            v.tensor_scalar(sb["mskE_t"][:], sb["ones_t"][:], CS[:, C_MSKE:C_MSKE + 1],
                            None, ALU.mult).then_inc(sems["vInit"], 1)
        cnt["vInit"] += 2
        em.emit("vector", c_init_vec)

        def c_loads(sync):
            for g in range(NSCC):
                sl = g % NPREP
                if g >= NPREP:
                    sync.wait_ge(sems["tSU"], 2 * (g - NPREP) + 2)
                    sync.wait_ge(sems["vOG2"], g - NPREP + 1)
                sync.wait_ge(sems["stB"], 32 * (g // 4 + 1))
                sync.wait_ge(sems["stG"], 16 * (g // 4 + 1))
                sync.wait_ge(sems["stE"], 16 * (g // 16 + 1))
                sync.dma_start(out=sb["krsb"][:, sl, 0:128],
                               in_=KT[:, 32 * g:32 * (g + 1), :]).then_inc(sems["ldp"], 16)
                sync.dma_start(out=sb["krsb"][:, sl, 128:129],
                               in_=ETJ[:, 32 * g:32 * (g + 1)]).then_inc(sems["ldp"], 16)
                sync.dma_start(out=sb["vsb"][:, sl, :],
                               in_=VT[:, 32 * g:32 * (g + 1), :]).then_inc(sems["ldp"], 16)
                sync.dma_start(out=sb["ssb"][:, sl, :],
                               in_=ST[:, 32 * g:32 * (g + 1), :]).then_inc(sems["ldp"], 16)
                sync.wait_ge(sems["stq2"], (32 if debug else 16) * 4 * min(NM, g // 16 + 1))
                sync.dma_start(out=sb["qisb"][:, sl, :, :],
                               in_=QTD[:, :, 32 * g:32 * (g + 1)].rearrange("c p t -> p c t")
                               ).then_inc(sems["ldp"], 16)
        cnt["ldp"] += 80 * NSCC
        em.emit("sync", c_loads)

        psb_n = {"pe": 0}

        def psb_wait(t):
            k = psb_n["pe"]
            if k > 0:
                t.wait_ge(sems["sEV"], k)
            psb_n["pe"] = k + 1

        def c_pe(t):
            t.wait_ge(sems["vQ"], cnt["vQ"])
            t.wait_ge(sems["sKV1"], cnt["sKV1"])
            t.wait_ge(sems["sGT1"], cnt["sGT1"])
            t.wait_ge(sems["sG1"], cnt["sG1"])
            t.wait_ge(sems["ld"], 64)
            t.wait_ge(sems["gInit"], 1)
            for i in range(3):
                rhs = (sb["lnw_s"], sb["lnb_s"], sb["nrmw_s"])[i]
                t.matmul(psA[:, 0:128], CS[0:4, 80:208], rhs[:],
                         start=True, stop=True).then_inc(sems["tPR"], 1)
                t.wait_ge(sems["sPR"], i + 1)
            for g in range(NSCC + 1):
                if g < NSCC:
                    sl = g % NPREP
                    t.wait_ge(sems["vPR"], 3 * g + 2)
                    psb_wait(t)
                    t.transpose(psB[:], sb["kisb"][:, sl, :], sb["ident_t"][:]).then_inc(sems["tPR"], 1)
                    psb_wait(t)
                    t.transpose(psB[:], sb["qisb"][:, sl, :, :].rearrange("p c t -> p (c t)"),
                                sb["ident_t"][:]).then_inc(sems["tPR"], 1)
                if g >= 1:
                    gg = g - 1
                    sl = gg % NPREP
                    for par in (0, 1):
                        ch = 2 * gg + par
                        if ch >= CP:
                            break
                        t.wait_ge(sems["sPR"], 3 + 4 * gg + 4)
                        t.wait_ge(sems["sSC"], ch)
                        if ch == 0:
                            t.wait_ge(sems["vInit"], 2)
                        t.wait_ge(sems["sE"], ch)
                        if gg >= 1:
                            t.wait_ge(sems["sOH"], 2 * (gg - 1) + par + 1)
                        mm = None
                        for p in range(4):
                            mm = t.matmul(psCH[32 * p:32 * p + 32, 0:128],
                                          sb["kitsb"][:, sl, 32 * p:32 * p + 32],
                                          sb["S_sb"][:, p, :], start=True, stop=True,
                                          tile_position=(0, 32 * p))
                        mm.then_inc(sems["tKH"], 1)
                        mm = None
                        for p in range(4):
                            mm = t.matmul(psA[32 * p:32 * p + 32, 0:32],
                                          sb["kitsb"][:, sl, 32 * p:32 * p + 32],
                                          sb["qisb"][:, sl, p, :], start=True, stop=True,
                                          tile_position=(0, 32 * p))
                        mm.then_inc(sems["tA"], 1)
                        t.wait_ge(sems["vAT"], ch + 1)
                        t.wait_ge(sems["vU"], ch + 1)
                        mm = None
                        for p in range(4):
                            t.matmul(psQHb[par][32 * p:32 * p + 32, 0:128],
                                     sb["atm"][32 * p:32 * p + 32, :],
                                     sb["usb"][32 * p:32 * p + 32, :],
                                     start=True, stop=False,
                                     tile_position=(32 * p, 32 * p))
                            mm = t.matmul(psQHb[par][32 * p:32 * p + 32, 0:128],
                                          sb["qisb"][:, sl, p, :],
                                          sb["S_sb"][:, p, :], start=False, stop=True,
                                          tile_position=(0, 32 * p))
                        mm.then_inc(sems["tCU"], 1)
                        t.wait_ge(sems["vBD"], ch + 1)
                        t.matmul(psS[:], sb["kesb"][:, sl, :], sb["ubd"][:],
                                 start=(ch == 0), stop=True,
                                 skip_group_check=(ch > 0)).then_inc(sems["tSU"], 1)
                    t.wait_ge(sems["vOG"], gg + 1)
                    psb_wait(t)
                    t.transpose(psB[:], sb["ogsb"][:], sb["ident_t"][:]).then_inc(sems["tOG"], 1)
        cnt["tPR"] += 3 + 2 * NSCC
        cnt["tKH"] += CP
        cnt["tA"] += CP
        cnt["tCU"] += CP
        cnt["tSU"] += CP
        cnt["tOG"] += NSCC
        em.emit("tensor", c_pe)

        scb_n = [0]
        sr_n = [0]
        sr_at_scan = {}
        sr_at_o = {}

        def c_act(s):
            for i, dstn in enumerate(("w_bt", "b_bt", "nw_bt")):
                s.wait_ge(sems["tPR"], i + 1)
                s.activation(sb[dstn][:], psA[:, 0:128], AF.Copy).then_inc(sems["sPR"], 1)
            for g in range(NSCC + 1):
                if g < NSCC:
                    sl = g % NPREP
                    s.wait_ge(sems["vPR"], 3 * g + 1)
                    s.activation(sb["ccol"][:, sl, 3:4], sb["ccol"][:, sl, 2:3], AF.Ln,
                                 bias=CS[:, C_EPSI:C_EPSI + 1]).then_inc(sems["sPR"], 1)
                    s.wait_ge(sems["sPR"], 3 + 4 * g + 1)
                    s.activation(sb["ccol"][:, sl, 0:1], sb["ccol"][:, sl, 3:4], AF.Exp,
                                 scale=-0.5).then_inc(sems["sPR"], 1)
                    s.wait_ge(sems["tPR"], 3 + 2 * g + 1)
                    s.activation(sb["kitsb"][:, sl, :], psB[:], AF.Copy).then_inc(sems["sPR"], 1)
                    s.activation(sb["scr"][:, 0:1], sb["kitsb"][:, sl, 0:1], AF.Copy).then_inc(sems["sEV"], 1)
                    s.wait_ge(sems["tPR"], 3 + 2 * g + 2)
                    s.activation(sb["qchsb"][:, sl, :], psB[:], AF.Copy).then_inc(sems["sPR"], 1)
                    s.activation(sb["scr"][:, 0:1], sb["qchsb"][:, sl, 0:1], AF.Copy).then_inc(sems["sEV"], 1)
                if g >= 1:
                    gg = g - 1
                    sl = gg % NPREP
                    for par in (0, 1):
                        ch = 2 * gg + par
                        if ch >= CP:
                            break
                        s.wait_ge(sems["tKH"], ch + 1)
                        if ch >= 1:
                            s.wait_ge(sems["vU"], ch)
                        s.activation(sb["khsb"][:], psCH[:, 0:128], AF.Copy).then_inc(sems["sE"], 1)
                        s.wait_ge(sems["vR"], ch + 1)
                        if ch >= 1:
                            s.wait_ge(sems["vU"], ch)
                        s.activation(sb["kcol"][:, 15:16], sb["kcol"][:, 0:1], AF.Ln,
                                     scale=1.0 / 128,
                                     bias=CS[:, C_EPSI:C_EPSI + 1]).then_inc(sems["sCB"], 1)
                        scb_n[0] += 1
                        s.wait_ge(sems["sCB"], scb_n[0])
                        s.activation(sb["kcol"][:, 12:13], sb["kcol"][:, 15:16], AF.Exp,
                                     scale=-0.5).then_inc(sems["sCB"], 1)
                        scb_n[0] += 1
                        s.wait_ge(sems["sCB"], scb_n[0])
                        s.activation(sb["kcol"][:, 13:14], sb["kcol"][:, 12:13],
                                     AF.Square).then_inc(sems["sR"], 1)
                        sr_n[0] += 1
                        sr_at_scan[ch] = sr_n[0]
                        s.wait_ge(sems["tA"], ch + 1)
                        s.activation(sb["ats"][:], psA[:, 0:32], AF.Copy,
                                     scale=sb["ccol"][:, sl, 4:5]).then_inc(sems["sAT"], 1)
                        s.wait_ge(sems["tSU"], ch + 1)
                        s.activation(sb["S_sb"][:].rearrange("p a b -> p (a b)"), psS[:],
                                     AF.Copy).then_inc(sems["sSC"], 1)
                        if debug and ch < 8:
                            s.wait_ge(sems["vU"], ch + 1)
                            s.dma_start(out=DBG_KH[ch], in_=sb["khsb"][:]).then_inc(sems["stDG"], 16)
                            s.dma_start(out=DBG_U[ch], in_=sb["usb"][:]).then_inc(sems["stDG"], 16)
                            s.dma_start(out=DBG_KC[ch], in_=sb["kcol"][:]).then_inc(sems["stDG"], 16)
                            s.dma_start(out=DBG_CC[ch], in_=sb["ccsb"][:, sl, :]).then_inc(sems["stDG"], 16)
                            s.dma_start(out=DBG_CL[ch][:, 0:5], in_=sb["ccol"][:, sl, 0:5]).then_inc(sems["stDG"], 16)
                            s.wait_ge(sems["sSC"], ch + 1)
                            s.dma_start(out=DBG_SS[ch],
                                        in_=sb["S_sb"][:].rearrange("p a b -> p (a b)")).then_inc(sems["stDG"], 16)
                    s.wait_ge(sems["tCU"], 2 * gg + 2)
                    s.activation(sb["ohsb"][:, 0, :], psQHb[0][:, 0:128], AF.Copy).then_inc(sems["sOH"], 1)
                    s.activation(sb["ohsb"][:, 1, :], psQHb[1][:, 0:128], AF.Copy).then_inc(sems["sOH"], 1)
                    s.wait_ge(sems["ccx"], 2 * gg + 1)
                    s.activation(sb["ocol"][:, 2:3], sb["ocol"][:, 0:1], AF.Ln,
                                 scale=1.0 / 128, bias=CS[:, C_EPSI:C_EPSI + 1]).then_inc(sems["sCB"], 1)
                    scb_n[0] += 1
                    s.wait_ge(sems["sCB"], scb_n[0])
                    s.activation(sb["ocol"][:, 1:2], sb["ocol"][:, 2:3], AF.Exp,
                                 scale=-0.5).then_inc(sems["sR"], 1)
                    sr_n[0] += 1
                    sr_at_o[(gg, 1)] = sr_n[0]
                    s.wait_ge(sems["ccx"], 2 * gg + 2)
                    s.activation(sb["ocol"][:, 2:3], sb["ocol"][:, 3:4], AF.Ln,
                                 scale=1.0 / 128, bias=CS[:, C_EPSI:C_EPSI + 1]).then_inc(sems["sCB"], 1)
                    scb_n[0] += 1
                    s.wait_ge(sems["sCB"], scb_n[0])
                    s.activation(sb["ocol"][:, 3:4], sb["ocol"][:, 2:3], AF.Exp,
                                 scale=-0.5).then_inc(sems["sR"], 1)
                    sr_n[0] += 1
                    sr_at_o[(gg, 2)] = sr_n[0]
                    s.wait_ge(sems["tOG"], gg + 1)
                    if gg >= 1:
                        s.wait_ge(sems["stOG"], 16 * gg)
                    s.activation(sb["ogts"][:], psB[:], AF.Copy).then_inc(sems["sEV"], 1)
                    blk = gg // 32
                    s.dma_start(out=OGT[blk].rearrange("(p e) (s u) -> e p s u", e=128, u=32)[:, :, gg % 32, :],
                                in_=sb["ogts"][:].rearrange("p (c u) -> p c u", u=32)
                                ).then_inc(sems["stOG"], 16)
                    if debug:
                        s.dma_start(out=dbg["dbg_ogt"][blk].rearrange("(p e) (s u) -> e p s u", e=128, u=32)[:, :, gg % 32, :],
                                    in_=sb["ogts"][:].rearrange("p (c u) -> p c u", u=32)
                                    ).then_inc(sems["stDG"], 16)
            if debug:
                s.wait_ge(sems["sSC"], CP)
                s.dma_start(out=DBG_S, in_=sb["S_sb"][:].rearrange("p a b -> p (a b)")
                            ).then_inc(sems["stDG"], 16)

        cnt["sPR"] += 3 + 4 * NSCC
        cnt["sE"] += CP
        cnt["sCB"] += 2 * CP + 2 * NSCC
        cnt["sR"] += CP + 2 * NSCC
        cnt["sAT"] += CP
        cnt["sSC"] += CP
        cnt["sOH"] += 2 * NSCC
        cnt["sEV"] += 3 * NSCC
        cnt["stOG"] += 16 * NSCC
        if debug:
            cnt["stDG"] += 16 * (NSCC + 6 * min(CP, 8) + 1)
        em.emit("scalar", c_act)

        def c_vec(v):
            for g in range(NSCC + 1):
                if g < NSCC:
                    sl = g % NPREP
                    v.wait_ge(sems["ldp"], 64 * (g + 1))
                    if g >= NPREP:
                        v.wait_ge(sems["vU"], 2 * (g - NPREP) + 2)
                    v.tensor_scalar(sb["ccol"][:, sl, 4:5], sb["krsb"][:, sl, 128:129], 2.0, None, ALU.mult)
                    v.scalar_tensor_tensor(sb["scr"][:], sb["krsb"][:, sl, 0:128], 1.0,
                                           sb["krsb"][:, sl, 0:128], ALU.mult, ALU.mult)
                    v.reduce_sum(sb["ccol"][:, sl, 2:3], sb["scr"][:],
                                 axis=mybir.AxisListType.X).then_inc(sems["vPR"], 1)
                    v.wait_ge(sems["sPR"], 3 + 4 * g + 2)
                    v.tensor_scalar(sb["kisb"][:, sl, :], sb["krsb"][:, sl, 0:128],
                                    sb["ccol"][:, sl, 0:1], None, ALU.mult)
                    v.tensor_scalar(sb["kesb"][:, sl, :], sb["kisb"][:, sl, :],
                                    sb["ccol"][:, sl, 4:5], None, ALU.mult).then_inc(sems["vPR"], 1)
                    v.tensor_sub(sb["vmk"][:], sb["vsb"][:, sl, :], sb["kisb"][:, sl, :])
                    v.scalar_tensor_tensor(sb["scr"][:], sb["vmk"][:], -1.0, sb["w_bt"][:],
                                           ALU.mult, ALU.mult)
                    v.tensor_add(sb["ccsb"][:, sl, :], sb["scr"][:], sb["wbb_bt"][:])
                    v.tensor_scalar(sb["cc128"][:, sl, :], sb["ccsb"][:, sl, :],
                                    1.0 / 128, None, ALU.mult)
                    v.reduce_sum(sb["ccol"][:, sl, 1:2], sb["cc128"][:, sl, :],
                                 axis=mybir.AxisListType.X).then_inc(sems["vPR"], 1)
                if g >= 1:
                    gg = g - 1
                    sl = gg % NPREP
                    for par in (0, 1):
                        ch = 2 * gg + par
                        if ch >= CP:
                            break
                        mask = CS[:, C_MSKE + par:C_MSKE + par + 1]
                        am0 = C_AME if par == 0 else C_AMO
                        amask = CS[:, am0:am0 + 32]
                        v.wait_ge(sems["sE"], ch + 1)
                        v.scalar_tensor_tensor(sb["redb"][:, 0, :], sb["khsb"][:], 1.0,
                                               sb["khsb"][:], ALU.mult, ALU.mult)
                        v.scalar_tensor_tensor(sb["redb"][:, 1, :], sb["khsb"][:], 1.0,
                                               sb["w2n_bt"][:], ALU.mult, ALU.mult)
                        v.scalar_tensor_tensor(sb["redb"][:, 2, :], sb["redb"][:, 1, :], 1.0,
                                               sb["khsb"][:], ALU.mult, ALU.mult)
                        v.scalar_tensor_tensor(sb["redb"][:, 3, :], sb["khsb"][:], 1.0,
                                               sb["cc128"][:, sl, :], ALU.mult, ALU.mult)
                        v.reduce_sum(sb["kcol"][:, 0:4], sb["redb"][:],
                                     axis=mybir.AxisListType.X).then_inc(sems["vR"], 1)
                        v.wait_ge(sems["sAT"], ch + 1)
                        if ch >= 1:
                            v.wait_ge(sems["tCU"], ch)
                        v.tensor_tensor(sb["atm"][:], sb["ats"][:], amask, ALU.mult).then_inc(sems["vAT"], 1)
                        v.wait_ge(sems["sR"], sr_at_scan[ch])
                        kc = sb["kcol"]
                        v.tensor_scalar(kc[:, 5:6], kc[:, 2:3], -1.0 / 128, None, ALU.mult)
                        v.tensor_scalar(kc[:, 6:7], kc[:, 1:2], -1.0 / 128, None, ALU.mult)
                        v.tensor_tensor(kc[:, 7:8], kc[:, 12:13], mask, ALU.mult)
                        v.tensor_tensor(kc[:, 8:9], kc[:, 13:14], mask, ALU.mult)
                        v.tensor_tensor(kc[:, 9:10], kc[:, 3:4], kc[:, 12:13], ALU.mult)
                        lastd1 = v.tensor_tensor(kc[:, 11:12], kc[:, 12:13],
                                                 sb["ccol"][:, sl, 1:2], ALU.mult)
                        lastd1.then_inc(sems["vCX"], 1)
                        vcx_n[0] += 1
                        v.wait_ge(sems["vCX"], vcx_n[0])
                        v.scalar_tensor_tensor(kc[:, 10:11], kc[:, 5:6], kc[:, 13:14], kc[:, 9:10],
                                               ALU.mult, ALU.add)
                        v.scalar_tensor_tensor(kc[:, 4:5], kc[:, 6:7], kc[:, 13:14], kc[:, 11:12],
                                               ALU.mult, ALU.add).then_inc(sems["vCX"], 1)
                        vcx_n[0] += 1
                        v.wait_ge(sems["vCX"], vcx_n[0])
                        v.tensor_tensor(kc[:, 15:16], kc[:, 4:5], mask, ALU.mult)
                        v.tensor_scalar(kc[:, 14:15], kc[:, 7:8], -1.0, None,
                                        ALU.mult).then_inc(sems["vCX"], 1)
                        vcx_n[0] += 1
                        v.scalar_tensor_tensor(sb["scr"][:], sb["khsb"][:], kc[:, 10:11],
                                               sb["redb"][:, 1, :], ALU.mult, ALU.add)
                        v.wait_ge(sems["vCX"], vcx_n[0])
                        v.tensor_scalar(sb["t1sb"][:], sb["ccsb"][:, sl, :], kc[:, 14:15],
                                        kc[:, 15:16], ALU.mult, ALU.add)
                        if ch >= 1:
                            v.wait_ge(sems["tSU"], ch)
                        v.scalar_tensor_tensor(sb["usb"][:], sb["scr"][:], kc[:, 8:9],
                                               sb["t1sb"][:], ALU.mult, ALU.add).then_inc(sems["vU"], 1)
                        cp = None
                        for p in range(4):
                            cp = v.tensor_copy(sb["ubd"][32 * p:32 * p + 32, 128 * p:128 * (p + 1)],
                                               sb["usb"][32 * p:32 * p + 32, :])
                        cp.then_inc(sems["vBD"], 1)
                    v.wait_ge(sems["sOH"], 2 * gg + 2)
                    v.tensor_sub(sb["scr"][:], sb["ohsb"][:, 0, :], sb["ohsb"][:, 1, :])
                    v.scalar_tensor_tensor(sb["mrg"][:], sb["scr"][:],
                                           CS[:, C_MSKE:C_MSKE + 1], sb["ohsb"][:, 1, :],
                                           ALU.mult, ALU.add)
                    v.scalar_tensor_tensor(sb["scr"][:], sb["mrg"][:], 1.0, sb["mrg"][:],
                                           ALU.mult, ALU.mult)
                    v.reduce_sum(sb["ocol"][:, 0:1], sb["scr"][:],
                                 axis=mybir.AxisListType.X).then_inc(sems["ccx"], 1)
                    v.wait_ge(sems["sR"], sr_at_o[(gg, 1)])
                    v.tensor_scalar(sb["scr"][:], sb["mrg"][:], sb["ocol"][:, 1:2], None,
                                    ALU.mult)
                    v.scalar_tensor_tensor(sb["scr"][:], sb["scr"][:], 1.0, sb["w_bt"][:],
                                           ALU.mult, ALU.mult)
                    v.tensor_add(sb["scr"][:], sb["scr"][:], sb["b_bt"][:])
                    v.tensor_add(sb["scr"][:], sb["scr"][:], sb["qchsb"][:, sl, :])
                    v.tensor_tensor(sb["scr"][:], sb["scr"][:], sb["ssb"][:, sl, :], ALU.mult)
                    v.scalar_tensor_tensor(sb["t1sb"][:], sb["scr"][:], 1.0, sb["scr"][:],
                                           ALU.mult, ALU.mult)
                    v.reduce_sum(sb["ocol"][:, 3:4], sb["t1sb"][:],
                                 axis=mybir.AxisListType.X).then_inc(sems["ccx"], 1)
                    v.wait_ge(sems["sR"], sr_at_o[(gg, 2)])
                    v.tensor_scalar(sb["scr"][:], sb["scr"][:], sb["ocol"][:, 3:4], None,
                                    ALU.mult)
                    if gg >= 1:
                        v.wait_ge(sems["tOG"], gg)
                    v.tensor_tensor(sb["ogsb"][:], sb["scr"][:], sb["nw_bt"][:],
                                    ALU.mult).then_inc(sems["vOG"], 1)
                    v.nop().then_inc(sems["vOG2"], 1)

        cnt["vPR"] += 3 * NSCC
        cnt["vCX"] += 3 * CP
        cnt["vR"] += CP
        cnt["vAT"] += CP
        cnt["vU"] += CP
        cnt["vBD"] += CP
        cnt["ccx"] += 2 * NSCC
        cnt["vOG"] += NSCC
        cnt["vOG2"] += NSCC
        em.emit("vector", c_vec)

    S("xqt_t", (128, 4, KC, 128), BF16)
    S("ogt_t", (128, 2, 4, 128), BF16)
    S("wqt4", (128, KC, 4, 128), BF16)
    S("ycols", (128, 8), F32)
    S("t2col", (128, 8), F32)
    S("outsb", (128, 2, 512), F32)
    S("xqf", (128, D), F32)
    S("xqi", (128, D), I32)
    S("xqd2", (128, D), F32)
    sb["wfq"] = sb["xn_tok"]

    S("wfcol", (128, 64), F32)
    S("wfsc", (128, 8), F32)
    S("wfsb", (128, 144), BF16)

    for s_ in ("ldwos", "ldog", "stRS", "sX2", "tWO", "ldrs", "ldr", "vX2",
               "ldf", "sWO", "tF", "vWO", "ldw2", "vQ2", "vT", "sT", "sT2",
               "vQX", "stX", "ldq", "ldff", "ldf2", "tFF", "vFF", "stO"):
        em.sem(s_)

    x2f = sb["xnt_m"][:].rearrange("p a k f -> p (a k f)")
    wov2 = sb["wq_b"][:].rearrange("p k c -> p (k c)")
    x2p = sb["g1sb"]
    LW2 = 3 if debug else 2
    DX = 2 if debug else 1
    psW = [psQ[0], psQ[1], psKV[0], psKV[1]]

    if has("T"):
        def t_gload(g):
            g.wait_ge(sems["tQ"], cnt["tQ"])
            wsr = WOS.rearrange("(c p) n -> c p n", p=128)
            for fc in range(4):
                g.dma_start(out=wov2[:, 2048 * fc:2048 * (fc + 1)],
                            in_=wsr[fc]).then_inc(sems["ldwos"], 16)

            def wfld(i):
                if i >= 2:
                    g.wait_ge(sems["sWO"], i - 1)
                g.dma_start(out=sb["xqf" if i % 2 == 0 else "xqd2"][:],
                            in_=WF.rearrange("(n p) d -> n p d", p=128)[i]
                            ).then_inc(sems["ldf"], 16)
            if has("F"):
                for i in range(5):
                    wfld(i)
            g.wait_ge(sems["stRS"], 16 * 128)
            g.collective_compute(
                "ReduceScatter", ALU.add,
                replica_groups=[[0, 1, 2, 3], [4, 5, 6, 7]],
                ins=[RSI.rearrange("b q d -> (b q d)").opt()],
                outs=[RSO.rearrange("q d -> (q d)").opt()],
            ).then_inc(sems["ccx"], 1)
            if has("F"):
                for i in range(5, KC):
                    wfld(i)
        cnt["ldwos"] += 64
        cnt["ccx"] += 1
        ccs_done = cnt["ccx"]
        if has("F"):
            cnt["ldf"] += 16 * 16
        em.emit("gpsimd", t_gload)

        def t_sync(sync):
            for tt in range(32):
                if tt >= 2:
                    sync.wait_ge(sems["tWO"], 4 * (tt - 1))
                sync.wait_ge(sems["stOG"], 16 * 4 * (tt + 1))
                sync.dma_start(out=sb["ogt_t"][:, tt % 2, :, :],
                               in_=OGT[tt // 8][:, 128 * (tt % 8):128 * (tt % 8) + 128]
                               .rearrange("(c p) t -> p c t", p=128)
                               ).then_inc(sems["ldog"], 16)
            sync.wait_ge(sems["ccx"], ccs_done)
            rsr = RSO.rearrange("(n p) d -> n p d", p=128)
            xqr = XQ.rearrange("(n p) d -> n p d", p=128)
            for t in range(8):
                sync.dma_start(out=x2f[:, 2048 * t:2048 * (t + 1)],
                               in_=rsr[t]).then_inc(sems["ldrs"], 16)
                if t >= 2:
                    sync.wait_ge(sems["vX2"], t - 1)
                sync.dma_start(out=sb["xtile"][:, t % 2, :],
                               in_=xqr[t]).then_inc(sems["ldr"], 16)
            def ldq1(t):
                sync.wait_ge(sems["stX"], 16 * DX * (t + 1))
                if t >= 4:
                    sync.wait_ge(sems["tFF"], 64)
                sync.dma_start(out=sb["xqt_t"][:, t % 4, :, :],
                               in_=XQD[t].rearrange("p (k f) -> p k f", f=128),
                               transpose=True).then_inc(sems["ldq"], 16)

            def wqt1(fg, a):
                nq = fg % 4
                if fg >= 1 and a == 0:
                    sync.wait_ge(sems["tFF"], 16 * fg)
                sync.wait_ge(sems["ldw2"], 16 * LW2 * (4 * nq + a + 1))
                sync.dma_start(out=sb["wqt4"][:, :, a, :],
                               in_=WQR[4 * nq + a].rearrange("p (k f) -> p k f", f=128),
                               transpose=True).then_inc(sems["ldff"], 16)
            for t in range(4):
                ldq1(t)
            for fg in range(4):
                for a in range(4):
                    wqt1(fg, a)
            for t in range(4, 8):
                ldq1(t)
            for fg in range(4, 8):
                for a in range(4):
                    wqt1(fg, a)
        cnt["ldog"] += 16 * 32
        cnt["ldrs"] += 16 * 8
        cnt["ldr"] += 16 * 8
        cnt["ldq"] += 16 * 8
        cnt["ldff"] += 16 * 4 * 8
        em.emit("sync", t_sync)

        def wo_pe(t):
            t.wait_ge(sems["ldwos"], 64)
            for grp in range(128):
                tt, nq = grp // 4, grp % 4
                if nq == 0:
                    t.wait_ge(sems["ldog"], 16 * (tt + 1))
                if grp >= 4:
                    t.wait_ge(sems["sX2"], grp - 3)
                for fc in range(4):
                    mm = t.matmul(psW[grp % 4][:, 0:512],
                                  sb["ogt_t"][:, tt % 2, fc, :],
                                  wov2[:, 2048 * fc + 512 * nq:2048 * fc + 512 * (nq + 1)],
                                  start=(fc == 0), stop=(fc == 3))
                mm.then_inc(sems["tWO"], 1)
        cnt["tWO"] += 128
        em.emit("tensor", wo_pe)

        def wo_act(s):
            for grp in range(128):
                tt, nq = grp // 4, grp % 4
                s.wait_ge(sems["tWO"], grp + 1)
                if grp >= 2:
                    s.wait_ge(sems["stRS"], 16 * (grp - 1))
                s.activation(x2p[:, grp % 2, :], psW[grp % 4][:, 0:512],
                             AF.Copy).then_inc(sems["sX2"], 1)
                s.dma_start(out=RSI[tt // 8][128 * (tt % 8):128 * (tt % 8) + 128,
                                            512 * nq:512 * (nq + 1)],
                            in_=x2p[:, grp % 2, :]).then_inc(sems["stRS"], 16)
        cnt["sX2"] += 128
        cnt["stRS"] += 16 * 128
        em.emit("scalar", wo_act)

        def t_vec(v):
            for t in range(8):
                v.wait_ge(sems["ldrs"], 16 * (t + 1))
                v.wait_ge(sems["ldr"], 16 * (t + 1))
                v.tensor_tensor(x2f[:, 2048 * t:2048 * (t + 1)],
                                x2f[:, 2048 * t:2048 * (t + 1)],
                                sb["xtile"][:, t % 2, :],
                                ALU.add).then_inc(sems["vX2"], 1)
        cnt["vX2"] += 8
        em.emit("vector", t_vec)

    if has("F"):
        def f1_act(s):
            red512 = sb["redb"][:].rearrange("p a f -> p (a f)")
            for i in range(KC):
                s.wait_ge(sems["ldf"], 16 * (i + 1))
                buf = sb["xqf" if i % 2 == 0 else "xqd2"][:]
                for j in range(4):
                    mm = s.activation(red512, buf[:, 512 * j:512 * (j + 1)], AF.Abs,
                                      accum_out=sb["wfcol"][:, 4 * i + j:4 * i + j + 1])
                mm.then_inc(sems["sWO"], 1)
        cnt["sWO"] += KC
        em.emit("scalar", f1_act)

        def f1_pe(t):
            t.wait_ge(sems["vWO"], 4)
            t.matmul(psGT[1][0:1, 0:1], sb["wfsb"][:, 1:2], sb["wfsb"][:, 0:1],
                     start=True, stop=True).then_inc(sems["tF"], 1)
            t.wait_ge(sems["vWO"], 8)
            t.matmul(psGT[1][:, 1:3], sb["wfsb"][0:1, 8:136], sb["wfsb"][0:1, 2:4],
                     start=True, stop=True).then_inc(sems["tF"], 1)
        cnt["tF"] += 2
        em.emit("tensor", f1_pe)

        def f1_vec(v):
            v.wait_ge(sems["ld"], 64)
            v.tensor_copy(sb["wfsb"][:, 1:2], CS[:, C_ONE:C_ONE + 1]).then_inc(sems["vWO"], 1)
            v.tensor_copy(sb["wfsb"][0:1, 8:136], CS[0:1, 208:336]).then_inc(sems["vWO"], 1)
            v.wait_ge(sems["sWO"], KC)
            v.reduce_sum(sb["wfsc"][:, 0:1], sb["wfcol"][:],
                         axis=mybir.AxisListType.X).then_inc(sems["vWO"], 1)
            v.wait_ge(sems["vWO"], 3)
            v.tensor_copy(sb["wfsb"][:, 0:1], sb["wfsc"][:, 0:1]).then_inc(sems["vWO"], 1)
            v.wait_ge(sems["tF"], 1)
            v.tensor_scalar(sb["wfsc"][0:1, 1:2], psGT[1][0:1, 0:1], 1.0 / (D * D),
                            None, ALU.mult).then_inc(sems["vWO"], 1)
            v.wait_ge(sems["vWO"], 5)
            v.tensor_scalar(sb["wfsc"][0:1, 2:3], sb["wfsc"][0:1, 1:2], 1e-6, None,
                            ALU.max).then_inc(sems["vWO"], 1)
            v.wait_ge(sems["vWO"], 6)
            v.reciprocal(sb["wfsc"][0:1, 3:4], sb["wfsc"][0:1, 2:3]).then_inc(sems["vWO"], 1)
            v.wait_ge(sems["vWO"], 7)
            v.tensor_copy(sb["wfsb"][0:1, 2:4], sb["wfsc"][0:1, 2:4]).then_inc(sems["vWO"], 1)
            v.wait_ge(sems["tF"], 2)
            v.tensor_copy(sb["wfsc"][:, 4:6], psGT[1][:, 1:3]).then_inc(sems["vWO"], 1)
        cnt["vWO"] += 9
        em.emit("vector", f1_vec)

        def f2_g(g):
            g.wait_ge(sems["sWO"], KC)
            g.wait_ge(sems["vX2"], 8)
            for i in range(KC):
                if i >= 2:
                    g.wait_ge(sems["vQ2"], i - 1)
                g.dma_start(out=sb["xtile"][:, i % 2, :],
                            in_=WF.rearrange("(n p) d -> n p d", p=128)[i]).then_inc(sems["ldw2"], 16)
                g.wait_ge(sems["vQ2"], i + 1)
                g.dma_start(out=WQR[i], in_=sb["wfq"][:, i % 2, :]).then_inc(sems["ldw2"], 16)
                if debug:
                    g.dma_start(out=dbg["dbg_wqt"][i], in_=sb["wfq"][:, i % 2, :]).then_inc(sems["ldw2"], 16)
        cnt["ldw2"] += 16 * KC * LW2
        em.emit("gpsimd", f2_g)

        def f2_vec(v):
            v.wait_ge(sems["vWO"], 9)
            for i in range(KC):
                v.wait_ge(sems["ldw2"], 16 * (LW2 * i + 1))
                v.tensor_scalar(sb["xqd2"][:], sb["xtile"][:, i % 2, :],
                                sb["wfsc"][:, 5:6], None, ALU.mult)
                v.tensor_scalar(sb["xqf"][:], sb["xqd2"][:], 0.5, None, ALU.is_gt)
                v.scalar_tensor_tensor(sb["wfq"][:, i % 2, :], sb["xqd2"][:], -0.5,
                                       sb["xqf"][:], ALU.is_lt,
                                       ALU.subtract).then_inc(sems["vQ2"], 1)
        cnt["vQ2"] += KC
        em.emit("vector", f2_vec)

    if has("T"):
        def t2_vec(v):
            v.wait_ge(sems["ldw2"], 16 * KC * LW2)
            for t in range(8):
                x2t = x2f[:, 2048 * t:2048 * (t + 1)]
                sl = t % 2
                for j in range(4):
                    v.bn_stats(sb["bnst"][:, 6 * j:6 * (j + 1)],
                               x2t[:, 512 * j:512 * (j + 1)]).then_inc(sems["vT"], 1)
                v.wait_ge(sems["vT"], 5 * t + 4)
                v.bn_aggr(sb["p0col"][:, sl, 0:2], sb["bnst"][:]).then_inc(sems["vT"], 1)
                v.wait_ge(sems["sT"], 2 * (t + 1))
                v.tensor_scalar(sb["xqf"][:], x2t, sb["p0col"][:, sl, 0:1],
                                sb["p0col"][:, sl, 2:3], ALU.subtract,
                                ALU.mult).then_inc(sems["vQX"], 1)
                v.wait_ge(sems["sT2"], t + 1)
                v.reduce_max(sb["t2col"][:, t:t + 1], sb["xqd2"][:],
                             axis=mybir.AxisListType.X).then_inc(sems["vQX"], 1)
                v.wait_ge(sems["vQX"], 12 * t + 2)
                v.tensor_scalar(sb["t2col"][:, t:t + 1], sb["t2col"][:, t:t + 1],
                                1e-8, None, ALU.max).then_inc(sems["vQX"], 1)
                v.wait_ge(sems["vQX"], 12 * t + 3)
                v.reciprocal(sb["p0col"][:, sl, 3:4],
                             sb["t2col"][:, t:t + 1]).then_inc(sems["vQX"], 1)
                v.wait_ge(sems["vQX"], 12 * t + 4)
                v.tensor_scalar(sb["ycols"][:, t:t + 1], sb["t2col"][:, t:t + 1],
                                sb["wfsc"][:, 4:5], 1.0 / 127, ALU.mult,
                                ALU.mult).then_inc(sems["vQX"], 1)
                v.tensor_scalar(sb["xtile"][:, 0, :], sb["xqf"][:],
                                sb["p0col"][:, sl, 3:4], 127.0, ALU.mult,
                                ALU.mult).then_inc(sems["vQX"], 1)
                v.tensor_copy(sb["xqi"][:], sb["xtile"][:, 0, :]).then_inc(sems["vQX"], 1)
                v.tensor_copy(sb["xqd2"][:], sb["xqi"][:]).then_inc(sems["vQX"], 1)
                v.tensor_tensor(sb["xtile"][:, 0, :], sb["xtile"][:, 0, :],
                                sb["xqd2"][:], ALU.subtract).then_inc(sems["vQX"], 1)
                v.scalar_tensor_tensor(sb["xtile"][:, 1, :], sb["xtile"][:, 0, :], 0.5,
                                       sb["xqd2"][:], ALU.is_ge,
                                       ALU.add).then_inc(sems["vQX"], 1)
                v.scalar_tensor_tensor(sb["xqf"][:], sb["xtile"][:, 0, :], -0.5,
                                       sb["xtile"][:, 1, :], ALU.is_le,
                                       ALU.subtract).then_inc(sems["vQX"], 1)
                if t >= 2:
                    v.wait_ge(sems["stX"], 16 * DX * (t - 1))
                v.tensor_scalar(sb["wfq"][:, sl, :], sb["xqf"][:], 127.0, -127.0,
                                ALU.min, ALU.max).then_inc(sems["vQX"], 1)
        cnt["vT"] += 5 * 8
        cnt["vQX"] += 12 * 8
        em.emit("vector", t2_vec)

        def t2_act(s):
            for t in range(8):
                sl = t % 2
                s.wait_ge(sems["vT"], 5 * (t + 1))
                s.activation(sb["p0col"][:, sl, 3:4], sb["p0col"][:, sl, 1:2], AF.Ln,
                             bias=CS[:, C_EPSL:C_EPSL + 1]).then_inc(sems["sT"], 1)
                s.wait_ge(sems["sT"], 2 * t + 1)
                s.activation(sb["p0col"][:, sl, 2:3], sb["p0col"][:, sl, 3:4], AF.Exp,
                             scale=-0.5).then_inc(sems["sT"], 1)
                s.wait_ge(sems["vQX"], 12 * t + 1)
                s.activation(sb["xqd2"][:], sb["xqf"][:], AF.Abs).then_inc(sems["sT2"], 1)
                s.wait_ge(sems["vQX"], 12 * (t + 1))
                s.dma_start(out=XQD[t], in_=sb["wfq"][:, sl, :]).then_inc(sems["stX"], 16)
                if debug:
                    s.dma_start(out=dbg["dbg_xq2"][t], in_=sb["wfq"][:, sl, :]).then_inc(sems["stX"], 16)
        cnt["sT"] += 2 * 8
        cnt["sT2"] += 8
        cnt["stX"] += 16 * 8 * DX
        em.emit("scalar", t2_act)

        def t3_pe(t):
            t.wait_ge(sems["sX2"], 128)
            for idx2 in range(128):
                th, nq, k = idx2 // 64, (idx2 // 16) % 4, idx2 % 16
                fg = idx2 // 16
                if k == 0:
                    t.wait_ge(sems["ldff"], 16 * 4 * (fg + 1))
                    t.wait_ge(sems["ldq"], 16 * 4 * (th + 1))
                    if fg >= 1:
                        t.wait_ge(sems["vFF"], 4 * fg)
                for ti in range(4):
                    mm = t.matmul(psW[ti][:, 0:512],
                                  sb["xqt_t"][:, ti, k, :],
                                  sb["wqt4"][:, k, :, :].rearrange("p a f -> p (a f)"),
                                  start=(k == 0), stop=(k == KC - 1))
                mm.then_inc(sems["tFF"], 1)
        cnt["tFF"] += 128
        em.emit("tensor", t3_pe)

        def t3_vec(v):
            for fg in range(8):
                th, nq = fg // 4, fg % 4
                v.wait_ge(sems["tFF"], 16 * (fg + 1))
                v.wait_ge(sems["vQX"], 12 * 4 * (th + 1))
                for ti in range(4):
                    tile = 4 * th + ti
                    j = 4 * fg + ti
                    if j >= 2:
                        v.wait_ge(sems["stO"], 16 * (j - 1))
                    v.scalar_tensor_tensor(
                        sb["outsb"][:, j % 2, :], psW[ti][:, 0:512],
                        sb["ycols"][:, tile:tile + 1],
                        x2f[:, 2048 * tile + 512 * nq:2048 * tile + 512 * (nq + 1)],
                        ALU.mult, ALU.add).then_inc(sems["vFF"], 1)
        cnt["vFF"] += 32
        em.emit("vector", t3_vec)

        def t3_st(s):
            for j in range(32):
                fg, ti = j // 4, j % 4
                th, nq = fg // 4, fg % 4
                tile = 4 * th + ti
                s.wait_ge(sems["vFF"], j + 1)
                s.dma_start(out=OUT[128 * tile:128 * (tile + 1), 512 * nq:512 * (nq + 1)],
                            in_=sb["outsb"][:, j % 2, :]).then_inc(sems["stO"], 16)
            if debug:
                s.dma_start(out=dbg["dbg_x2"], in_=x2f).then_inc(sems["stDG"], 16)
                s.dma_start(out=dbg["dbg_tc"][:, 0:8], in_=sb["t2col"][:]).then_inc(sems["stDG"], 16)
                s.dma_start(out=dbg["dbg_tc"][:, 8:16], in_=sb["ycols"][:]).then_inc(sems["stDG"], 16)
        cnt["stO"] += 16 * 32
        if debug:
            cnt["stDG"] += 16 * 3
        em.emit("scalar", t3_st)

    finals = {k: v for k, v in cnt.items()}

    def final_sync(sync):
        for k, v in finals.items():
            if v > 0:
                sync.wait_ge(sems[k], v)
    em.emit("sync", final_sync)

    em.build()
    ctx.close()
    return nc



def build_consts():
    c = np.zeros((128, 640), np.float32)
    c[:, C_ONE] = 1.0
    c[:, C_I128] = 1.0 / 128
    c[:, C_EPSI] = EPS_IN
    c[:, C_EPSL] = EPS_LN
    c[:, C_HALF] = 0.5
    c[:, C_NEG1] = -1.0
    c[:, C_127] = 127.0
    t = np.arange(128) % 32
    c[:, C_MSKE] = (t < 16).astype(np.float32)
    c[:, C_MSKO] = (t >= 16).astype(np.float32)
    for ci in range(32):
        even = (t < 16) & (ci < 16) & (t <= ci)
        odd = (t >= 16) & (ci >= 16) & ((t - 16) <= (ci - 16))
        c[:, C_AME + ci] = even.astype(np.float32)
        c[:, C_AMO + ci] = odd.astype(np.float32)
    c[:, 208:336] = 1.0
    m = np.arange(128)
    for j in range(4):
        c[j, 80:208] = (m // 32 == j).astype(np.float32)
    return c


def host_prep(inputs):
    x = np.asarray(inputs["L0_attn_x"])
    wq = np.asarray(inputs["L0_attn_Wq"])
    wk = np.asarray(inputs["L0_attn_Wk"])
    wv = np.asarray(inputs["L0_attn_Wv"])
    wo = np.asarray(inputs["L0_attn_Wo"])
    wbeta = np.asarray(inputs["L0_attn_Wbeta"])
    lnw = np.asarray(inputs["L0_attn_ln_w"])
    lnb = np.asarray(inputs["L0_attn_ln_b"])
    wg1 = np.asarray(inputs["L0_attn_Wg1"])
    wg2 = np.asarray(inputs["L0_attn_Wg2"])
    nrm = np.asarray(inputs["L0_attn_norm_w"])
    wf = np.asarray(inputs["L0_ffn_weight"])
    consts = build_consts()

    in_maps = []
    for c in range(8):
        b = c // 4
        hg = c % 4
        hsl = slice(4 * hg * Dh, 4 * (hg + 1) * Dh)
        heads = [4 * hg + j for j in range(HPC)]
        wkvb = np.concatenate(
            [wk[:, hsl], wv[:, hsl], wbeta[:, 4 * hg:4 * hg + 4]], axis=1)
        m = {
            "x": np.ascontiguousarray(x[b]),
            "xq": np.ascontiguousarray(x[b, QTOK * hg:QTOK * (hg + 1)]),
            "wq": np.ascontiguousarray(wq[:, hsl]),
            "wkvb": np.ascontiguousarray(wkvb),
            "wg1": np.ascontiguousarray(wg1),
            "wg2": np.ascontiguousarray(wg2[:, hsl]),
            "wos": np.ascontiguousarray(wo[hsl, :]),
            "wf": np.ascontiguousarray(wf),
            "lnw": np.ascontiguousarray(lnw[heads]),
            "lnb": np.ascontiguousarray(lnb[heads]),
            "nrmw": np.ascontiguousarray(nrm[hsl]).reshape(HPC, Dh),
            "consts": consts,
        }
        in_maps.append(m)
    return in_maps


def kernel(**inputs):
    in_maps = host_prep(inputs)
    nc = build_program(debug=DEBUG)
    res = run_bass_kernel_spmd(nc, in_maps, core_ids=list(range(8)))
    out = np.zeros((B, T, D), np.float32)
    for ci in range(8):
        b = ci // 4
        hg = ci % 4
        out[b, QTOK * hg:QTOK * (hg + 1)] = res.results[ci]["out"]
    return out


# version 09
# speedup vs baseline: 1.1882x; 1.0186x over previous
import numpy as np
import ml_dtypes
import concourse.bass as bass
import concourse.mybir as mybir
from concourse.bass_utils import run_bass_kernel_spmd
from contextlib import ExitStack

F32 = mybir.dt.float32
BF16 = mybir.dt.bfloat16
I32 = mybir.dt.int32
AF = mybir.ActivationFunctionType
ALU = mybir.AluOpType

B, T, D, H = 2, 4096, 2048, 16
HPC = 4
Dh = 128
SC = 32
NSC = T // SC
CH = 16
NT = T // 128
KC = D // 128
QTOK = T // 4
EPS_LN = 1e-5
EPS_IN = 1e-6
DEBUG = False
import os
PARTS = set(os.environ.get("KB_PARTS", "all").split(","))
def has(p):
    return "all" in PARTS or p in PARTS

C_ONE, C_I128, C_EPSI, C_EPSL, C_HALF, C_ZERO, C_NEG1, C_127 = range(8)
C_MSKE, C_MSKO = 8, 9
C_AME, C_AMO = 16, 48


class Emitter:

    def __init__(self, nc, ctx):
        self.nc = nc
        self.ops = {k: [] for k in ("sync", "vector", "scalar", "tensor", "gpsimd")}
        self.sems = {}
        self.cnt = {}
        self.ctx = ctx

    def sem(self, name):
        if name not in self.sems:
            self.sems[name] = self.ctx.enter_context(self.nc.semaphore(name))
            self.cnt[name] = 0
        return self.sems[name]

    def emit(self, eng, fn):
        self.ops[eng].append(fn)

    def inc(self, name, n=1):
        self.sem(name)
        self.cnt[name] += n
        return self.cnt[name]

    def build(self):
        blk_ctx = self.nc.Block()
        block = blk_ctx.__enter__()

        def run(eng_name):
            def body(eng):
                for fn in self.ops[eng_name]:
                    fn(eng)
            return body

        block.sync(run("sync"))
        block.scalar(run("scalar"))
        block.vector(run("vector"))
        block.tensor(run("tensor"))
        block.gpsimd(run("gpsimd"))
        blk_ctx.__exit__(None, None, None)


def build_program(debug=False):
    nc = bass.Bass("TRN2", target_bir_lowering=False, detect_race_conditions=False)
    ctx = ExitStack()
    em = Emitter(nc, ctx)
    sems, cnt = em.sems, em.cnt

    dram = {}

    def din(name, shape, dtype=F32):
        dram[name] = nc.dram_tensor(name, list(shape), dtype, kind="ExternalInput").ap()
        return dram[name]

    def dout(name, shape, dtype=F32):
        dram[name] = nc.dram_tensor(name, list(shape), dtype, kind="ExternalOutput").ap()
        return dram[name]

    def dint(name, shape, dtype=BF16):
        dram[name] = nc.dram_tensor(name, list(shape), dtype, kind="Internal").ap()
        return dram[name]

    X = din("x", (T, D))
    XQ = din("xq", (QTOK, D))
    WQ = din("wq", (D, 512))
    WKVB = din("wkvb", (D, 4 * 257))
    WG1 = din("wg1", (D, 128))
    WG2 = din("wg2", (128, 512))
    WOS = din("wos", (512, D))
    WF = din("wf", (D, D))
    LNW = din("lnw", (HPC, Dh))
    LNB = din("lnb", (HPC, Dh))
    NRMW = din("nrmw", (HPC, Dh))
    CONSTS = din("consts", (128, 640))
    OUT = dout("out", (QTOK, D), F32)

    XN = dint("xnb", (NT, 128, D), BF16)
    QTD = dint("qtd", (4, 128, T), BF16)
    KT = dint("kt", (4, T, 128), BF16)
    ETJ = dint("etj", (4, T), BF16)
    VT = dint("vt4", (4, T, 128), BF16)
    ST = dint("st4", (4, T, 128), BF16)
    WQR = dint("wqraw", (KC, 128, D), BF16)
    OGT = dint("ogt", (4, 512, QTOK), BF16)
    RSI = dint("rsi", (4, QTOK, D), BF16)
    RSO = dint("rso", (QTOK, D), BF16)
    XQD = dint("xqd", (8, 128, D), BF16)

    dbg = {}
    if debug:
        for nm, src in (("dbg_xnt", XN), ("dbg_qtd", QTD), ("dbg_wqt", WQR),
                        ("dbg_ogt", OGT), ("dbg_xq2", XQD)):
            dbg[nm] = dout(nm, src.shape, BF16)
        dbg["dbg_x2"] = dout("dbg_x2", (128, 16384), BF16)
        dbg["dbg_tc"] = dout("dbg_tc", (128, 16), F32)

    sb = {}

    def S(name, shape, dtype):
        sb[name] = ctx.enter_context(nc.sbuf_tensor(name, list(shape), dtype))
        return sb[name]

    S("wq_b", (128, KC, 512), BF16)
    S("wkvb_b", (128, KC, 1028), BF16)
    S("wg1_b", (128, KC, 128), BF16)
    S("wg2_b", (128, 512), BF16)
    S("lnw_s", (HPC, Dh), F32)
    S("lnb_s", (HPC, Dh), F32)
    S("nrmw_s", (HPC, Dh), F32)
    CS = S("consts_s", (128, 640), F32)

    S("xtile", (128, 2, D), F32)
    S("bnst", (128, 24), F32)
    S("p0col", (128, 2, 4), F32)
    S("xn_tok", (128, 2, D), BF16)

    def p0_gload(g):
        g.dma_start(out=sb["wq_b"][:], in_=WQ.rearrange("(k p) c -> p k c", p=128)).then_inc(sems["ldw"], 16)
        g.dma_start(out=sb["wkvb_b"][:], in_=WKVB.rearrange("(k p) c -> p k c", p=128)).then_inc(sems["ldw"], 16)
        g.dma_start(out=sb["wg1_b"][:], in_=WG1.rearrange("(k p) c -> p k c", p=128)).then_inc(sems["ldw"], 16)
        g.dma_start(out=sb["wg2_b"][:], in_=WG2).then_inc(sems["ldw"], 16)
    em.sem("ldw")
    if has("wload"):
        cnt["ldw"] += 64
        em.emit("gpsimd", p0_gload)

    def p0_small(sync):
        sync.dma_start(out=sb["lnw_s"][:], in_=LNW).then_inc(sems["ld"], 16)
        sync.dma_start(out=sb["lnb_s"][:], in_=LNB).then_inc(sems["ld"], 16)
        sync.dma_start(out=sb["nrmw_s"][:], in_=NRMW).then_inc(sems["ld"], 16)
        sync.dma_start(out=sb["consts_s"][:], in_=CONSTS).then_inc(sems["ld"], 16)
    em.sem("ld"); cnt["ld"] += 64
    em.emit("sync", p0_small)

    em.sem("ldx"); em.sem("vA"); em.sem("sA"); em.sem("xs"); em.sem("stq")

    def p0_x_loads(sync):
        for i in range(int(os.environ.get("KB_NTX", NT))):
            if i >= 2 and has("xvec"):
                sync.wait_ge(sems["vA"], 2 * (i - 1))
            sync.dma_start(out=sb["xtile"][:, i % 2, :],
                           in_=X.rearrange("(n p) d -> n p d", p=128)[i]).then_inc(sems["ldx"], 16)
    if has("xln"):
        cnt["ldx"] += 16 * NT
        em.emit("sync", p0_x_loads)

    def p0_x_vec(v):
        v.wait_ge(sems["ld"], 64)
        for i in range(NT):
            sl = i % 2
            v.wait_ge(sems["ldx"], 16 * (i + 1))
            for j in range(4):
                v.bn_stats(sb["bnst"][:, 6 * j:6 * (j + 1)],
                           sb["xtile"][:, sl, 512 * j:512 * (j + 1)]).then_inc(sems["vB"], 1)
            v.wait_ge(sems["vB"], 4 * (i + 1))
            v.bn_aggr(sb["p0col"][:, sl, 0:2], sb["bnst"][:]).then_inc(sems["vA"], 1)
            if has("xact"):
                v.wait_ge(sems["sA"], i + 1)
                if i >= 2:
                    v.wait_ge(sems["stq"], 16 * (i - 1) * (2 if debug else 1))
            v.tensor_scalar(sb["xn_tok"][:, sl, :], sb["xtile"][:, sl, :],
                            sb["p0col"][:, sl, 0:1], sb["p0col"][:, sl, 2:3],
                            ALU.subtract, ALU.mult).then_inc(sems["vA"], 1)
    if has("xln") and has("xvec"):
        cnt["vA"] += 2 * NT
        em.emit("vector", p0_x_vec)

    def p0_x_act(s):
        s.wait_ge(sems["ld"], 64)
        for i in range(NT):
            sl = i % 2
            s.wait_ge(sems["vA"], 2 * i + 1)
            s.activation(sb["p0col"][:, sl, 3:4], sb["p0col"][:, sl, 1:2], AF.Ln,
                         bias=CS[:, C_EPSL:C_EPSL + 1]).then_inc(sems["sB"], 1)
            s.wait_ge(sems["sB"], i + 1)
            s.activation(sb["p0col"][:, sl, 2:3], sb["p0col"][:, sl, 3:4], AF.Exp,
                         scale=-0.5).then_inc(sems["sA"], 1)
            s.wait_ge(sems["vA"], 2 * i + 2)
            s.dma_start(out=XN[i], in_=sb["xn_tok"][:, sl, :]).then_inc(sems["stq"], 16)
            if debug:
                s.dma_start(out=dbg["dbg_xnt"][i], in_=sb["xn_tok"][:, sl, :]).then_inc(sems["stq"], 16)

    em.sem("sB"); em.sem("vB")
    if has("xln") and has("xvec"):
        cnt["vB"] += 4 * NT
    if has("xln") and has("xact"):
        cnt["sB"] += NT
        cnt["sA"] += NT
        cnt["stq"] += 16 * NT * (2 if debug else 1)
        em.emit("scalar", p0_x_act)




    SCM = 16
    NM = 8
    S("xnt_m", (128, 2, KC, 512), BF16)
    S("sigq", (128, 512), F32)
    S("qsil", (128, 512), BF16)
    S("g1sb", (128, 2, 512), BF16)
    S("ktok", (128, 2, 512), BF16)
    S("vtok", (128, 2, 512), BF16)
    S("sgtok", (128, 2, 512), BF16)
    S("etst", (128, 16), BF16)
    psQ = [ctx.enter_context(nc.psum_tensor(f"psQ{i}", [128, 512], F32)) for i in range(2)]
    psKV = [ctx.enter_context(nc.psum_tensor(f"psKV{i}", [128, 512], F32)) for i in range(2)]
    psG = ctx.enter_context(nc.psum_tensor("psG", [128, 512], F32))
    psGT = [ctx.enter_context(nc.psum_tensor(f"psGT{i}", [128, 512], F32)) for i in range(2)]

    for s_ in ("ldB", "tQ", "tKV", "tG", "tGT", "tB", "sQ1", "vQ", "sKV1",
               "vKV", "sG1", "sGT1", "stB", "stG", "stq2", "stE"):
        em.sem(s_)

    if has("B"):
        def b_loads(sync):
            for m in range(NM):
                for j in range(4):
                    i = 4 * m + j
                    if m >= 2 and has("Bpe") and has("Bsc"):
                        sync.wait_ge(sems["tKV"], 4 * (m - 1))
                    sync.wait_ge(sems["stq"], 16 * (i + 1) * (2 if debug else 1))
                    sync.dma_start(out=sb["xnt_m"][:, m % 2, :, 128 * j:128 * (j + 1)],
                                   in_=XN[i].rearrange("p (k f) -> p k f", k=KC),
                                   transpose=True).then_inc(sems["ldB"], 16)
        cnt["ldB"] += 16 * 4 * NM
        em.emit("sync", b_loads)
    if has("B") and has("Bpe"):

        def b_pe(t):
            t.wait_ge(sems["ldw"], 64)
            t.wait_ge(sems["gInit"], 1)

            def sg_mm(j):
                m2 = j // 4
                t.wait_ge(sems["sG1"], m2 + 1)
                t.wait_ge(sems["sKV1"], j + 1)
                t.matmul(psGT[j % 2][:, 0:512],
                         sb["g1sb"][:, m2 % 2, 128 * (j % 4):128 * (j % 4) + 128],
                         sb["wg2_b"][:], start=True, stop=True).then_inc(sems["tGT"], 1)
            for m in range(NM):
                ms = m % 2
                t.wait_ge(sems["ldB"], 16 * 4 * (m + 1))
                xm = sb["xnt_m"][:, ms, :, :]
                for mch in range(4):
                    if 4 * m + mch >= 2 and has("Bsc"):
                        t.wait_ge(sems["vQ"], 4 * m + mch - 1)
                    mm = None
                    for k in range(KC):
                        mm = t.matmul(psQ[mch % 2][:],
                                      sb["wq_b"][:, k, 128 * mch:128 * (mch + 1)],
                                      xm[:, k, :], start=(k == 0), stop=(k == KC - 1))
                    mm.then_inc(sems["tQ"], 1)
                if m >= 2 and has("Bsc"):
                    t.wait_ge(sems["sG1"], m - 1)
                if m >= 1 and has("Bsc"):
                    t.wait_ge(sems["vKV"], 2 * m - 1)
                mm = None
                for k in range(KC):
                    mm = t.matmul(psG[:], sb["wg1_b"][:, k, :], xm[:, k, :],
                                  start=(k == 0), stop=(k == KC - 1))
                mm.then_inc(sems["tG"], 1)
                if has("Bsc"):
                    t.wait_ge(sems["sG1"], m + 1)
                mm = None
                for mc in range(4):
                    for k in range(KC):
                        mm = t.matmul(psG[:, 4 * mc:4 * mc + 4],
                                      xm[:, k, 128 * mc:128 * (mc + 1)],
                                      sb["wkvb_b"][:, k, 1024:1028],
                                      start=(k == 0), stop=(k == KC - 1))
                mm.then_inc(sems["tG"], 1)
                t.wait_ge(sems["vKV"], 2 * m + 1)
                t.transpose(psB[0:16, :], sb["etst"][:], sb["ident_t"][:]).then_inc(sems["tB"], 1)
                for mc in range(4):
                    idx = 4 * m + mc
                    if idx >= 2:
                        sg_mm(idx - 2)
                    if idx >= 2 and has("Bsc"):
                        t.wait_ge(sems["sKV1"], idx - 1)
                        t.wait_ge(sems["sGT1"], idx - 1)
                    mm = None
                    for k in range(KC):
                        mm = t.matmul(psKV[mc % 2][:, 0:512],
                                      xm[:, k, 128 * mc:128 * (mc + 1)],
                                      sb["wkvb_b"][:, k, 0:512],
                                      start=(k == 0), stop=(k == KC - 1))
                    for k in range(KC):
                        mm = t.matmul(psGT[mc % 2][:, 0:512],
                                      xm[:, k, 128 * mc:128 * (mc + 1)],
                                      sb["wkvb_b"][:, k, 512:1024],
                                      start=(k == 0), stop=(k == KC - 1))
                    mm.then_inc(sems["tKV"], 1)
            sg_mm(4 * NM - 2)
            sg_mm(4 * NM - 1)
        if has("Bpe"):
            cnt["tQ"] += 4 * NM
            cnt["tKV"] += 4 * NM
            cnt["tG"] += 2 * NM
            cnt["tGT"] += 4 * NM
            em.emit("tensor", b_pe)

        def b_scalar(s):
            for m in range(NM):
                ms = m % 2
                for mch in range(4):
                    s.wait_ge(sems["tQ"], 4 * m + mch + 1)
                    s.activation(sb["sigq"][:], psQ[mch % 2][:], AF.Sigmoid).then_inc(sems["sQ1"], 1)
                    s.wait_ge(sems["vQ"], 4 * m + mch + 1)
                    s.dma_start(out=QTD[mch][:, 512 * m:512 * (m + 1)],
                                in_=sb["qsil"][:]).then_inc(sems["stq2"], 16)
                    if debug:
                        s.dma_start(out=dbg["dbg_qtd"][mch][:, 512 * m:512 * (m + 1)],
                                    in_=sb["qsil"][:]).then_inc(sems["stq2"], 16)
                    s.wait_ge(sems["stq2"], (32 if debug else 16) * (4 * m + mch + 1))
                s.wait_ge(sems["tG"], 2 * m + 1)
                s.activation(sb["g1sb"][:, ms, :], psG[:], AF.Copy).then_inc(sems["sG1"], 1)
                s.wait_ge(sems["tG"], 2 * m + 2)
                s.activation(sb["etst"][:], psG[:, 0:16], AF.Copy).then_inc(sems["vKV"], 1)
                s.wait_ge(sems["tB"], m + 1)
                if m >= 1:
                    s.wait_ge(sems["stE"], 16 * m)
                s.activation(sb["ogts"][0:16, :], psB[0:16, :], AF.Sigmoid).then_inc(sems["vKV"], 1)
                s.dma_start(out=ETJ[:, 512 * m:512 * (m + 1)].rearrange("j (mc t) -> mc j t", t=128),
                            in_=sb["ogts"][0:16, :]).then_inc(sems["stE"], 16)

                def sg_ev(j):
                    s.wait_ge(sems["tGT"], j + 1)
                    if j >= 2:
                        s.wait_ge(sems["stG"], 16 * (j - 1))
                    s.activation(sb["sgtok"][:, j % 2, :], psGT[j % 2][:, 0:512],
                                 AF.Sigmoid).then_inc(sems["sGT1"], 1)
                    s.dma_start(out=ST[:, 128 * j:128 * (j + 1), :].rearrange("j t f -> t j f"),
                                in_=sb["sgtok"][:, j % 2, :].rearrange("t (j f) -> t j f", f=128)
                                ).then_inc(sems["stG"], 16)
                for mc in range(4):
                    idx = 4 * m + mc
                    if idx >= 2:
                        sg_ev(idx - 2)
                    s.wait_ge(sems["tKV"], idx + 1)
                    if idx >= 2:
                        s.wait_ge(sems["stB"], 32 * (idx - 1))
                    s.activation(sb["ktok"][:, mc % 2, :], psKV[mc % 2][:, 0:512],
                                 AF.Copy)
                    s.activation(sb["vtok"][:, mc % 2, :], psGT[mc % 2][:, 0:512],
                                 AF.Copy).then_inc(sems["sKV1"], 1)
                    s.dma_start(out=KT[:, 128 * idx:128 * (idx + 1), :].rearrange("j t f -> t j f"),
                                in_=sb["ktok"][:, mc % 2, :].rearrange("t (j f) -> t j f", f=128)
                                ).then_inc(sems["stB"], 16)
                    s.dma_start(out=VT[:, 128 * idx:128 * (idx + 1), :].rearrange("j t f -> t j f"),
                                in_=sb["vtok"][:, mc % 2, :].rearrange("t (j f) -> t j f", f=128)
                                ).then_inc(sems["stB"], 16)
            sg_ev(4 * NM - 2)
            sg_ev(4 * NM - 1)
        if has("Bpe") and has("Bsc"):
            cnt["sQ1"] += 4 * NM
            cnt["stq2"] += (32 if debug else 16) * 4 * NM
            cnt["sG1"] += NM
            cnt["vKV"] += 2 * NM
            cnt["tB"] += NM
            cnt["stE"] += 16 * NM
            cnt["sKV1"] += 4 * NM
            cnt["sGT1"] += 4 * NM
            cnt["stB"] += 32 * 4 * NM
            cnt["stG"] += 16 * 4 * NM
            em.emit("scalar", b_scalar)

        def b_vec(v):
            for m in range(NM):
                for mch in range(4):
                    v.wait_ge(sems["sQ1"], 4 * m + mch + 1)
                    v.tensor_tensor(sb["qsil"][:], psQ[mch % 2][:], sb["sigq"][:],
                                    ALU.mult).then_inc(sems["vQ"], 1)
        if has("Bpe") and has("Bsc"):
            cnt["vQ"] += 4 * NM
            em.emit("vector", b_vec)

    NPREP = 3
    S("krsb", (128, NPREP, 132), BF16)
    S("vsb", (128, NPREP, 128), BF16)
    S("ssb", (128, NPREP, 128), BF16)
    S("qisb", (128, NPREP, 4, 32), BF16)
    S("kisb", (128, NPREP, 128), BF16)
    S("kesb", (128, NPREP, 128), BF16)
    S("kitsb", (128, NPREP, 128), BF16)
    S("qchsb", (128, NPREP, 128), BF16)
    S("ccsb", (128, NPREP, 128), F32)
    S("cc128", (128, NPREP, 128), F32)
    S("vmk", (128, 128), BF16)
    S("ccol", (128, NPREP, 8), F32)
    S("S_sb", (128, 2, 4, 128), BF16)
    S("khsb", (128, 128), BF16)
    S("t1sb", (128, 128), F32)
    S("redb", (128, 4, 128), F32)
    S("scr", (128, 128), F32)
    S("ats", (128, 32), F32)
    S("usb", (128, 128), BF16)
    S("ubd", (128, 512), BF16)
    S("atm", (128, 32), BF16)
    S("kcol", (128, 16), F32)
    S("ohsb", (128, 2, 128), BF16)
    S("mrg", (128, 128), BF16)
    S("ogsb", (128, 128), BF16)
    S("ogts", (128, 128), BF16)
    S("ocol", (128, 4), F32)
    S("w_bt", (128, 128), F32)
    S("w2n_bt", (128, 128), F32)
    S("b_bt", (128, 128), F32)
    S("wbb_bt", (128, 128), F32)
    S("nw_bt", (128, 128), F32)
    S("mskE_t", (128, 128), BF16)
    S("ident_t", (128, 128), BF16)
    S("ones_t", (128, 128), F32)

    psS = psQ[0]
    psCH = psGT[0]
    psA = psGT[1]
    psQHb = psKV
    psB = ctx.enter_context(nc.psum_tensor("psB", [128, 128], BF16))

    for s_ in ("vInit", "gInit", "ldp", "vPR", "sPR", "tPR", "tKH", "tA",
               "tCU", "tSU", "sE", "vR", "sR", "sCB", "vU", "vBD", "sSC",
               "sAT", "vAT", "sOH", "vOG", "vOG2", "tOG", "sEV", "stOG", "stDG", "vCX",
               "ccx", "ldo", "tWO", "sWO", "vWO", "stO", "ldw2", "tF", "ldf"):
        em.sem(s_)

    CP = int(os.environ.get("KB_NCHUNK", 2 * NSC))
    vcx_n = [0]
    NSCC = (CP + 1) // 2

    if has("C"):
        if debug:
            DBG_S = dout("dbg_S", (128, 512), BF16)
            DBG_KH = dout("dbg_kh", (8, 128, 128), BF16)
            DBG_U = dout("dbg_u", (8, 128, 128), BF16)
            DBG_KC = dout("dbg_kc", (8, 128, 16), F32)
            DBG_CC = dout("dbg_cc", (8, 128, 128), F32)
            DBG_CL = dout("dbg_cl", (8, 128, 8), F32)
            DBG_SS = dout("dbg_ss", (8, 128, 512), BF16)

        def c_init_g(g):
            g.memset(sb["ident_t"][:], 0.0)
            g.affine_select(out=sb["ident_t"][:], in_=sb["ident_t"][:],
                            compare_op=ALU.not_equal, fill=1.0, base=0,
                            pattern=[[-1, 128]], channel_multiplier=1)
            g.tensor_copy(sb["scr"][:, 0:1], sb["ident_t"][:, 0:1]).then_inc(sems["gInit"], 1)
        cnt["gInit"] += 1
        em.emit("gpsimd", c_init_g)

        def c_init_vec(v):
            v.wait_ge(sems["sPR"], 3)
            v.memset(sb["S_sb"][:], 0.0)
            v.memset(sb["ubd"][:], 0.0)
            v.memset(sb["ones_t"][:], 1.0)
            v.scalar_tensor_tensor(sb["w2n_bt"][:], sb["w_bt"][:], -1.0, sb["w_bt"][:],
                                   ALU.mult, ALU.mult).then_inc(sems["vInit"], 1)
            v.wait_ge(sems["vInit"], 1)
            v.tensor_tensor(sb["wbb_bt"][:], sb["w_bt"][:], sb["b_bt"][:], ALU.mult)
            v.tensor_scalar(sb["mskE_t"][:], sb["ones_t"][:], CS[:, C_MSKE:C_MSKE + 1],
                            None, ALU.mult).then_inc(sems["vInit"], 1)
        cnt["vInit"] += 2
        em.emit("vector", c_init_vec)

        def c_loads(sync):
            for g in range(NSCC):
                sl = g % NPREP
                if g >= NPREP:
                    sync.wait_ge(sems["tSU"], 2 * (g - NPREP) + 2)
                    sync.wait_ge(sems["vOG2"], g - NPREP + 1)
                sync.wait_ge(sems["stB"], 32 * (g // 4 + 1))
                sync.wait_ge(sems["stG"], 16 * (g // 4 + 1))
                sync.wait_ge(sems["stE"], 16 * (g // 16 + 1))
                sync.dma_start(out=sb["krsb"][:, sl, 0:128],
                               in_=KT[:, 32 * g:32 * (g + 1), :]).then_inc(sems["ldp"], 16)
                sync.dma_start(out=sb["krsb"][:, sl, 128:129],
                               in_=ETJ[:, 32 * g:32 * (g + 1)]).then_inc(sems["ldp"], 16)
                sync.dma_start(out=sb["vsb"][:, sl, :],
                               in_=VT[:, 32 * g:32 * (g + 1), :]).then_inc(sems["ldp"], 16)
                sync.dma_start(out=sb["ssb"][:, sl, :],
                               in_=ST[:, 32 * g:32 * (g + 1), :]).then_inc(sems["ldp"], 16)
                sync.wait_ge(sems["stq2"], (32 if debug else 16) * 4 * min(NM, g // 16 + 1))
                sync.dma_start(out=sb["qisb"][:, sl, :, :],
                               in_=QTD[:, :, 32 * g:32 * (g + 1)].rearrange("c p t -> p c t")
                               ).then_inc(sems["ldp"], 16)
        cnt["ldp"] += 80 * NSCC
        em.emit("sync", c_loads)

        psb_n = {"pe": 0}

        def psb_wait(t):
            k = psb_n["pe"]
            if k > 0:
                t.wait_ge(sems["sEV"], k)
            psb_n["pe"] = k + 1

        def c_pe(t):
            t.wait_ge(sems["vQ"], cnt["vQ"])
            t.wait_ge(sems["sKV1"], cnt["sKV1"])
            t.wait_ge(sems["sGT1"], cnt["sGT1"])
            t.wait_ge(sems["sG1"], cnt["sG1"])
            t.wait_ge(sems["ld"], 64)
            t.wait_ge(sems["gInit"], 1)
            for i in range(3):
                rhs = (sb["lnw_s"], sb["lnb_s"], sb["nrmw_s"])[i]
                t.matmul(psA[:, 0:128], CS[0:4, 80:208], rhs[:],
                         start=True, stop=True).then_inc(sems["tPR"], 1)
                t.wait_ge(sems["sPR"], i + 1)
            for g in range(NSCC + 1):
                if g < NSCC:
                    sl = g % NPREP
                    t.wait_ge(sems["vPR"], 3 * g + 2)
                    psb_wait(t)
                    t.transpose(psB[:], sb["kisb"][:, sl, :], sb["ident_t"][:]).then_inc(sems["tPR"], 1)
                    psb_wait(t)
                    t.transpose(psB[:], sb["qisb"][:, sl, :, :].rearrange("p c t -> p (c t)"),
                                sb["ident_t"][:]).then_inc(sems["tPR"], 1)
                if g >= 1:
                    gg = g - 1
                    sl = gg % NPREP
                    for par in (0, 1):
                        ch = 2 * gg + par
                        if ch >= CP:
                            break
                        t.wait_ge(sems["sPR"], 3 + 4 * gg + 4)
                        t.wait_ge(sems["sSC"], ch)
                        if ch == 0:
                            t.wait_ge(sems["vInit"], 2)
                        t.wait_ge(sems["sE"], ch)
                        if gg >= 1:
                            t.wait_ge(sems["sOH"], 2 * (gg - 1) + par + 1)
                        mm = None
                        for p in range(4):
                            mm = t.matmul(psCH[32 * p:32 * p + 32, 0:128],
                                          sb["kitsb"][:, sl, 32 * p:32 * p + 32],
                                          sb["S_sb"][:, ch % 2, p, :], start=True, stop=True,
                                          tile_position=(0, 32 * p))
                        mm.then_inc(sems["tKH"], 1)
                        mm = None
                        for p in range(4):
                            mm = t.matmul(psA[32 * p:32 * p + 32, 0:32],
                                          sb["kitsb"][:, sl, 32 * p:32 * p + 32],
                                          sb["qisb"][:, sl, p, :], start=True, stop=True,
                                          tile_position=(0, 32 * p))
                        mm.then_inc(sems["tA"], 1)
                        t.wait_ge(sems["vBD"], ch + 1)
                        t.matmul(psS[:], sb["kesb"][:, sl, :], sb["ubd"][:],
                                 start=(ch == 0), stop=True,
                                 skip_group_check=(ch > 0)).then_inc(sems["tSU"], 1)
                        t.wait_ge(sems["vAT"], ch + 1)
                        mm = None
                        for p in range(4):
                            t.matmul(psQHb[par][32 * p:32 * p + 32, 0:128],
                                     sb["atm"][32 * p:32 * p + 32, :],
                                     sb["usb"][32 * p:32 * p + 32, :],
                                     start=True, stop=False,
                                     tile_position=(32 * p, 32 * p))
                            mm = t.matmul(psQHb[par][32 * p:32 * p + 32, 0:128],
                                          sb["qisb"][:, sl, p, :],
                                          sb["S_sb"][:, ch % 2, p, :], start=False, stop=True,
                                          tile_position=(0, 32 * p))
                        mm.then_inc(sems["tCU"], 1)
                    t.wait_ge(sems["vOG"], gg + 1)
                    psb_wait(t)
                    t.transpose(psB[:], sb["ogsb"][:], sb["ident_t"][:]).then_inc(sems["tOG"], 1)
        cnt["tPR"] += 3 + 2 * NSCC
        cnt["tKH"] += CP
        cnt["tA"] += CP
        cnt["tCU"] += CP
        cnt["tSU"] += CP
        cnt["tOG"] += NSCC
        em.emit("tensor", c_pe)

        scb_n = [0]
        sr_n = [0]
        sr_at_scan = {}
        sr_at_o = {}

        def c_act(s):
            for i, dstn in enumerate(("w_bt", "b_bt", "nw_bt")):
                s.wait_ge(sems["tPR"], i + 1)
                s.activation(sb[dstn][:], psA[:, 0:128], AF.Copy).then_inc(sems["sPR"], 1)
            for g in range(NSCC + 1):
                if g < NSCC:
                    sl = g % NPREP
                    s.wait_ge(sems["vPR"], 3 * g + 1)
                    s.activation(sb["ccol"][:, sl, 3:4], sb["ccol"][:, sl, 2:3], AF.Ln,
                                 bias=CS[:, C_EPSI:C_EPSI + 1]).then_inc(sems["sPR"], 1)
                    s.wait_ge(sems["sPR"], 3 + 4 * g + 1)
                    s.activation(sb["ccol"][:, sl, 0:1], sb["ccol"][:, sl, 3:4], AF.Exp,
                                 scale=-0.5).then_inc(sems["sPR"], 1)
                    s.wait_ge(sems["tPR"], 3 + 2 * g + 1)
                    s.activation(sb["kitsb"][:, sl, :], psB[:], AF.Copy).then_inc(sems["sPR"], 1)
                    s.activation(sb["scr"][:, 0:1], sb["kitsb"][:, sl, 0:1], AF.Copy).then_inc(sems["sEV"], 1)
                    s.wait_ge(sems["tPR"], 3 + 2 * g + 2)
                    s.activation(sb["qchsb"][:, sl, :], psB[:], AF.Copy).then_inc(sems["sPR"], 1)
                    s.activation(sb["scr"][:, 0:1], sb["qchsb"][:, sl, 0:1], AF.Copy).then_inc(sems["sEV"], 1)
                if g >= 1:
                    gg = g - 1
                    sl = gg % NPREP
                    for par in (0, 1):
                        ch = 2 * gg + par
                        if ch >= CP:
                            break
                        s.wait_ge(sems["tKH"], ch + 1)
                        if ch >= 1:
                            s.wait_ge(sems["vU"], ch)
                        s.activation(sb["khsb"][:], psCH[:, 0:128], AF.Copy).then_inc(sems["sE"], 1)
                        s.wait_ge(sems["vR"], ch + 1)
                        if ch >= 1:
                            s.wait_ge(sems["vU"], ch)
                        s.activation(sb["kcol"][:, 15:16], sb["kcol"][:, 0:1], AF.Ln,
                                     scale=1.0 / 128,
                                     bias=CS[:, C_EPSI:C_EPSI + 1]).then_inc(sems["sCB"], 1)
                        scb_n[0] += 1
                        s.wait_ge(sems["sCB"], scb_n[0])
                        s.activation(sb["kcol"][:, 12:13], sb["kcol"][:, 15:16], AF.Exp,
                                     scale=-0.5).then_inc(sems["sCB"], 1)
                        scb_n[0] += 1
                        s.wait_ge(sems["sCB"], scb_n[0])
                        s.activation(sb["kcol"][:, 13:14], sb["kcol"][:, 12:13],
                                     AF.Square).then_inc(sems["sR"], 1)
                        sr_n[0] += 1
                        sr_at_scan[ch] = sr_n[0]
                        s.wait_ge(sems["tA"], ch + 1)
                        s.activation(sb["ats"][:], psA[:, 0:32], AF.Copy,
                                     scale=sb["ccol"][:, sl, 4:5]).then_inc(sems["sAT"], 1)
                        s.wait_ge(sems["tSU"], ch + 1)
                        if ch >= 1:
                            s.wait_ge(sems["tCU"], ch)
                        s.activation(sb["S_sb"][:, (ch + 1) % 2, :, :].rearrange("p a b -> p (a b)"),
                                     psS[:], AF.Copy).then_inc(sems["sSC"], 1)
                        if debug and ch < 8:
                            s.wait_ge(sems["vU"], ch + 1)
                            s.dma_start(out=DBG_KH[ch], in_=sb["khsb"][:]).then_inc(sems["stDG"], 16)
                            s.dma_start(out=DBG_U[ch], in_=sb["usb"][:]).then_inc(sems["stDG"], 16)
                            s.dma_start(out=DBG_KC[ch], in_=sb["kcol"][:]).then_inc(sems["stDG"], 16)
                            s.dma_start(out=DBG_CC[ch], in_=sb["ccsb"][:, sl, :]).then_inc(sems["stDG"], 16)
                            s.dma_start(out=DBG_CL[ch][:, 0:5], in_=sb["ccol"][:, sl, 0:5]).then_inc(sems["stDG"], 16)
                            s.wait_ge(sems["sSC"], ch + 1)
                            s.dma_start(out=DBG_SS[ch],
                                        in_=sb["S_sb"][:, (ch + 1) % 2, :, :].rearrange("p a b -> p (a b)")).then_inc(sems["stDG"], 16)
                    s.wait_ge(sems["tCU"], 2 * gg + 2)
                    s.activation(sb["ohsb"][:, 0, :], psQHb[0][:, 0:128], AF.Copy).then_inc(sems["sOH"], 1)
                    s.activation(sb["ohsb"][:, 1, :], psQHb[1][:, 0:128], AF.Copy).then_inc(sems["sOH"], 1)
                    s.wait_ge(sems["ccx"], 2 * gg + 1)
                    s.activation(sb["ocol"][:, 2:3], sb["ocol"][:, 0:1], AF.Ln,
                                 scale=1.0 / 128, bias=CS[:, C_EPSI:C_EPSI + 1]).then_inc(sems["sCB"], 1)
                    scb_n[0] += 1
                    s.wait_ge(sems["sCB"], scb_n[0])
                    s.activation(sb["ocol"][:, 1:2], sb["ocol"][:, 2:3], AF.Exp,
                                 scale=-0.5).then_inc(sems["sR"], 1)
                    sr_n[0] += 1
                    sr_at_o[(gg, 1)] = sr_n[0]
                    s.wait_ge(sems["ccx"], 2 * gg + 2)
                    s.activation(sb["ocol"][:, 2:3], sb["ocol"][:, 3:4], AF.Ln,
                                 scale=1.0 / 128, bias=CS[:, C_EPSI:C_EPSI + 1]).then_inc(sems["sCB"], 1)
                    scb_n[0] += 1
                    s.wait_ge(sems["sCB"], scb_n[0])
                    s.activation(sb["ocol"][:, 3:4], sb["ocol"][:, 2:3], AF.Exp,
                                 scale=-0.5).then_inc(sems["sR"], 1)
                    sr_n[0] += 1
                    sr_at_o[(gg, 2)] = sr_n[0]
                    s.wait_ge(sems["tOG"], gg + 1)
                    if gg >= 1:
                        s.wait_ge(sems["stOG"], 16 * gg)
                    s.activation(sb["ogts"][:], psB[:], AF.Copy).then_inc(sems["sEV"], 1)
                    blk = gg // 32
                    s.dma_start(out=OGT[blk].rearrange("(p e) (s u) -> e p s u", e=128, u=32)[:, :, gg % 32, :],
                                in_=sb["ogts"][:].rearrange("p (c u) -> p c u", u=32)
                                ).then_inc(sems["stOG"], 16)
                    if debug:
                        s.dma_start(out=dbg["dbg_ogt"][blk].rearrange("(p e) (s u) -> e p s u", e=128, u=32)[:, :, gg % 32, :],
                                    in_=sb["ogts"][:].rearrange("p (c u) -> p c u", u=32)
                                    ).then_inc(sems["stDG"], 16)
            if debug:
                s.wait_ge(sems["sSC"], CP)
                s.dma_start(out=DBG_S, in_=sb["S_sb"][:, CP % 2, :, :].rearrange("p a b -> p (a b)")
                            ).then_inc(sems["stDG"], 16)

        cnt["sPR"] += 3 + 4 * NSCC
        cnt["sE"] += CP
        cnt["sCB"] += 2 * CP + 2 * NSCC
        cnt["sR"] += CP + 2 * NSCC
        cnt["sAT"] += CP
        cnt["sSC"] += CP
        cnt["sOH"] += 2 * NSCC
        cnt["sEV"] += 3 * NSCC
        cnt["stOG"] += 16 * NSCC
        if debug:
            cnt["stDG"] += 16 * (NSCC + 6 * min(CP, 8) + 1)
        em.emit("scalar", c_act)

        def c_vec(v):
            for g in range(NSCC + 1):
                if g < NSCC:
                    sl = g % NPREP
                    v.wait_ge(sems["ldp"], 64 * (g + 1))
                    if g >= NPREP:
                        v.wait_ge(sems["vU"], 2 * (g - NPREP) + 2)
                    v.tensor_scalar(sb["ccol"][:, sl, 4:5], sb["krsb"][:, sl, 128:129], 2.0, None, ALU.mult)
                    v.scalar_tensor_tensor(sb["scr"][:], sb["krsb"][:, sl, 0:128], 1.0,
                                           sb["krsb"][:, sl, 0:128], ALU.mult, ALU.mult)
                    v.reduce_sum(sb["ccol"][:, sl, 2:3], sb["scr"][:],
                                 axis=mybir.AxisListType.X).then_inc(sems["vPR"], 1)
                    v.wait_ge(sems["sPR"], 3 + 4 * g + 2)
                    v.tensor_scalar(sb["kisb"][:, sl, :], sb["krsb"][:, sl, 0:128],
                                    sb["ccol"][:, sl, 0:1], None, ALU.mult)
                    v.tensor_scalar(sb["kesb"][:, sl, :], sb["kisb"][:, sl, :],
                                    sb["ccol"][:, sl, 4:5], None, ALU.mult).then_inc(sems["vPR"], 1)
                    v.tensor_sub(sb["vmk"][:], sb["vsb"][:, sl, :], sb["kisb"][:, sl, :])
                    v.scalar_tensor_tensor(sb["scr"][:], sb["vmk"][:], -1.0, sb["w_bt"][:],
                                           ALU.mult, ALU.mult)
                    v.tensor_add(sb["ccsb"][:, sl, :], sb["scr"][:], sb["wbb_bt"][:])
                    v.tensor_scalar(sb["cc128"][:, sl, :], sb["ccsb"][:, sl, :],
                                    1.0 / 128, None, ALU.mult)
                    v.reduce_sum(sb["ccol"][:, sl, 1:2], sb["cc128"][:, sl, :],
                                 axis=mybir.AxisListType.X).then_inc(sems["vPR"], 1)
                if g >= 1:
                    gg = g - 1
                    sl = gg % NPREP
                    for par in (0, 1):
                        ch = 2 * gg + par
                        if ch >= CP:
                            break
                        mask = CS[:, C_MSKE + par:C_MSKE + par + 1]
                        am0 = C_AME if par == 0 else C_AMO
                        amask = CS[:, am0:am0 + 32]
                        v.wait_ge(sems["sE"], ch + 1)
                        v.scalar_tensor_tensor(sb["redb"][:, 0, :], sb["khsb"][:], 1.0,
                                               sb["khsb"][:], ALU.mult, ALU.mult)
                        v.scalar_tensor_tensor(sb["redb"][:, 1, :], sb["khsb"][:], 1.0,
                                               sb["w2n_bt"][:], ALU.mult, ALU.mult)
                        v.scalar_tensor_tensor(sb["redb"][:, 2, :], sb["redb"][:, 1, :], 1.0,
                                               sb["khsb"][:], ALU.mult, ALU.mult)
                        v.scalar_tensor_tensor(sb["redb"][:, 3, :], sb["khsb"][:], 1.0,
                                               sb["cc128"][:, sl, :], ALU.mult, ALU.mult)
                        v.reduce_sum(sb["kcol"][:, 0:4], sb["redb"][:],
                                     axis=mybir.AxisListType.X).then_inc(sems["vR"], 1)
                        v.wait_ge(sems["sAT"], ch + 1)
                        if ch >= 1:
                            v.wait_ge(sems["tCU"], ch)
                        v.tensor_tensor(sb["atm"][:], sb["ats"][:], amask, ALU.mult).then_inc(sems["vAT"], 1)
                        v.wait_ge(sems["sR"], sr_at_scan[ch])
                        kc = sb["kcol"]
                        v.tensor_scalar(kc[:, 5:6], kc[:, 2:3], -1.0 / 128, None, ALU.mult)
                        v.tensor_scalar(kc[:, 6:7], kc[:, 1:2], -1.0 / 128, None, ALU.mult)
                        v.tensor_tensor(kc[:, 7:8], kc[:, 12:13], mask, ALU.mult)
                        v.tensor_tensor(kc[:, 8:9], kc[:, 13:14], mask, ALU.mult)
                        v.tensor_tensor(kc[:, 9:10], kc[:, 3:4], kc[:, 12:13], ALU.mult)
                        lastd1 = v.tensor_tensor(kc[:, 11:12], kc[:, 12:13],
                                                 sb["ccol"][:, sl, 1:2], ALU.mult)
                        lastd1.then_inc(sems["vCX"], 1)
                        vcx_n[0] += 1
                        v.wait_ge(sems["vCX"], vcx_n[0])
                        v.scalar_tensor_tensor(kc[:, 10:11], kc[:, 5:6], kc[:, 13:14], kc[:, 9:10],
                                               ALU.mult, ALU.add)
                        v.scalar_tensor_tensor(kc[:, 4:5], kc[:, 6:7], kc[:, 13:14], kc[:, 11:12],
                                               ALU.mult, ALU.add).then_inc(sems["vCX"], 1)
                        vcx_n[0] += 1
                        v.wait_ge(sems["vCX"], vcx_n[0])
                        v.tensor_tensor(kc[:, 15:16], kc[:, 4:5], mask, ALU.mult)
                        v.tensor_scalar(kc[:, 14:15], kc[:, 7:8], -1.0, None,
                                        ALU.mult).then_inc(sems["vCX"], 1)
                        vcx_n[0] += 1
                        v.scalar_tensor_tensor(sb["scr"][:], sb["khsb"][:], kc[:, 10:11],
                                               sb["redb"][:, 1, :], ALU.mult, ALU.add)
                        v.wait_ge(sems["vCX"], vcx_n[0])
                        v.tensor_scalar(sb["t1sb"][:], sb["ccsb"][:, sl, :], kc[:, 14:15],
                                        kc[:, 15:16], ALU.mult, ALU.add)
                        if ch >= 1:
                            v.wait_ge(sems["tSU"], ch)
                        v.scalar_tensor_tensor(sb["usb"][:], sb["scr"][:], kc[:, 8:9],
                                               sb["t1sb"][:], ALU.mult, ALU.add).then_inc(sems["vU"], 1)
                        cp = None
                        for p in range(4):
                            cp = v.tensor_copy(sb["ubd"][32 * p:32 * p + 32, 128 * p:128 * (p + 1)],
                                               sb["usb"][32 * p:32 * p + 32, :])
                        cp.then_inc(sems["vBD"], 1)
                    v.wait_ge(sems["sOH"], 2 * gg + 2)
                    v.tensor_sub(sb["scr"][:], sb["ohsb"][:, 0, :], sb["ohsb"][:, 1, :])
                    v.scalar_tensor_tensor(sb["mrg"][:], sb["scr"][:],
                                           CS[:, C_MSKE:C_MSKE + 1], sb["ohsb"][:, 1, :],
                                           ALU.mult, ALU.add)
                    v.scalar_tensor_tensor(sb["scr"][:], sb["mrg"][:], 1.0, sb["mrg"][:],
                                           ALU.mult, ALU.mult)
                    v.reduce_sum(sb["ocol"][:, 0:1], sb["scr"][:],
                                 axis=mybir.AxisListType.X).then_inc(sems["ccx"], 1)
                    v.wait_ge(sems["sR"], sr_at_o[(gg, 1)])
                    v.tensor_scalar(sb["scr"][:], sb["mrg"][:], sb["ocol"][:, 1:2], None,
                                    ALU.mult)
                    v.scalar_tensor_tensor(sb["scr"][:], sb["scr"][:], 1.0, sb["w_bt"][:],
                                           ALU.mult, ALU.mult)
                    v.tensor_add(sb["scr"][:], sb["scr"][:], sb["b_bt"][:])
                    v.tensor_add(sb["scr"][:], sb["scr"][:], sb["qchsb"][:, sl, :])
                    v.tensor_tensor(sb["scr"][:], sb["scr"][:], sb["ssb"][:, sl, :], ALU.mult)
                    v.scalar_tensor_tensor(sb["t1sb"][:], sb["scr"][:], 1.0, sb["scr"][:],
                                           ALU.mult, ALU.mult)
                    v.reduce_sum(sb["ocol"][:, 3:4], sb["t1sb"][:],
                                 axis=mybir.AxisListType.X).then_inc(sems["ccx"], 1)
                    v.wait_ge(sems["sR"], sr_at_o[(gg, 2)])
                    v.tensor_scalar(sb["scr"][:], sb["scr"][:], sb["ocol"][:, 3:4], None,
                                    ALU.mult)
                    if gg >= 1:
                        v.wait_ge(sems["tOG"], gg)
                    v.tensor_tensor(sb["ogsb"][:], sb["scr"][:], sb["nw_bt"][:],
                                    ALU.mult).then_inc(sems["vOG"], 1)
                    v.nop().then_inc(sems["vOG2"], 1)

        cnt["vPR"] += 3 * NSCC
        cnt["vCX"] += 3 * CP
        cnt["vR"] += CP
        cnt["vAT"] += CP
        cnt["vU"] += CP
        cnt["vBD"] += CP
        cnt["ccx"] += 2 * NSCC
        cnt["vOG"] += NSCC
        cnt["vOG2"] += NSCC
        em.emit("vector", c_vec)

    S("xqt_t", (128, 4, KC, 128), BF16)
    S("ogt_t", (128, 2, 4, 128), BF16)
    S("wqt4", (128, KC, 4, 128), BF16)
    S("ycols", (128, 8), F32)
    S("t2col", (128, 8), F32)
    S("outsb", (128, 2, 512), F32)
    S("xqf", (128, D), F32)
    S("xqi", (128, D), I32)
    S("xqd2", (128, D), F32)
    sb["wfq"] = sb["xn_tok"]

    S("wfcol", (128, 64), F32)
    S("wfsc", (128, 8), F32)
    S("wfsb", (128, 144), BF16)

    for s_ in ("ldwos", "ldog", "stRS", "sX2", "tWO", "ldrs", "ldr", "vX2",
               "ldf", "sWO", "tF", "vWO", "ldw2", "vQ2", "vT", "sT", "sT2",
               "vQX", "stX", "ldq", "ldff", "ldf2", "tFF", "vFF", "stO"):
        em.sem(s_)

    x2f = sb["xnt_m"][:].rearrange("p a k f -> p (a k f)")
    wov2 = sb["wq_b"][:].rearrange("p k c -> p (k c)")
    x2p = sb["g1sb"]
    LW2 = 3 if debug else 2
    DX = 2 if debug else 1
    psW = [psQ[0], psQ[1], psKV[0], psKV[1]]

    if has("T"):
        def t_gload(g):
            g.wait_ge(sems["tQ"], cnt["tQ"])
            wsr = WOS.rearrange("(c p) n -> c p n", p=128)
            for fc in range(4):
                g.dma_start(out=wov2[:, 2048 * fc:2048 * (fc + 1)],
                            in_=wsr[fc]).then_inc(sems["ldwos"], 16)

            def wfld(i):
                if i >= 2:
                    g.wait_ge(sems["sWO"], i - 1)
                g.dma_start(out=sb["xqf" if i % 2 == 0 else "xqd2"][:],
                            in_=WF.rearrange("(n p) d -> n p d", p=128)[i]
                            ).then_inc(sems["ldf"], 16)
            if has("F"):
                for i in range(5):
                    wfld(i)
            g.wait_ge(sems["stRS"], 16 * 128)
            g.collective_compute(
                "ReduceScatter", ALU.add,
                replica_groups=[[0, 1, 2, 3], [4, 5, 6, 7]],
                ins=[RSI.rearrange("b q d -> (b q d)").opt()],
                outs=[RSO.rearrange("q d -> (q d)").opt()],
            ).then_inc(sems["ccx"], 1)
            if has("F"):
                for i in range(5, KC):
                    wfld(i)
        cnt["ldwos"] += 64
        cnt["ccx"] += 1
        ccs_done = cnt["ccx"]
        if has("F"):
            cnt["ldf"] += 16 * 16
        em.emit("gpsimd", t_gload)

        def t_sync(sync):
            for tt in range(32):
                if tt >= 2:
                    sync.wait_ge(sems["tWO"], 4 * (tt - 1))
                sync.wait_ge(sems["stOG"], 16 * 4 * (tt + 1))
                sync.dma_start(out=sb["ogt_t"][:, tt % 2, :, :],
                               in_=OGT[tt // 8][:, 128 * (tt % 8):128 * (tt % 8) + 128]
                               .rearrange("(c p) t -> p c t", p=128)
                               ).then_inc(sems["ldog"], 16)
            sync.wait_ge(sems["ccx"], ccs_done)
            rsr = RSO.rearrange("(n p) d -> n p d", p=128)
            xqr = XQ.rearrange("(n p) d -> n p d", p=128)
            for t in range(8):
                sync.dma_start(out=x2f[:, 2048 * t:2048 * (t + 1)],
                               in_=rsr[t]).then_inc(sems["ldrs"], 16)
                if t >= 2:
                    sync.wait_ge(sems["vX2"], t - 1)
                sync.dma_start(out=sb["xtile"][:, t % 2, :],
                               in_=xqr[t]).then_inc(sems["ldr"], 16)
            def ldq1(t):
                sync.wait_ge(sems["stX"], 16 * DX * (t + 1))
                if t >= 4:
                    sync.wait_ge(sems["tFF"], 64)
                sync.dma_start(out=sb["xqt_t"][:, t % 4, :, :],
                               in_=XQD[t].rearrange("p (k f) -> p k f", f=128),
                               transpose=True).then_inc(sems["ldq"], 16)

            def wqt1(fg, a):
                nq = fg % 4
                if fg >= 1 and a == 0:
                    sync.wait_ge(sems["tFF"], 16 * fg)
                sync.wait_ge(sems["ldw2"], 16 * LW2 * (4 * nq + a + 1))
                sync.dma_start(out=sb["wqt4"][:, :, a, :],
                               in_=WQR[4 * nq + a].rearrange("p (k f) -> p k f", f=128),
                               transpose=True).then_inc(sems["ldff"], 16)
            for t in range(4):
                ldq1(t)
            for fg in range(4):
                for a in range(4):
                    wqt1(fg, a)
            for t in range(4, 8):
                ldq1(t)
            for fg in range(4, 8):
                for a in range(4):
                    wqt1(fg, a)
        cnt["ldog"] += 16 * 32
        cnt["ldrs"] += 16 * 8
        cnt["ldr"] += 16 * 8
        cnt["ldq"] += 16 * 8
        cnt["ldff"] += 16 * 4 * 8
        em.emit("sync", t_sync)

        def wo_pe(t):
            t.wait_ge(sems["ldwos"], 64)
            for grp in range(128):
                tt, nq = grp // 4, grp % 4
                if nq == 0:
                    t.wait_ge(sems["ldog"], 16 * (tt + 1))
                if grp >= 4:
                    t.wait_ge(sems["sX2"], grp - 3)
                for fc in range(4):
                    mm = t.matmul(psW[grp % 4][:, 0:512],
                                  sb["ogt_t"][:, tt % 2, fc, :],
                                  wov2[:, 2048 * fc + 512 * nq:2048 * fc + 512 * (nq + 1)],
                                  start=(fc == 0), stop=(fc == 3))
                mm.then_inc(sems["tWO"], 1)
        cnt["tWO"] += 128
        em.emit("tensor", wo_pe)

        def wo_act(s):
            for grp in range(128):
                tt, nq = grp // 4, grp % 4
                s.wait_ge(sems["tWO"], grp + 1)
                if grp >= 2:
                    s.wait_ge(sems["stRS"], 16 * (grp - 1))
                s.activation(x2p[:, grp % 2, :], psW[grp % 4][:, 0:512],
                             AF.Copy).then_inc(sems["sX2"], 1)
                s.dma_start(out=RSI[tt // 8][128 * (tt % 8):128 * (tt % 8) + 128,
                                            512 * nq:512 * (nq + 1)],
                            in_=x2p[:, grp % 2, :]).then_inc(sems["stRS"], 16)
        cnt["sX2"] += 128
        cnt["stRS"] += 16 * 128
        em.emit("scalar", wo_act)

        def t_vec(v):
            for t in range(8):
                v.wait_ge(sems["ldrs"], 16 * (t + 1))
                v.wait_ge(sems["ldr"], 16 * (t + 1))
                v.tensor_tensor(x2f[:, 2048 * t:2048 * (t + 1)],
                                x2f[:, 2048 * t:2048 * (t + 1)],
                                sb["xtile"][:, t % 2, :],
                                ALU.add).then_inc(sems["vX2"], 1)
        cnt["vX2"] += 8
        em.emit("vector", t_vec)

    if has("F"):
        def f1_act(s):
            red512 = sb["redb"][:].rearrange("p a f -> p (a f)")
            for i in range(KC):
                s.wait_ge(sems["ldf"], 16 * (i + 1))
                buf = sb["xqf" if i % 2 == 0 else "xqd2"][:]
                for j in range(4):
                    mm = s.activation(red512, buf[:, 512 * j:512 * (j + 1)], AF.Abs,
                                      accum_out=sb["wfcol"][:, 4 * i + j:4 * i + j + 1])
                mm.then_inc(sems["sWO"], 1)
        cnt["sWO"] += KC
        em.emit("scalar", f1_act)

        def f1_pe(t):
            t.wait_ge(sems["vWO"], 4)
            t.matmul(psGT[1][0:1, 0:1], sb["wfsb"][:, 1:2], sb["wfsb"][:, 0:1],
                     start=True, stop=True).then_inc(sems["tF"], 1)
            t.wait_ge(sems["vWO"], 8)
            t.matmul(psGT[1][:, 1:3], sb["wfsb"][0:1, 8:136], sb["wfsb"][0:1, 2:4],
                     start=True, stop=True).then_inc(sems["tF"], 1)
        cnt["tF"] += 2
        em.emit("tensor", f1_pe)

        def f1_vec(v):
            v.wait_ge(sems["ld"], 64)
            v.tensor_copy(sb["wfsb"][:, 1:2], CS[:, C_ONE:C_ONE + 1]).then_inc(sems["vWO"], 1)
            v.tensor_copy(sb["wfsb"][0:1, 8:136], CS[0:1, 208:336]).then_inc(sems["vWO"], 1)
            v.wait_ge(sems["sWO"], KC)
            v.reduce_sum(sb["wfsc"][:, 0:1], sb["wfcol"][:],
                         axis=mybir.AxisListType.X).then_inc(sems["vWO"], 1)
            v.wait_ge(sems["vWO"], 3)
            v.tensor_copy(sb["wfsb"][:, 0:1], sb["wfsc"][:, 0:1]).then_inc(sems["vWO"], 1)
            v.wait_ge(sems["tF"], 1)
            v.tensor_scalar(sb["wfsc"][0:1, 1:2], psGT[1][0:1, 0:1], 1.0 / (D * D),
                            None, ALU.mult).then_inc(sems["vWO"], 1)
            v.wait_ge(sems["vWO"], 5)
            v.tensor_scalar(sb["wfsc"][0:1, 2:3], sb["wfsc"][0:1, 1:2], 1e-6, None,
                            ALU.max).then_inc(sems["vWO"], 1)
            v.wait_ge(sems["vWO"], 6)
            v.reciprocal(sb["wfsc"][0:1, 3:4], sb["wfsc"][0:1, 2:3]).then_inc(sems["vWO"], 1)
            v.wait_ge(sems["vWO"], 7)
            v.tensor_copy(sb["wfsb"][0:1, 2:4], sb["wfsc"][0:1, 2:4]).then_inc(sems["vWO"], 1)
            v.wait_ge(sems["tF"], 2)
            v.tensor_copy(sb["wfsc"][:, 4:6], psGT[1][:, 1:3]).then_inc(sems["vWO"], 1)
        cnt["vWO"] += 9
        em.emit("vector", f1_vec)

        def f2_g(g):
            g.wait_ge(sems["sWO"], KC)
            g.wait_ge(sems["vX2"], 8)
            for i in range(KC):
                if i >= 2:
                    g.wait_ge(sems["vQ2"], i - 1)
                g.dma_start(out=sb["xtile"][:, i % 2, :],
                            in_=WF.rearrange("(n p) d -> n p d", p=128)[i]).then_inc(sems["ldw2"], 16)
                g.wait_ge(sems["vQ2"], i + 1)
                g.dma_start(out=WQR[i], in_=sb["wfq"][:, i % 2, :]).then_inc(sems["ldw2"], 16)
                if debug:
                    g.dma_start(out=dbg["dbg_wqt"][i], in_=sb["wfq"][:, i % 2, :]).then_inc(sems["ldw2"], 16)
        cnt["ldw2"] += 16 * KC * LW2
        em.emit("gpsimd", f2_g)

        def f2_vec(v):
            v.wait_ge(sems["vWO"], 9)
            for i in range(KC):
                v.wait_ge(sems["ldw2"], 16 * (LW2 * i + 1))
                v.tensor_scalar(sb["xqd2"][:], sb["xtile"][:, i % 2, :],
                                sb["wfsc"][:, 5:6], None, ALU.mult)
                v.tensor_scalar(sb["xqf"][:], sb["xqd2"][:], 0.5, None, ALU.is_gt)
                v.scalar_tensor_tensor(sb["wfq"][:, i % 2, :], sb["xqd2"][:], -0.5,
                                       sb["xqf"][:], ALU.is_lt,
                                       ALU.subtract).then_inc(sems["vQ2"], 1)
        cnt["vQ2"] += KC
        em.emit("vector", f2_vec)

    if has("T"):
        def t2_vec(v):
            v.wait_ge(sems["ldw2"], 16 * KC * LW2)
            for t in range(8):
                x2t = x2f[:, 2048 * t:2048 * (t + 1)]
                sl = t % 2
                for j in range(4):
                    v.bn_stats(sb["bnst"][:, 6 * j:6 * (j + 1)],
                               x2t[:, 512 * j:512 * (j + 1)]).then_inc(sems["vT"], 1)
                v.wait_ge(sems["vT"], 5 * t + 4)
                v.bn_aggr(sb["p0col"][:, sl, 0:2], sb["bnst"][:]).then_inc(sems["vT"], 1)
                v.wait_ge(sems["sT"], 2 * (t + 1))
                v.tensor_scalar(sb["xqf"][:], x2t, sb["p0col"][:, sl, 0:1],
                                sb["p0col"][:, sl, 2:3], ALU.subtract,
                                ALU.mult).then_inc(sems["vQX"], 1)
                v.wait_ge(sems["sT2"], t + 1)
                v.reduce_max(sb["t2col"][:, t:t + 1], sb["xqd2"][:],
                             axis=mybir.AxisListType.X).then_inc(sems["vQX"], 1)
                v.wait_ge(sems["vQX"], 12 * t + 2)
                v.tensor_scalar(sb["t2col"][:, t:t + 1], sb["t2col"][:, t:t + 1],
                                1e-8, None, ALU.max).then_inc(sems["vQX"], 1)
                v.wait_ge(sems["vQX"], 12 * t + 3)
                v.reciprocal(sb["p0col"][:, sl, 3:4],
                             sb["t2col"][:, t:t + 1]).then_inc(sems["vQX"], 1)
                v.wait_ge(sems["vQX"], 12 * t + 4)
                v.tensor_scalar(sb["ycols"][:, t:t + 1], sb["t2col"][:, t:t + 1],
                                sb["wfsc"][:, 4:5], 1.0 / 127, ALU.mult,
                                ALU.mult).then_inc(sems["vQX"], 1)
                v.tensor_scalar(sb["xtile"][:, 0, :], sb["xqf"][:],
                                sb["p0col"][:, sl, 3:4], 127.0, ALU.mult,
                                ALU.mult).then_inc(sems["vQX"], 1)
                v.tensor_copy(sb["xqi"][:], sb["xtile"][:, 0, :]).then_inc(sems["vQX"], 1)
                v.tensor_copy(sb["xqd2"][:], sb["xqi"][:]).then_inc(sems["vQX"], 1)
                v.tensor_tensor(sb["xtile"][:, 0, :], sb["xtile"][:, 0, :],
                                sb["xqd2"][:], ALU.subtract).then_inc(sems["vQX"], 1)
                v.scalar_tensor_tensor(sb["xtile"][:, 1, :], sb["xtile"][:, 0, :], 0.5,
                                       sb["xqd2"][:], ALU.is_ge,
                                       ALU.add).then_inc(sems["vQX"], 1)
                v.scalar_tensor_tensor(sb["xqf"][:], sb["xtile"][:, 0, :], -0.5,
                                       sb["xtile"][:, 1, :], ALU.is_le,
                                       ALU.subtract).then_inc(sems["vQX"], 1)
                if t >= 2:
                    v.wait_ge(sems["stX"], 16 * DX * (t - 1))
                v.tensor_scalar(sb["wfq"][:, sl, :], sb["xqf"][:], 127.0, -127.0,
                                ALU.min, ALU.max).then_inc(sems["vQX"], 1)
        cnt["vT"] += 5 * 8
        cnt["vQX"] += 12 * 8
        em.emit("vector", t2_vec)

        def t2_act(s):
            for t in range(8):
                sl = t % 2
                s.wait_ge(sems["vT"], 5 * (t + 1))
                s.activation(sb["p0col"][:, sl, 3:4], sb["p0col"][:, sl, 1:2], AF.Ln,
                             bias=CS[:, C_EPSL:C_EPSL + 1]).then_inc(sems["sT"], 1)
                s.wait_ge(sems["sT"], 2 * t + 1)
                s.activation(sb["p0col"][:, sl, 2:3], sb["p0col"][:, sl, 3:4], AF.Exp,
                             scale=-0.5).then_inc(sems["sT"], 1)
                s.wait_ge(sems["vQX"], 12 * t + 1)
                s.activation(sb["xqd2"][:], sb["xqf"][:], AF.Abs).then_inc(sems["sT2"], 1)
                s.wait_ge(sems["vQX"], 12 * (t + 1))
                s.dma_start(out=XQD[t], in_=sb["wfq"][:, sl, :]).then_inc(sems["stX"], 16)
                if debug:
                    s.dma_start(out=dbg["dbg_xq2"][t], in_=sb["wfq"][:, sl, :]).then_inc(sems["stX"], 16)
        cnt["sT"] += 2 * 8
        cnt["sT2"] += 8
        cnt["stX"] += 16 * 8 * DX
        em.emit("scalar", t2_act)

        def t3_pe(t):
            t.wait_ge(sems["sX2"], 128)
            for idx2 in range(128):
                th, nq, k = idx2 // 64, (idx2 // 16) % 4, idx2 % 16
                fg = idx2 // 16
                if k == 0:
                    t.wait_ge(sems["ldff"], 16 * 4 * (fg + 1))
                    t.wait_ge(sems["ldq"], 16 * 4 * (th + 1))
                    if fg >= 1:
                        t.wait_ge(sems["vFF"], 4 * fg)
                for ti in range(4):
                    mm = t.matmul(psW[ti][:, 0:512],
                                  sb["xqt_t"][:, ti, k, :],
                                  sb["wqt4"][:, k, :, :].rearrange("p a f -> p (a f)"),
                                  start=(k == 0), stop=(k == KC - 1))
                mm.then_inc(sems["tFF"], 1)
        cnt["tFF"] += 128
        em.emit("tensor", t3_pe)

        def t3_vec(v):
            for fg in range(8):
                th, nq = fg // 4, fg % 4
                v.wait_ge(sems["tFF"], 16 * (fg + 1))
                v.wait_ge(sems["vQX"], 12 * 4 * (th + 1))
                for ti in range(4):
                    tile = 4 * th + ti
                    j = 4 * fg + ti
                    if j >= 2:
                        v.wait_ge(sems["stO"], 16 * (j - 1))
                    v.scalar_tensor_tensor(
                        sb["outsb"][:, j % 2, :], psW[ti][:, 0:512],
                        sb["ycols"][:, tile:tile + 1],
                        x2f[:, 2048 * tile + 512 * nq:2048 * tile + 512 * (nq + 1)],
                        ALU.mult, ALU.add).then_inc(sems["vFF"], 1)
        cnt["vFF"] += 32
        em.emit("vector", t3_vec)

        def t3_st(s):
            for j in range(32):
                fg, ti = j // 4, j % 4
                th, nq = fg // 4, fg % 4
                tile = 4 * th + ti
                s.wait_ge(sems["vFF"], j + 1)
                s.dma_start(out=OUT[128 * tile:128 * (tile + 1), 512 * nq:512 * (nq + 1)],
                            in_=sb["outsb"][:, j % 2, :]).then_inc(sems["stO"], 16)
            if debug:
                s.dma_start(out=dbg["dbg_x2"], in_=x2f).then_inc(sems["stDG"], 16)
                s.dma_start(out=dbg["dbg_tc"][:, 0:8], in_=sb["t2col"][:]).then_inc(sems["stDG"], 16)
                s.dma_start(out=dbg["dbg_tc"][:, 8:16], in_=sb["ycols"][:]).then_inc(sems["stDG"], 16)
        cnt["stO"] += 16 * 32
        if debug:
            cnt["stDG"] += 16 * 3
        em.emit("scalar", t3_st)

    finals = {k: v for k, v in cnt.items()}

    def final_sync(sync):
        for k, v in finals.items():
            if v > 0:
                sync.wait_ge(sems[k], v)
    em.emit("sync", final_sync)

    em.build()
    ctx.close()
    return nc



def build_consts():
    c = np.zeros((128, 640), np.float32)
    c[:, C_ONE] = 1.0
    c[:, C_I128] = 1.0 / 128
    c[:, C_EPSI] = EPS_IN
    c[:, C_EPSL] = EPS_LN
    c[:, C_HALF] = 0.5
    c[:, C_NEG1] = -1.0
    c[:, C_127] = 127.0
    t = np.arange(128) % 32
    c[:, C_MSKE] = (t < 16).astype(np.float32)
    c[:, C_MSKO] = (t >= 16).astype(np.float32)
    for ci in range(32):
        even = (t < 16) & (ci < 16) & (t <= ci)
        odd = (t >= 16) & (ci >= 16) & ((t - 16) <= (ci - 16))
        c[:, C_AME + ci] = even.astype(np.float32)
        c[:, C_AMO + ci] = odd.astype(np.float32)
    c[:, 208:336] = 1.0
    m = np.arange(128)
    for j in range(4):
        c[j, 80:208] = (m // 32 == j).astype(np.float32)
    return c


def host_prep(inputs):
    x = np.asarray(inputs["L0_attn_x"])
    wq = np.asarray(inputs["L0_attn_Wq"])
    wk = np.asarray(inputs["L0_attn_Wk"])
    wv = np.asarray(inputs["L0_attn_Wv"])
    wo = np.asarray(inputs["L0_attn_Wo"])
    wbeta = np.asarray(inputs["L0_attn_Wbeta"])
    lnw = np.asarray(inputs["L0_attn_ln_w"])
    lnb = np.asarray(inputs["L0_attn_ln_b"])
    wg1 = np.asarray(inputs["L0_attn_Wg1"])
    wg2 = np.asarray(inputs["L0_attn_Wg2"])
    nrm = np.asarray(inputs["L0_attn_norm_w"])
    wf = np.asarray(inputs["L0_ffn_weight"])
    consts = build_consts()

    in_maps = []
    for c in range(8):
        b = c // 4
        hg = c % 4
        hsl = slice(4 * hg * Dh, 4 * (hg + 1) * Dh)
        heads = [4 * hg + j for j in range(HPC)]
        wkvb = np.concatenate(
            [wk[:, hsl], wv[:, hsl], wbeta[:, 4 * hg:4 * hg + 4]], axis=1)
        m = {
            "x": np.ascontiguousarray(x[b]),
            "xq": np.ascontiguousarray(x[b, QTOK * hg:QTOK * (hg + 1)]),
            "wq": np.ascontiguousarray(wq[:, hsl]),
            "wkvb": np.ascontiguousarray(wkvb),
            "wg1": np.ascontiguousarray(wg1),
            "wg2": np.ascontiguousarray(wg2[:, hsl]),
            "wos": np.ascontiguousarray(wo[hsl, :]),
            "wf": np.ascontiguousarray(wf),
            "lnw": np.ascontiguousarray(lnw[heads]),
            "lnb": np.ascontiguousarray(lnb[heads]),
            "nrmw": np.ascontiguousarray(nrm[hsl]).reshape(HPC, Dh),
            "consts": consts,
        }
        in_maps.append(m)
    return in_maps


def kernel(**inputs):
    in_maps = host_prep(inputs)
    nc = build_program(debug=DEBUG)
    res = run_bass_kernel_spmd(nc, in_maps, core_ids=list(range(8)))
    out = np.zeros((B, T, D), np.float32)
    for ci in range(8):
        b = ci // 4
        hg = ci % 4
        out[b, QTOK * hg:QTOK * (hg + 1)] = res.results[ci]["out"]
    return out


# version 10 (final)
# speedup vs baseline: 1.1892x; 1.0009x over previous
import numpy as np
import ml_dtypes
import concourse.bass as bass
import concourse.mybir as mybir
from concourse.bass_utils import run_bass_kernel_spmd
from contextlib import ExitStack

F32 = mybir.dt.float32
BF16 = mybir.dt.bfloat16
I32 = mybir.dt.int32
AF = mybir.ActivationFunctionType
ALU = mybir.AluOpType

B, T, D, H = 2, 4096, 2048, 16
HPC = 4
Dh = 128
SC = 32
NSC = T // SC
CH = 16
NT = T // 128
KC = D // 128
QTOK = T // 4
EPS_LN = 1e-5
EPS_IN = 1e-6
DEBUG = False
import os
PARTS = set(os.environ.get("KB_PARTS", "all").split(","))
def has(p):
    return "all" in PARTS or p in PARTS

C_ONE, C_I128, C_EPSI, C_EPSL, C_HALF, C_ZERO, C_NEG1, C_127 = range(8)
C_MSKE, C_MSKO = 8, 9
C_AME, C_AMO = 16, 48


class Emitter:

    def __init__(self, nc, ctx):
        self.nc = nc
        self.ops = {k: [] for k in ("sync", "vector", "scalar", "tensor", "gpsimd")}
        self.sems = {}
        self.cnt = {}
        self.ctx = ctx

    def sem(self, name):
        if name not in self.sems:
            self.sems[name] = self.ctx.enter_context(self.nc.semaphore(name))
            self.cnt[name] = 0
        return self.sems[name]

    def emit(self, eng, fn):
        self.ops[eng].append(fn)

    def inc(self, name, n=1):
        self.sem(name)
        self.cnt[name] += n
        return self.cnt[name]

    def build(self):
        blk_ctx = self.nc.Block()
        block = blk_ctx.__enter__()

        def run(eng_name):
            def body(eng):
                for fn in self.ops[eng_name]:
                    fn(eng)
            return body

        block.sync(run("sync"))
        block.scalar(run("scalar"))
        block.vector(run("vector"))
        block.tensor(run("tensor"))
        block.gpsimd(run("gpsimd"))
        blk_ctx.__exit__(None, None, None)


def build_program(debug=False):
    nc = bass.Bass("TRN2", target_bir_lowering=False, detect_race_conditions=False)
    ctx = ExitStack()
    em = Emitter(nc, ctx)
    sems, cnt = em.sems, em.cnt

    dram = {}

    def din(name, shape, dtype=F32):
        dram[name] = nc.dram_tensor(name, list(shape), dtype, kind="ExternalInput").ap()
        return dram[name]

    def dout(name, shape, dtype=F32):
        dram[name] = nc.dram_tensor(name, list(shape), dtype, kind="ExternalOutput").ap()
        return dram[name]

    def dint(name, shape, dtype=BF16):
        dram[name] = nc.dram_tensor(name, list(shape), dtype, kind="Internal").ap()
        return dram[name]

    X = din("x", (T, D))
    XQ = din("xq", (QTOK, D))
    WQ = din("wq", (D, 512))
    WKVB = din("wkvb", (D, 4 * 257))
    WG1 = din("wg1", (D, 128))
    WG2 = din("wg2", (128, 512))
    WOS = din("wos", (512, D))
    WF = din("wf", (D, D))
    LNW = din("lnw", (HPC, Dh))
    LNB = din("lnb", (HPC, Dh))
    NRMW = din("nrmw", (HPC, Dh))
    CONSTS = din("consts", (128, 640))
    OUT = dout("out", (QTOK, D), F32)

    XN = dint("xnb", (NT, 128, D), BF16)
    QTD = dint("qtd", (4, 128, T), BF16)
    KT = dint("kt", (4, T, 128), BF16)
    ETJ = dint("etj", (4, T), BF16)
    VT = dint("vt4", (4, T, 128), BF16)
    ST = dint("st4", (4, T, 128), BF16)
    WQR = dint("wqraw", (KC, 128, D), BF16)
    OGT = dint("ogt", (4, 512, QTOK), BF16)
    RSI = dint("rsi", (4, QTOK, D), BF16)
    RSO = dint("rso", (QTOK, D), BF16)
    XQD = dint("xqd", (8, 128, D), BF16)

    dbg = {}
    if debug:
        for nm, src in (("dbg_xnt", XN), ("dbg_qtd", QTD), ("dbg_wqt", WQR),
                        ("dbg_ogt", OGT), ("dbg_xq2", XQD)):
            dbg[nm] = dout(nm, src.shape, BF16)
        dbg["dbg_x2"] = dout("dbg_x2", (128, 16384), BF16)
        dbg["dbg_tc"] = dout("dbg_tc", (128, 16), F32)

    sb = {}

    def S(name, shape, dtype):
        sb[name] = ctx.enter_context(nc.sbuf_tensor(name, list(shape), dtype))
        return sb[name]

    S("wq_b", (128, KC, 512), BF16)
    S("wkvb_b", (128, KC, 1028), BF16)
    S("wg1_b", (128, KC, 128), BF16)
    S("wg2_b", (128, 512), BF16)
    S("lnw_s", (HPC, Dh), F32)
    S("lnb_s", (HPC, Dh), F32)
    S("nrmw_s", (HPC, Dh), F32)
    CS = S("consts_s", (128, 640), F32)

    S("xtile", (128, 2, D), F32)
    S("bnst", (128, 24), F32)
    S("p0col", (128, 2, 4), F32)
    S("xn_tok", (128, 2, D), BF16)

    def p0_gload(g):
        g.dma_start(out=sb["wq_b"][:], in_=WQ.rearrange("(k p) c -> p k c", p=128)).then_inc(sems["ldw"], 16)
        g.dma_start(out=sb["wkvb_b"][:], in_=WKVB.rearrange("(k p) c -> p k c", p=128)).then_inc(sems["ldw"], 16)
        g.dma_start(out=sb["wg1_b"][:], in_=WG1.rearrange("(k p) c -> p k c", p=128)).then_inc(sems["ldw"], 16)
        g.dma_start(out=sb["wg2_b"][:], in_=WG2).then_inc(sems["ldw"], 16)
    em.sem("ldw")
    if has("wload"):
        cnt["ldw"] += 64
        em.emit("gpsimd", p0_gload)

    def p0_small(sync):
        sync.dma_start(out=sb["lnw_s"][:], in_=LNW).then_inc(sems["ld"], 16)
        sync.dma_start(out=sb["lnb_s"][:], in_=LNB).then_inc(sems["ld"], 16)
        sync.dma_start(out=sb["nrmw_s"][:], in_=NRMW).then_inc(sems["ld"], 16)
        sync.dma_start(out=sb["consts_s"][:], in_=CONSTS).then_inc(sems["ld"], 16)
    em.sem("ld"); cnt["ld"] += 64
    em.emit("sync", p0_small)

    em.sem("ldx"); em.sem("vA"); em.sem("sA"); em.sem("xs"); em.sem("stq")

    def p0_x_loads(sync):
        for i in range(int(os.environ.get("KB_NTX", NT))):
            if i >= 2 and has("xvec"):
                sync.wait_ge(sems["vA"], 2 * (i - 1))
            sync.dma_start(out=sb["xtile"][:, i % 2, :],
                           in_=X.rearrange("(n p) d -> n p d", p=128)[i]).then_inc(sems["ldx"], 16)
    if has("xln"):
        cnt["ldx"] += 16 * NT
        em.emit("sync", p0_x_loads)

    def p0_x_vec(v):
        v.wait_ge(sems["ld"], 64)
        for i in range(NT):
            sl = i % 2
            v.wait_ge(sems["ldx"], 16 * (i + 1))
            for j in range(4):
                v.bn_stats(sb["bnst"][:, 6 * j:6 * (j + 1)],
                           sb["xtile"][:, sl, 512 * j:512 * (j + 1)]).then_inc(sems["vB"], 1)
            v.wait_ge(sems["vB"], 4 * (i + 1))
            v.bn_aggr(sb["p0col"][:, sl, 0:2], sb["bnst"][:]).then_inc(sems["vA"], 1)
            if has("xact"):
                v.wait_ge(sems["sA"], i + 1)
                if i >= 2:
                    v.wait_ge(sems["stq"], 16 * (i - 1) * (2 if debug else 1))
            v.tensor_scalar(sb["xn_tok"][:, sl, :], sb["xtile"][:, sl, :],
                            sb["p0col"][:, sl, 0:1], sb["p0col"][:, sl, 2:3],
                            ALU.subtract, ALU.mult).then_inc(sems["vA"], 1)
    if has("xln") and has("xvec"):
        cnt["vA"] += 2 * NT
        em.emit("vector", p0_x_vec)

    def p0_x_act(s):
        s.wait_ge(sems["ld"], 64)
        for i in range(NT):
            sl = i % 2
            s.wait_ge(sems["vA"], 2 * i + 1)
            s.activation(sb["p0col"][:, sl, 3:4], sb["p0col"][:, sl, 1:2], AF.Ln,
                         bias=CS[:, C_EPSL:C_EPSL + 1]).then_inc(sems["sB"], 1)
            s.wait_ge(sems["sB"], i + 1)
            s.activation(sb["p0col"][:, sl, 2:3], sb["p0col"][:, sl, 3:4], AF.Exp,
                         scale=-0.5).then_inc(sems["sA"], 1)
            s.wait_ge(sems["vA"], 2 * i + 2)
            s.dma_start(out=XN[i], in_=sb["xn_tok"][:, sl, :]).then_inc(sems["stq"], 16)
            if debug:
                s.dma_start(out=dbg["dbg_xnt"][i], in_=sb["xn_tok"][:, sl, :]).then_inc(sems["stq"], 16)

    em.sem("sB"); em.sem("vB")
    if has("xln") and has("xvec"):
        cnt["vB"] += 4 * NT
    if has("xln") and has("xact"):
        cnt["sB"] += NT
        cnt["sA"] += NT
        cnt["stq"] += 16 * NT * (2 if debug else 1)
        em.emit("scalar", p0_x_act)




    SCM = 16
    NM = 8
    S("xnt_m", (128, 2, KC, 512), BF16)
    S("sigq", (128, 512), F32)
    S("qsil", (128, 512), BF16)
    S("g1sb", (128, 2, 512), BF16)
    S("ktok", (128, 2, 512), BF16)
    S("vtok", (128, 2, 512), BF16)
    S("sgtok", (128, 2, 512), BF16)
    S("etst", (128, 16), BF16)
    psQ = [ctx.enter_context(nc.psum_tensor(f"psQ{i}", [128, 512], F32)) for i in range(2)]
    psKV = [ctx.enter_context(nc.psum_tensor(f"psKV{i}", [128, 512], F32)) for i in range(2)]
    psG = ctx.enter_context(nc.psum_tensor("psG", [128, 512], F32))
    psGT = [ctx.enter_context(nc.psum_tensor(f"psGT{i}", [128, 512], F32)) for i in range(2)]

    for s_ in ("ldB", "tQ", "tKV", "tG", "tGT", "tB", "sQ1", "vQ", "sKV1",
               "vKV", "sG1", "sGT1", "stB", "stG", "stq2", "stE"):
        em.sem(s_)

    if has("B"):
        def b_loads(sync):
            for m in range(NM):
                for j in range(4):
                    i = 4 * m + j
                    if m >= 2 and has("Bpe") and has("Bsc"):
                        sync.wait_ge(sems["tKV"], 4 * (m - 1))
                    sync.wait_ge(sems["stq"], 16 * (i + 1) * (2 if debug else 1))
                    sync.dma_start(out=sb["xnt_m"][:, m % 2, :, 128 * j:128 * (j + 1)],
                                   in_=XN[i].rearrange("p (k f) -> p k f", k=KC),
                                   transpose=True).then_inc(sems["ldB"], 16)
        cnt["ldB"] += 16 * 4 * NM
        em.emit("sync", b_loads)
    if has("B") and has("Bpe"):

        def b_pe(t):
            t.wait_ge(sems["ldw"], 64)
            t.wait_ge(sems["gInit"], 1)

            def sg_mm(j):
                m2 = j // 4
                t.wait_ge(sems["sG1"], m2 + 1)
                t.wait_ge(sems["sKV1"], j + 1)
                t.matmul(psGT[j % 2][:, 0:512],
                         sb["g1sb"][:, m2 % 2, 128 * (j % 4):128 * (j % 4) + 128],
                         sb["wg2_b"][:], start=True, stop=True).then_inc(sems["tGT"], 1)
            for m in range(NM):
                ms = m % 2
                t.wait_ge(sems["ldB"], 16 * 4 * (m + 1))
                xm = sb["xnt_m"][:, ms, :, :]
                for mch in range(4):
                    if 4 * m + mch >= 2 and has("Bsc"):
                        t.wait_ge(sems["vQ"], 4 * m + mch - 1)
                    mm = None
                    for k in range(KC):
                        mm = t.matmul(psQ[mch % 2][:],
                                      sb["wq_b"][:, k, 128 * mch:128 * (mch + 1)],
                                      xm[:, k, :], start=(k == 0), stop=(k == KC - 1))
                    mm.then_inc(sems["tQ"], 1)
                if m >= 2 and has("Bsc"):
                    t.wait_ge(sems["sG1"], m - 1)
                if m >= 1 and has("Bsc"):
                    t.wait_ge(sems["vKV"], 2 * m - 1)
                mm = None
                for k in range(KC):
                    mm = t.matmul(psG[:], sb["wg1_b"][:, k, :], xm[:, k, :],
                                  start=(k == 0), stop=(k == KC - 1))
                mm.then_inc(sems["tG"], 1)
                if has("Bsc"):
                    t.wait_ge(sems["sG1"], m + 1)
                mm = None
                for mc in range(4):
                    for k in range(KC):
                        mm = t.matmul(psG[:, 4 * mc:4 * mc + 4],
                                      xm[:, k, 128 * mc:128 * (mc + 1)],
                                      sb["wkvb_b"][:, k, 1024:1028],
                                      start=(k == 0), stop=(k == KC - 1))
                mm.then_inc(sems["tG"], 1)
                t.wait_ge(sems["vKV"], 2 * m + 1)
                t.transpose(psB[0:16, :], sb["etst"][:], sb["ident_t"][:]).then_inc(sems["tB"], 1)
                for mc in range(4):
                    idx = 4 * m + mc
                    if idx >= 2:
                        sg_mm(idx - 2)
                    if idx >= 2 and has("Bsc"):
                        t.wait_ge(sems["sKV1"], idx - 1)
                        t.wait_ge(sems["sGT1"], idx - 1)
                    mm = None
                    for k in range(KC):
                        mm = t.matmul(psKV[mc % 2][:, 0:512],
                                      xm[:, k, 128 * mc:128 * (mc + 1)],
                                      sb["wkvb_b"][:, k, 0:512],
                                      start=(k == 0), stop=(k == KC - 1))
                    for k in range(KC):
                        mm = t.matmul(psGT[mc % 2][:, 0:512],
                                      xm[:, k, 128 * mc:128 * (mc + 1)],
                                      sb["wkvb_b"][:, k, 512:1024],
                                      start=(k == 0), stop=(k == KC - 1))
                    mm.then_inc(sems["tKV"], 1)
            sg_mm(4 * NM - 2)
            sg_mm(4 * NM - 1)
        if has("Bpe"):
            cnt["tQ"] += 4 * NM
            cnt["tKV"] += 4 * NM
            cnt["tG"] += 2 * NM
            cnt["tGT"] += 4 * NM
            em.emit("tensor", b_pe)

        def b_scalar(s):
            for m in range(NM):
                ms = m % 2
                for mch in range(4):
                    s.wait_ge(sems["tQ"], 4 * m + mch + 1)
                    s.activation(sb["sigq"][:], psQ[mch % 2][:], AF.Sigmoid).then_inc(sems["sQ1"], 1)
                    s.wait_ge(sems["vQ"], 4 * m + mch + 1)
                    s.dma_start(out=QTD[mch][:, 512 * m:512 * (m + 1)],
                                in_=sb["qsil"][:]).then_inc(sems["stq2"], 16)
                    if debug:
                        s.dma_start(out=dbg["dbg_qtd"][mch][:, 512 * m:512 * (m + 1)],
                                    in_=sb["qsil"][:]).then_inc(sems["stq2"], 16)
                    s.wait_ge(sems["stq2"], (32 if debug else 16) * (4 * m + mch + 1))
                s.wait_ge(sems["tG"], 2 * m + 1)
                s.activation(sb["g1sb"][:, ms, :], psG[:], AF.Copy).then_inc(sems["sG1"], 1)
                s.wait_ge(sems["tG"], 2 * m + 2)
                s.activation(sb["etst"][:], psG[:, 0:16], AF.Copy).then_inc(sems["vKV"], 1)
                s.wait_ge(sems["tB"], m + 1)
                if m >= 1:
                    s.wait_ge(sems["stE"], 16 * m)
                s.activation(sb["ogts"][0:16, :], psB[0:16, :], AF.Sigmoid).then_inc(sems["vKV"], 1)
                s.dma_start(out=ETJ[:, 512 * m:512 * (m + 1)].rearrange("j (mc t) -> mc j t", t=128),
                            in_=sb["ogts"][0:16, :]).then_inc(sems["stE"], 16)

                def sg_ev(j):
                    s.wait_ge(sems["tGT"], j + 1)
                    if j >= 2:
                        s.wait_ge(sems["stG"], 16 * (j - 1))
                    s.activation(sb["sgtok"][:, j % 2, :], psGT[j % 2][:, 0:512],
                                 AF.Sigmoid).then_inc(sems["sGT1"], 1)
                    s.dma_start(out=ST[:, 128 * j:128 * (j + 1), :].rearrange("j t f -> t j f"),
                                in_=sb["sgtok"][:, j % 2, :].rearrange("t (j f) -> t j f", f=128)
                                ).then_inc(sems["stG"], 16)
                for mc in range(4):
                    idx = 4 * m + mc
                    if idx >= 2:
                        sg_ev(idx - 2)
                    s.wait_ge(sems["tKV"], idx + 1)
                    if idx >= 2:
                        s.wait_ge(sems["stB"], 32 * (idx - 1))
                    s.activation(sb["ktok"][:, mc % 2, :], psKV[mc % 2][:, 0:512],
                                 AF.Copy)
                    s.activation(sb["vtok"][:, mc % 2, :], psGT[mc % 2][:, 0:512],
                                 AF.Copy).then_inc(sems["sKV1"], 1)
                    s.dma_start(out=KT[:, 128 * idx:128 * (idx + 1), :].rearrange("j t f -> t j f"),
                                in_=sb["ktok"][:, mc % 2, :].rearrange("t (j f) -> t j f", f=128)
                                ).then_inc(sems["stB"], 16)
                    s.dma_start(out=VT[:, 128 * idx:128 * (idx + 1), :].rearrange("j t f -> t j f"),
                                in_=sb["vtok"][:, mc % 2, :].rearrange("t (j f) -> t j f", f=128)
                                ).then_inc(sems["stB"], 16)
            sg_ev(4 * NM - 2)
            sg_ev(4 * NM - 1)
        if has("Bpe") and has("Bsc"):
            cnt["sQ1"] += 4 * NM
            cnt["stq2"] += (32 if debug else 16) * 4 * NM
            cnt["sG1"] += NM
            cnt["vKV"] += 2 * NM
            cnt["tB"] += NM
            cnt["stE"] += 16 * NM
            cnt["sKV1"] += 4 * NM
            cnt["sGT1"] += 4 * NM
            cnt["stB"] += 32 * 4 * NM
            cnt["stG"] += 16 * 4 * NM
            em.emit("scalar", b_scalar)

        def b_vec(v):
            for m in range(NM):
                for mch in range(4):
                    v.wait_ge(sems["sQ1"], 4 * m + mch + 1)
                    v.tensor_tensor(sb["qsil"][:], psQ[mch % 2][:], sb["sigq"][:],
                                    ALU.mult).then_inc(sems["vQ"], 1)
        if has("Bpe") and has("Bsc"):
            cnt["vQ"] += 4 * NM
            em.emit("vector", b_vec)

    NPREP = 3
    S("krsb", (128, NPREP, 132), BF16)
    S("vsb", (128, NPREP, 128), BF16)
    S("ssb", (128, NPREP, 128), BF16)
    S("qisb", (128, NPREP, 4, 32), BF16)
    S("kisb", (128, NPREP, 128), BF16)
    S("kesb", (128, NPREP, 128), BF16)
    S("kitsb", (128, NPREP, 128), BF16)
    S("qchsb", (128, NPREP, 128), BF16)
    S("ccsb", (128, NPREP, 128), F32)
    S("cc128", (128, NPREP, 128), F32)
    S("vmk", (128, 128), BF16)
    S("ccol", (128, NPREP, 8), F32)
    S("S_sb", (128, 2, 4, 128), BF16)
    S("khsb", (128, 128), BF16)
    S("t1sb", (128, 128), F32)
    S("redb", (128, 4, 128), F32)
    S("scr", (128, 128), F32)
    S("ats", (128, 32), F32)
    S("usb", (128, 128), BF16)
    S("ubd", (128, 512), BF16)
    S("atm", (128, 32), BF16)
    S("kcol", (128, 16), F32)
    S("ohsb", (128, 2, 128), BF16)
    S("mrg", (128, 128), BF16)
    S("ogsb", (128, 128), BF16)
    S("ogts", (128, 128), BF16)
    S("ocol", (128, 4), F32)
    S("w_bt", (128, 128), F32)
    S("w2n_bt", (128, 128), F32)
    S("b_bt", (128, 128), F32)
    S("wbb_bt", (128, 128), F32)
    S("nw_bt", (128, 128), F32)
    S("mskE_t", (128, 128), BF16)
    S("ident_t", (128, 128), BF16)
    S("ones_t", (128, 128), F32)

    psS = psQ[0]
    psCH = psGT[0]
    psA = psGT[1]
    psQHb = psKV
    psB = ctx.enter_context(nc.psum_tensor("psB", [128, 128], BF16))

    for s_ in ("vInit", "gInit", "ldp", "vPR", "sPR", "tPR", "tKH", "tA",
               "tCU", "tSU", "sE", "vR", "sR", "sCB", "vU", "vBD", "sSC",
               "sAT", "vAT", "sOH", "vOG", "vOG2", "tOG", "sEV", "stOG", "stDG", "vCX",
               "ccx", "ldo", "tWO", "sWO", "vWO", "stO", "ldw2", "tF", "ldf"):
        em.sem(s_)

    CP = int(os.environ.get("KB_NCHUNK", 2 * NSC))
    vcx_n = [0]
    NSCC = (CP + 1) // 2

    if has("C"):
        if debug:
            DBG_S = dout("dbg_S", (128, 512), BF16)
            DBG_KH = dout("dbg_kh", (8, 128, 128), BF16)
            DBG_U = dout("dbg_u", (8, 128, 128), BF16)
            DBG_KC = dout("dbg_kc", (8, 128, 16), F32)
            DBG_CC = dout("dbg_cc", (8, 128, 128), F32)
            DBG_CL = dout("dbg_cl", (8, 128, 8), F32)
            DBG_SS = dout("dbg_ss", (8, 128, 512), BF16)

        def c_init_g(g):
            g.memset(sb["ident_t"][:], 0.0)
            g.affine_select(out=sb["ident_t"][:], in_=sb["ident_t"][:],
                            compare_op=ALU.not_equal, fill=1.0, base=0,
                            pattern=[[-1, 128]], channel_multiplier=1)
            g.tensor_copy(sb["scr"][:, 0:1], sb["ident_t"][:, 0:1]).then_inc(sems["gInit"], 1)
        cnt["gInit"] += 1
        em.emit("gpsimd", c_init_g)

        def c_init_vec(v):
            v.wait_ge(sems["sPR"], 3)
            v.memset(sb["S_sb"][:], 0.0)
            v.memset(sb["ubd"][:], 0.0)
            v.memset(sb["ones_t"][:], 1.0)
            v.scalar_tensor_tensor(sb["w2n_bt"][:], sb["w_bt"][:], -1.0, sb["w_bt"][:],
                                   ALU.mult, ALU.mult).then_inc(sems["vInit"], 1)
            v.wait_ge(sems["vInit"], 1)
            v.tensor_tensor(sb["wbb_bt"][:], sb["w_bt"][:], sb["b_bt"][:], ALU.mult)
            v.tensor_scalar(sb["mskE_t"][:], sb["ones_t"][:], CS[:, C_MSKE:C_MSKE + 1],
                            None, ALU.mult).then_inc(sems["vInit"], 1)
        cnt["vInit"] += 2
        em.emit("vector", c_init_vec)

        def c_loads(sync):
            for g in range(NSCC):
                sl = g % NPREP
                if g >= NPREP:
                    sync.wait_ge(sems["tSU"], 2 * (g - NPREP) + 2)
                    sync.wait_ge(sems["vOG2"], g - NPREP + 1)
                sync.wait_ge(sems["stB"], 32 * (g // 4 + 1))
                sync.wait_ge(sems["stG"], 16 * (g // 4 + 1))
                sync.wait_ge(sems["stE"], 16 * (g // 16 + 1))
                sync.dma_start(out=sb["krsb"][:, sl, 0:128],
                               in_=KT[:, 32 * g:32 * (g + 1), :]).then_inc(sems["ldp"], 16)
                sync.dma_start(out=sb["krsb"][:, sl, 128:129],
                               in_=ETJ[:, 32 * g:32 * (g + 1)]).then_inc(sems["ldp"], 16)
                sync.dma_start(out=sb["vsb"][:, sl, :],
                               in_=VT[:, 32 * g:32 * (g + 1), :]).then_inc(sems["ldp"], 16)
                sync.dma_start(out=sb["ssb"][:, sl, :],
                               in_=ST[:, 32 * g:32 * (g + 1), :]).then_inc(sems["ldp"], 16)
                sync.wait_ge(sems["stq2"], (32 if debug else 16) * 4 * min(NM, g // 16 + 1))
                sync.dma_start(out=sb["qisb"][:, sl, :, :],
                               in_=QTD[:, :, 32 * g:32 * (g + 1)].rearrange("c p t -> p c t")
                               ).then_inc(sems["ldp"], 16)
        cnt["ldp"] += 80 * NSCC
        em.emit("sync", c_loads)

        psb_n = {"pe": 0}

        def psb_wait(t):
            k = psb_n["pe"]
            if k > 0:
                t.wait_ge(sems["sEV"], k)
            psb_n["pe"] = k + 1

        def c_pe(t):
            t.wait_ge(sems["vQ"], cnt["vQ"])
            t.wait_ge(sems["sKV1"], cnt["sKV1"])
            t.wait_ge(sems["sGT1"], cnt["sGT1"])
            t.wait_ge(sems["sG1"], cnt["sG1"])
            t.wait_ge(sems["ld"], 64)
            t.wait_ge(sems["gInit"], 1)
            for i in range(3):
                rhs = (sb["lnw_s"], sb["lnb_s"], sb["nrmw_s"])[i]
                t.matmul(psA[:, 0:128], CS[0:4, 80:208], rhs[:],
                         start=True, stop=True).then_inc(sems["tPR"], 1)
                t.wait_ge(sems["sPR"], i + 1)
            for g in range(NSCC + 1):
                if g < NSCC:
                    sl = g % NPREP
                    t.wait_ge(sems["vPR"], 3 * g + 2)
                    psb_wait(t)
                    t.transpose(psB[:], sb["kisb"][:, sl, :], sb["ident_t"][:]).then_inc(sems["tPR"], 1)
                    psb_wait(t)
                    t.transpose(psB[:], sb["qisb"][:, sl, :, :].rearrange("p c t -> p (c t)"),
                                sb["ident_t"][:]).then_inc(sems["tPR"], 1)
                if g >= 1:
                    gg = g - 1
                    sl = gg % NPREP
                    for par in (0, 1):
                        ch = 2 * gg + par
                        if ch >= CP:
                            break
                        t.wait_ge(sems["sPR"], 3 + 4 * gg + 4)
                        t.wait_ge(sems["sSC"], ch)
                        if ch == 0:
                            t.wait_ge(sems["vInit"], 2)
                        t.wait_ge(sems["sE"], ch)
                        mm = None
                        for p in range(4):
                            mm = t.matmul(psCH[32 * p:32 * p + 32, 0:128],
                                          sb["kitsb"][:, sl, 32 * p:32 * p + 32],
                                          sb["S_sb"][:, ch % 2, p, :], start=True, stop=True,
                                          tile_position=(0, 32 * p))
                        mm.then_inc(sems["tKH"], 1)
                        mm = None
                        for p in range(4):
                            mm = t.matmul(psA[32 * p:32 * p + 32, 0:32],
                                          sb["kitsb"][:, sl, 32 * p:32 * p + 32],
                                          sb["qisb"][:, sl, p, :], start=True, stop=True,
                                          tile_position=(0, 32 * p))
                        mm.then_inc(sems["tA"], 1)
                        t.wait_ge(sems["vBD"], ch + 1)
                        t.matmul(psS[:], sb["kesb"][:, sl, :], sb["ubd"][:],
                                 start=(ch == 0), stop=True,
                                 skip_group_check=(ch > 0)).then_inc(sems["tSU"], 1)
                        t.wait_ge(sems["vAT"], ch + 1)
                        if gg >= 1:
                            t.wait_ge(sems["sOH"], 2 * (gg - 1) + par + 1)
                        mm = None
                        for p in range(4):
                            t.matmul(psQHb[par][32 * p:32 * p + 32, 0:128],
                                     sb["atm"][32 * p:32 * p + 32, :],
                                     sb["usb"][32 * p:32 * p + 32, :],
                                     start=True, stop=False,
                                     tile_position=(32 * p, 32 * p))
                            mm = t.matmul(psQHb[par][32 * p:32 * p + 32, 0:128],
                                          sb["qisb"][:, sl, p, :],
                                          sb["S_sb"][:, ch % 2, p, :], start=False, stop=True,
                                          tile_position=(0, 32 * p))
                        mm.then_inc(sems["tCU"], 1)
                    t.wait_ge(sems["vOG"], gg + 1)
                    psb_wait(t)
                    t.transpose(psB[:], sb["ogsb"][:], sb["ident_t"][:]).then_inc(sems["tOG"], 1)
        cnt["tPR"] += 3 + 2 * NSCC
        cnt["tKH"] += CP
        cnt["tA"] += CP
        cnt["tCU"] += CP
        cnt["tSU"] += CP
        cnt["tOG"] += NSCC
        em.emit("tensor", c_pe)

        scb_n = [0]
        sr_n = [0]
        sr_at_scan = {}
        sr_at_o = {}

        def c_act(s):
            for i, dstn in enumerate(("w_bt", "b_bt", "nw_bt")):
                s.wait_ge(sems["tPR"], i + 1)
                s.activation(sb[dstn][:], psA[:, 0:128], AF.Copy).then_inc(sems["sPR"], 1)
            for g in range(NSCC + 1):
                if g < NSCC:
                    sl = g % NPREP
                    s.wait_ge(sems["vPR"], 3 * g + 1)
                    s.activation(sb["ccol"][:, sl, 3:4], sb["ccol"][:, sl, 2:3], AF.Ln,
                                 bias=CS[:, C_EPSI:C_EPSI + 1]).then_inc(sems["sPR"], 1)
                    s.wait_ge(sems["sPR"], 3 + 4 * g + 1)
                    s.activation(sb["ccol"][:, sl, 0:1], sb["ccol"][:, sl, 3:4], AF.Exp,
                                 scale=-0.5).then_inc(sems["sPR"], 1)
                    s.wait_ge(sems["tPR"], 3 + 2 * g + 1)
                    s.activation(sb["kitsb"][:, sl, :], psB[:], AF.Copy).then_inc(sems["sPR"], 1)
                    s.activation(sb["scr"][:, 0:1], sb["kitsb"][:, sl, 0:1], AF.Copy).then_inc(sems["sEV"], 1)
                    s.wait_ge(sems["tPR"], 3 + 2 * g + 2)
                    s.activation(sb["qchsb"][:, sl, :], psB[:], AF.Copy).then_inc(sems["sPR"], 1)
                    s.activation(sb["scr"][:, 0:1], sb["qchsb"][:, sl, 0:1], AF.Copy).then_inc(sems["sEV"], 1)
                if g >= 1:
                    gg = g - 1
                    sl = gg % NPREP
                    for par in (0, 1):
                        ch = 2 * gg + par
                        if ch >= CP:
                            break
                        s.wait_ge(sems["tKH"], ch + 1)
                        if ch >= 1:
                            s.wait_ge(sems["vU"], ch)
                        s.activation(sb["khsb"][:], psCH[:, 0:128], AF.Copy).then_inc(sems["sE"], 1)
                        s.wait_ge(sems["vR"], ch + 1)
                        if ch >= 1:
                            s.wait_ge(sems["vU"], ch)
                        s.activation(sb["kcol"][:, 15:16], sb["kcol"][:, 0:1], AF.Ln,
                                     scale=1.0 / 128,
                                     bias=CS[:, C_EPSI:C_EPSI + 1]).then_inc(sems["sCB"], 1)
                        scb_n[0] += 1
                        s.wait_ge(sems["sCB"], scb_n[0])
                        s.activation(sb["kcol"][:, 12:13], sb["kcol"][:, 15:16], AF.Exp,
                                     scale=-0.5).then_inc(sems["sCB"], 1)
                        scb_n[0] += 1
                        s.wait_ge(sems["sCB"], scb_n[0])
                        s.activation(sb["kcol"][:, 13:14], sb["kcol"][:, 12:13],
                                     AF.Square).then_inc(sems["sR"], 1)
                        sr_n[0] += 1
                        sr_at_scan[ch] = sr_n[0]
                        s.wait_ge(sems["tA"], ch + 1)
                        s.activation(sb["ats"][:], psA[:, 0:32], AF.Copy,
                                     scale=sb["ccol"][:, sl, 4:5]).then_inc(sems["sAT"], 1)
                        s.wait_ge(sems["tSU"], ch + 1)
                        if ch >= 1:
                            s.wait_ge(sems["tCU"], ch)
                        s.activation(sb["S_sb"][:, (ch + 1) % 2, :, :].rearrange("p a b -> p (a b)"),
                                     psS[:], AF.Copy).then_inc(sems["sSC"], 1)
                        if debug and ch < 8:
                            s.wait_ge(sems["vU"], ch + 1)
                            s.dma_start(out=DBG_KH[ch], in_=sb["khsb"][:]).then_inc(sems["stDG"], 16)
                            s.dma_start(out=DBG_U[ch], in_=sb["usb"][:]).then_inc(sems["stDG"], 16)
                            s.dma_start(out=DBG_KC[ch], in_=sb["kcol"][:]).then_inc(sems["stDG"], 16)
                            s.dma_start(out=DBG_CC[ch], in_=sb["ccsb"][:, sl, :]).then_inc(sems["stDG"], 16)
                            s.dma_start(out=DBG_CL[ch][:, 0:5], in_=sb["ccol"][:, sl, 0:5]).then_inc(sems["stDG"], 16)
                            s.wait_ge(sems["sSC"], ch + 1)
                            s.dma_start(out=DBG_SS[ch],
                                        in_=sb["S_sb"][:, (ch + 1) % 2, :, :].rearrange("p a b -> p (a b)")).then_inc(sems["stDG"], 16)
                    s.wait_ge(sems["tCU"], 2 * gg + 2)
                    s.activation(sb["ohsb"][:, 0, :], psQHb[0][:, 0:128], AF.Copy).then_inc(sems["sOH"], 1)
                    s.activation(sb["ohsb"][:, 1, :], psQHb[1][:, 0:128], AF.Copy).then_inc(sems["sOH"], 1)
                    s.wait_ge(sems["ccx"], 2 * gg + 1)
                    s.activation(sb["ocol"][:, 2:3], sb["ocol"][:, 0:1], AF.Ln,
                                 scale=1.0 / 128, bias=CS[:, C_EPSI:C_EPSI + 1]).then_inc(sems["sCB"], 1)
                    scb_n[0] += 1
                    s.wait_ge(sems["sCB"], scb_n[0])
                    s.activation(sb["ocol"][:, 1:2], sb["ocol"][:, 2:3], AF.Exp,
                                 scale=-0.5).then_inc(sems["sR"], 1)
                    sr_n[0] += 1
                    sr_at_o[(gg, 1)] = sr_n[0]
                    s.wait_ge(sems["ccx"], 2 * gg + 2)
                    s.activation(sb["ocol"][:, 2:3], sb["ocol"][:, 3:4], AF.Ln,
                                 scale=1.0 / 128, bias=CS[:, C_EPSI:C_EPSI + 1]).then_inc(sems["sCB"], 1)
                    scb_n[0] += 1
                    s.wait_ge(sems["sCB"], scb_n[0])
                    s.activation(sb["ocol"][:, 3:4], sb["ocol"][:, 2:3], AF.Exp,
                                 scale=-0.5).then_inc(sems["sR"], 1)
                    sr_n[0] += 1
                    sr_at_o[(gg, 2)] = sr_n[0]
                    s.wait_ge(sems["tOG"], gg + 1)
                    if gg >= 1:
                        s.wait_ge(sems["stOG"], 16 * gg)
                    s.activation(sb["ogts"][:], psB[:], AF.Copy).then_inc(sems["sEV"], 1)
                    blk = gg // 32
                    s.dma_start(out=OGT[blk].rearrange("(p e) (s u) -> e p s u", e=128, u=32)[:, :, gg % 32, :],
                                in_=sb["ogts"][:].rearrange("p (c u) -> p c u", u=32)
                                ).then_inc(sems["stOG"], 16)
                    if debug:
                        s.dma_start(out=dbg["dbg_ogt"][blk].rearrange("(p e) (s u) -> e p s u", e=128, u=32)[:, :, gg % 32, :],
                                    in_=sb["ogts"][:].rearrange("p (c u) -> p c u", u=32)
                                    ).then_inc(sems["stDG"], 16)
            if debug:
                s.wait_ge(sems["sSC"], CP)
                s.dma_start(out=DBG_S, in_=sb["S_sb"][:, CP % 2, :, :].rearrange("p a b -> p (a b)")
                            ).then_inc(sems["stDG"], 16)

        cnt["sPR"] += 3 + 4 * NSCC
        cnt["sE"] += CP
        cnt["sCB"] += 2 * CP + 2 * NSCC
        cnt["sR"] += CP + 2 * NSCC
        cnt["sAT"] += CP
        cnt["sSC"] += CP
        cnt["sOH"] += 2 * NSCC
        cnt["sEV"] += 3 * NSCC
        cnt["stOG"] += 16 * NSCC
        if debug:
            cnt["stDG"] += 16 * (NSCC + 6 * min(CP, 8) + 1)
        em.emit("scalar", c_act)

        def c_vec(v):
            for g in range(NSCC + 1):
                if g < NSCC:
                    sl = g % NPREP
                    v.wait_ge(sems["ldp"], 64 * (g + 1))
                    if g >= NPREP:
                        v.wait_ge(sems["vU"], 2 * (g - NPREP) + 2)
                    v.tensor_scalar(sb["ccol"][:, sl, 4:5], sb["krsb"][:, sl, 128:129], 2.0, None, ALU.mult)
                    v.scalar_tensor_tensor(sb["scr"][:], sb["krsb"][:, sl, 0:128], 1.0,
                                           sb["krsb"][:, sl, 0:128], ALU.mult, ALU.mult)
                    v.reduce_sum(sb["ccol"][:, sl, 2:3], sb["scr"][:],
                                 axis=mybir.AxisListType.X).then_inc(sems["vPR"], 1)
                    v.wait_ge(sems["sPR"], 3 + 4 * g + 2)
                    v.tensor_scalar(sb["kisb"][:, sl, :], sb["krsb"][:, sl, 0:128],
                                    sb["ccol"][:, sl, 0:1], None, ALU.mult)
                    v.tensor_scalar(sb["kesb"][:, sl, :], sb["kisb"][:, sl, :],
                                    sb["ccol"][:, sl, 4:5], None, ALU.mult).then_inc(sems["vPR"], 1)
                    v.tensor_sub(sb["vmk"][:], sb["vsb"][:, sl, :], sb["kisb"][:, sl, :])
                    v.scalar_tensor_tensor(sb["scr"][:], sb["vmk"][:], -1.0, sb["w_bt"][:],
                                           ALU.mult, ALU.mult)
                    v.tensor_add(sb["ccsb"][:, sl, :], sb["scr"][:], sb["wbb_bt"][:])
                    v.tensor_scalar(sb["cc128"][:, sl, :], sb["ccsb"][:, sl, :],
                                    1.0 / 128, None, ALU.mult)
                    v.reduce_sum(sb["ccol"][:, sl, 1:2], sb["cc128"][:, sl, :],
                                 axis=mybir.AxisListType.X).then_inc(sems["vPR"], 1)
                if g >= 1:
                    gg = g - 1
                    sl = gg % NPREP
                    for par in (0, 1):
                        ch = 2 * gg + par
                        if ch >= CP:
                            break
                        mask = CS[:, C_MSKE + par:C_MSKE + par + 1]
                        am0 = C_AME if par == 0 else C_AMO
                        amask = CS[:, am0:am0 + 32]
                        v.wait_ge(sems["sE"], ch + 1)
                        v.scalar_tensor_tensor(sb["redb"][:, 0, :], sb["khsb"][:], 1.0,
                                               sb["khsb"][:], ALU.mult, ALU.mult)
                        v.scalar_tensor_tensor(sb["redb"][:, 1, :], sb["khsb"][:], 1.0,
                                               sb["w2n_bt"][:], ALU.mult, ALU.mult)
                        v.scalar_tensor_tensor(sb["redb"][:, 2, :], sb["redb"][:, 1, :], 1.0,
                                               sb["khsb"][:], ALU.mult, ALU.mult)
                        v.scalar_tensor_tensor(sb["redb"][:, 3, :], sb["khsb"][:], 1.0,
                                               sb["cc128"][:, sl, :], ALU.mult, ALU.mult)
                        v.reduce_sum(sb["kcol"][:, 0:4], sb["redb"][:],
                                     axis=mybir.AxisListType.X).then_inc(sems["vR"], 1)
                        v.wait_ge(sems["sAT"], ch + 1)
                        if ch >= 1:
                            v.wait_ge(sems["tCU"], ch)
                        v.tensor_tensor(sb["atm"][:], sb["ats"][:], amask, ALU.mult).then_inc(sems["vAT"], 1)
                        v.wait_ge(sems["sR"], sr_at_scan[ch])
                        kc = sb["kcol"]
                        v.tensor_scalar(kc[:, 5:6], kc[:, 2:3], -1.0 / 128, None, ALU.mult)
                        v.tensor_scalar(kc[:, 6:7], kc[:, 1:2], -1.0 / 128, None, ALU.mult)
                        v.tensor_tensor(kc[:, 7:8], kc[:, 12:13], mask, ALU.mult)
                        v.tensor_tensor(kc[:, 8:9], kc[:, 13:14], mask, ALU.mult)
                        v.tensor_tensor(kc[:, 9:10], kc[:, 3:4], kc[:, 12:13], ALU.mult)
                        lastd1 = v.tensor_tensor(kc[:, 11:12], kc[:, 12:13],
                                                 sb["ccol"][:, sl, 1:2], ALU.mult)
                        lastd1.then_inc(sems["vCX"], 1)
                        vcx_n[0] += 1
                        v.wait_ge(sems["vCX"], vcx_n[0])
                        v.scalar_tensor_tensor(kc[:, 10:11], kc[:, 5:6], kc[:, 13:14], kc[:, 9:10],
                                               ALU.mult, ALU.add)
                        v.scalar_tensor_tensor(kc[:, 4:5], kc[:, 6:7], kc[:, 13:14], kc[:, 11:12],
                                               ALU.mult, ALU.add).then_inc(sems["vCX"], 1)
                        vcx_n[0] += 1
                        v.wait_ge(sems["vCX"], vcx_n[0])
                        v.tensor_tensor(kc[:, 15:16], kc[:, 4:5], mask, ALU.mult)
                        v.tensor_scalar(kc[:, 14:15], kc[:, 7:8], -1.0, None,
                                        ALU.mult).then_inc(sems["vCX"], 1)
                        vcx_n[0] += 1
                        v.scalar_tensor_tensor(sb["scr"][:], sb["khsb"][:], kc[:, 10:11],
                                               sb["redb"][:, 1, :], ALU.mult, ALU.add)
                        v.wait_ge(sems["vCX"], vcx_n[0])
                        v.tensor_scalar(sb["t1sb"][:], sb["ccsb"][:, sl, :], kc[:, 14:15],
                                        kc[:, 15:16], ALU.mult, ALU.add)
                        if ch >= 1:
                            v.wait_ge(sems["tSU"], ch)
                        v.scalar_tensor_tensor(sb["usb"][:], sb["scr"][:], kc[:, 8:9],
                                               sb["t1sb"][:], ALU.mult, ALU.add).then_inc(sems["vU"], 1)
                        cp = None
                        for p in range(4):
                            cp = v.tensor_copy(sb["ubd"][32 * p:32 * p + 32, 128 * p:128 * (p + 1)],
                                               sb["usb"][32 * p:32 * p + 32, :])
                        cp.then_inc(sems["vBD"], 1)
                    v.wait_ge(sems["sOH"], 2 * gg + 2)
                    v.tensor_sub(sb["scr"][:], sb["ohsb"][:, 0, :], sb["ohsb"][:, 1, :])
                    v.scalar_tensor_tensor(sb["mrg"][:], sb["scr"][:],
                                           CS[:, C_MSKE:C_MSKE + 1], sb["ohsb"][:, 1, :],
                                           ALU.mult, ALU.add)
                    v.scalar_tensor_tensor(sb["scr"][:], sb["mrg"][:], 1.0, sb["mrg"][:],
                                           ALU.mult, ALU.mult)
                    v.reduce_sum(sb["ocol"][:, 0:1], sb["scr"][:],
                                 axis=mybir.AxisListType.X).then_inc(sems["ccx"], 1)
                    v.wait_ge(sems["sR"], sr_at_o[(gg, 1)])
                    v.tensor_scalar(sb["scr"][:], sb["mrg"][:], sb["ocol"][:, 1:2], None,
                                    ALU.mult)
                    v.scalar_tensor_tensor(sb["scr"][:], sb["scr"][:], 1.0, sb["w_bt"][:],
                                           ALU.mult, ALU.mult)
                    v.tensor_add(sb["scr"][:], sb["scr"][:], sb["b_bt"][:])
                    v.tensor_add(sb["scr"][:], sb["scr"][:], sb["qchsb"][:, sl, :])
                    v.tensor_tensor(sb["scr"][:], sb["scr"][:], sb["ssb"][:, sl, :], ALU.mult)
                    v.scalar_tensor_tensor(sb["t1sb"][:], sb["scr"][:], 1.0, sb["scr"][:],
                                           ALU.mult, ALU.mult)
                    v.reduce_sum(sb["ocol"][:, 3:4], sb["t1sb"][:],
                                 axis=mybir.AxisListType.X).then_inc(sems["ccx"], 1)
                    v.wait_ge(sems["sR"], sr_at_o[(gg, 2)])
                    v.tensor_scalar(sb["scr"][:], sb["scr"][:], sb["ocol"][:, 3:4], None,
                                    ALU.mult)
                    if gg >= 1:
                        v.wait_ge(sems["tOG"], gg)
                    v.tensor_tensor(sb["ogsb"][:], sb["scr"][:], sb["nw_bt"][:],
                                    ALU.mult).then_inc(sems["vOG"], 1)
                    v.nop().then_inc(sems["vOG2"], 1)

        cnt["vPR"] += 3 * NSCC
        cnt["vCX"] += 3 * CP
        cnt["vR"] += CP
        cnt["vAT"] += CP
        cnt["vU"] += CP
        cnt["vBD"] += CP
        cnt["ccx"] += 2 * NSCC
        cnt["vOG"] += NSCC
        cnt["vOG2"] += NSCC
        em.emit("vector", c_vec)

    S("xqt_t", (128, 4, KC, 128), BF16)
    S("ogt_t", (128, 2, 4, 128), BF16)
    S("wqt4", (128, KC, 4, 128), BF16)
    S("ycols", (128, 8), F32)
    S("t2col", (128, 8), F32)
    S("outsb", (128, 2, 512), F32)
    S("xqf", (128, D), F32)
    S("xqi", (128, D), I32)
    S("xqd2", (128, D), F32)
    sb["wfq"] = sb["xn_tok"]

    S("wfcol", (128, 64), F32)
    S("wfsc", (128, 8), F32)
    S("wfsb", (128, 144), BF16)

    for s_ in ("ldwos", "ldog", "stRS", "sX2", "tWO", "ldrs", "ldr", "vX2",
               "ldf", "sWO", "tF", "vWO", "ldw2", "vQ2", "vT", "sT", "sT2",
               "vQX", "stX", "ldq", "ldff", "ldf2", "tFF", "vFF", "stO"):
        em.sem(s_)

    x2f = sb["xnt_m"][:].rearrange("p a k f -> p (a k f)")
    wov2 = sb["wq_b"][:].rearrange("p k c -> p (k c)")
    x2p = sb["g1sb"]
    LW2 = 3 if debug else 2
    DX = 2 if debug else 1
    psW = [psQ[0], psQ[1], psKV[0], psKV[1]]

    if has("T"):
        def t_gload(g):
            g.wait_ge(sems["tQ"], cnt["tQ"])
            wsr = WOS.rearrange("(c p) n -> c p n", p=128)
            for fc in range(4):
                g.dma_start(out=wov2[:, 2048 * fc:2048 * (fc + 1)],
                            in_=wsr[fc]).then_inc(sems["ldwos"], 16)

            def wfld(i):
                if i >= 2:
                    g.wait_ge(sems["sWO"], i - 1)
                g.dma_start(out=sb["xqf" if i % 2 == 0 else "xqd2"][:],
                            in_=WF.rearrange("(n p) d -> n p d", p=128)[i]
                            ).then_inc(sems["ldf"], 16)
            if has("F"):
                for i in range(5):
                    wfld(i)
            g.wait_ge(sems["stRS"], 16 * 128)
            g.collective_compute(
                "ReduceScatter", ALU.add,
                replica_groups=[[0, 1, 2, 3], [4, 5, 6, 7]],
                ins=[RSI.rearrange("b q d -> (b q d)").opt()],
                outs=[RSO.rearrange("q d -> (q d)").opt()],
            ).then_inc(sems["ccx"], 1)
            if has("F"):
                for i in range(5, KC):
                    wfld(i)
        cnt["ldwos"] += 64
        cnt["ccx"] += 1
        ccs_done = cnt["ccx"]
        if has("F"):
            cnt["ldf"] += 16 * 16
        em.emit("gpsimd", t_gload)

        def t_sync(sync):
            for tt in range(32):
                if tt >= 2:
                    sync.wait_ge(sems["tWO"], 4 * (tt - 1))
                sync.wait_ge(sems["stOG"], 16 * 4 * (tt + 1))
                sync.dma_start(out=sb["ogt_t"][:, tt % 2, :, :],
                               in_=OGT[tt // 8][:, 128 * (tt % 8):128 * (tt % 8) + 128]
                               .rearrange("(c p) t -> p c t", p=128)
                               ).then_inc(sems["ldog"], 16)
            sync.wait_ge(sems["ccx"], ccs_done)
            rsr = RSO.rearrange("(n p) d -> n p d", p=128)
            xqr = XQ.rearrange("(n p) d -> n p d", p=128)
            for t in range(8):
                sync.dma_start(out=x2f[:, 2048 * t:2048 * (t + 1)],
                               in_=rsr[t]).then_inc(sems["ldrs"], 16)
                if t >= 2:
                    sync.wait_ge(sems["vX2"], t - 1)
                sync.dma_start(out=sb["xtile"][:, t % 2, :],
                               in_=xqr[t]).then_inc(sems["ldr"], 16)
            def ldq1(t):
                sync.wait_ge(sems["stX"], 16 * DX * (t + 1))
                if t >= 4:
                    sync.wait_ge(sems["tFF"], 64)
                sync.dma_start(out=sb["xqt_t"][:, t % 4, :, :],
                               in_=XQD[t].rearrange("p (k f) -> p k f", f=128),
                               transpose=True).then_inc(sems["ldq"], 16)

            def wqt1(fg, a):
                nq = fg % 4
                if fg >= 1 and a == 0:
                    sync.wait_ge(sems["tFF"], 16 * fg)
                sync.wait_ge(sems["ldw2"], 16 * LW2 * (4 * nq + a + 1))
                sync.dma_start(out=sb["wqt4"][:, :, a, :],
                               in_=WQR[4 * nq + a].rearrange("p (k f) -> p k f", f=128),
                               transpose=True).then_inc(sems["ldff"], 16)
            for t in range(4):
                ldq1(t)
            for fg in range(4):
                for a in range(4):
                    wqt1(fg, a)
            for t in range(4, 8):
                ldq1(t)
            for fg in range(4, 8):
                for a in range(4):
                    wqt1(fg, a)
        cnt["ldog"] += 16 * 32
        cnt["ldrs"] += 16 * 8
        cnt["ldr"] += 16 * 8
        cnt["ldq"] += 16 * 8
        cnt["ldff"] += 16 * 4 * 8
        em.emit("sync", t_sync)

        def wo_pe(t):
            t.wait_ge(sems["ldwos"], 64)
            for grp in range(128):
                tt, nq = grp // 4, grp % 4
                if nq == 0:
                    t.wait_ge(sems["ldog"], 16 * (tt + 1))
                if grp >= 4:
                    t.wait_ge(sems["sX2"], grp - 3)
                for fc in range(4):
                    mm = t.matmul(psW[grp % 4][:, 0:512],
                                  sb["ogt_t"][:, tt % 2, fc, :],
                                  wov2[:, 2048 * fc + 512 * nq:2048 * fc + 512 * (nq + 1)],
                                  start=(fc == 0), stop=(fc == 3))
                mm.then_inc(sems["tWO"], 1)
        cnt["tWO"] += 128
        em.emit("tensor", wo_pe)

        def wo_act(s):
            for grp in range(128):
                tt, nq = grp // 4, grp % 4
                s.wait_ge(sems["tWO"], grp + 1)
                if grp >= 2:
                    s.wait_ge(sems["stRS"], 16 * (grp - 1))
                s.activation(x2p[:, grp % 2, :], psW[grp % 4][:, 0:512],
                             AF.Copy).then_inc(sems["sX2"], 1)
                s.dma_start(out=RSI[tt // 8][128 * (tt % 8):128 * (tt % 8) + 128,
                                            512 * nq:512 * (nq + 1)],
                            in_=x2p[:, grp % 2, :]).then_inc(sems["stRS"], 16)
        cnt["sX2"] += 128
        cnt["stRS"] += 16 * 128
        em.emit("scalar", wo_act)

        def t_vec(v):
            for t in range(8):
                v.wait_ge(sems["ldrs"], 16 * (t + 1))
                v.wait_ge(sems["ldr"], 16 * (t + 1))
                v.tensor_tensor(x2f[:, 2048 * t:2048 * (t + 1)],
                                x2f[:, 2048 * t:2048 * (t + 1)],
                                sb["xtile"][:, t % 2, :],
                                ALU.add).then_inc(sems["vX2"], 1)
        cnt["vX2"] += 8
        em.emit("vector", t_vec)

    if has("F"):
        def f1_act(s):
            red512 = sb["redb"][:].rearrange("p a f -> p (a f)")
            for i in range(KC):
                s.wait_ge(sems["ldf"], 16 * (i + 1))
                buf = sb["xqf" if i % 2 == 0 else "xqd2"][:]
                for j in range(4):
                    mm = s.activation(red512, buf[:, 512 * j:512 * (j + 1)], AF.Abs,
                                      accum_out=sb["wfcol"][:, 4 * i + j:4 * i + j + 1])
                mm.then_inc(sems["sWO"], 1)
        cnt["sWO"] += KC
        em.emit("scalar", f1_act)

        def f1_pe(t):
            t.wait_ge(sems["vWO"], 4)
            t.matmul(psGT[1][0:1, 0:1], sb["wfsb"][:, 1:2], sb["wfsb"][:, 0:1],
                     start=True, stop=True).then_inc(sems["tF"], 1)
            t.wait_ge(sems["vWO"], 8)
            t.matmul(psGT[1][:, 1:3], sb["wfsb"][0:1, 8:136], sb["wfsb"][0:1, 2:4],
                     start=True, stop=True).then_inc(sems["tF"], 1)
        cnt["tF"] += 2
        em.emit("tensor", f1_pe)

        def f1_vec(v):
            v.wait_ge(sems["ld"], 64)
            v.tensor_copy(sb["wfsb"][:, 1:2], CS[:, C_ONE:C_ONE + 1]).then_inc(sems["vWO"], 1)
            v.tensor_copy(sb["wfsb"][0:1, 8:136], CS[0:1, 208:336]).then_inc(sems["vWO"], 1)
            v.wait_ge(sems["sWO"], KC)
            v.reduce_sum(sb["wfsc"][:, 0:1], sb["wfcol"][:],
                         axis=mybir.AxisListType.X).then_inc(sems["vWO"], 1)
            v.wait_ge(sems["vWO"], 3)
            v.tensor_copy(sb["wfsb"][:, 0:1], sb["wfsc"][:, 0:1]).then_inc(sems["vWO"], 1)
            v.wait_ge(sems["tF"], 1)
            v.tensor_scalar(sb["wfsc"][0:1, 1:2], psGT[1][0:1, 0:1], 1.0 / (D * D),
                            None, ALU.mult).then_inc(sems["vWO"], 1)
            v.wait_ge(sems["vWO"], 5)
            v.tensor_scalar(sb["wfsc"][0:1, 2:3], sb["wfsc"][0:1, 1:2], 1e-6, None,
                            ALU.max).then_inc(sems["vWO"], 1)
            v.wait_ge(sems["vWO"], 6)
            v.reciprocal(sb["wfsc"][0:1, 3:4], sb["wfsc"][0:1, 2:3]).then_inc(sems["vWO"], 1)
            v.wait_ge(sems["vWO"], 7)
            v.tensor_copy(sb["wfsb"][0:1, 2:4], sb["wfsc"][0:1, 2:4]).then_inc(sems["vWO"], 1)
            v.wait_ge(sems["tF"], 2)
            v.tensor_copy(sb["wfsc"][:, 4:6], psGT[1][:, 1:3]).then_inc(sems["vWO"], 1)
        cnt["vWO"] += 9
        em.emit("vector", f1_vec)

        def f2_g(g):
            g.wait_ge(sems["sWO"], KC)
            g.wait_ge(sems["vX2"], 8)
            for i in range(KC):
                if i >= 2:
                    g.wait_ge(sems["vQ2"], i - 1)
                g.dma_start(out=sb["xtile"][:, i % 2, :],
                            in_=WF.rearrange("(n p) d -> n p d", p=128)[i]).then_inc(sems["ldw2"], 16)
                g.wait_ge(sems["vQ2"], i + 1)
                g.dma_start(out=WQR[i], in_=sb["wfq"][:, i % 2, :]).then_inc(sems["ldw2"], 16)
                if debug:
                    g.dma_start(out=dbg["dbg_wqt"][i], in_=sb["wfq"][:, i % 2, :]).then_inc(sems["ldw2"], 16)
        cnt["ldw2"] += 16 * KC * LW2
        em.emit("gpsimd", f2_g)

        def f2_vec(v):
            v.wait_ge(sems["vWO"], 9)
            for i in range(KC):
                v.wait_ge(sems["ldw2"], 16 * (LW2 * i + 1))
                v.tensor_scalar(sb["xqd2"][:], sb["xtile"][:, i % 2, :],
                                sb["wfsc"][:, 5:6], None, ALU.mult)
                v.tensor_scalar(sb["xqf"][:], sb["xqd2"][:], 0.5, None, ALU.is_gt)
                v.scalar_tensor_tensor(sb["wfq"][:, i % 2, :], sb["xqd2"][:], -0.5,
                                       sb["xqf"][:], ALU.is_lt,
                                       ALU.subtract).then_inc(sems["vQ2"], 1)
        cnt["vQ2"] += KC
        em.emit("vector", f2_vec)

    if has("T"):
        def t2_vec(v):
            v.wait_ge(sems["ldw2"], 16 * KC * LW2)
            for t in range(8):
                x2t = x2f[:, 2048 * t:2048 * (t + 1)]
                sl = t % 2
                for j in range(4):
                    v.bn_stats(sb["bnst"][:, 6 * j:6 * (j + 1)],
                               x2t[:, 512 * j:512 * (j + 1)]).then_inc(sems["vT"], 1)
                v.wait_ge(sems["vT"], 5 * t + 4)
                v.bn_aggr(sb["p0col"][:, sl, 0:2], sb["bnst"][:]).then_inc(sems["vT"], 1)
                v.wait_ge(sems["sT"], 2 * (t + 1))
                v.tensor_scalar(sb["xqf"][:], x2t, sb["p0col"][:, sl, 0:1],
                                sb["p0col"][:, sl, 2:3], ALU.subtract,
                                ALU.mult).then_inc(sems["vQX"], 1)
                v.wait_ge(sems["sT2"], t + 1)
                v.reduce_max(sb["t2col"][:, t:t + 1], sb["xqd2"][:],
                             axis=mybir.AxisListType.X).then_inc(sems["vQX"], 1)
                v.wait_ge(sems["vQX"], 12 * t + 2)
                v.tensor_scalar(sb["t2col"][:, t:t + 1], sb["t2col"][:, t:t + 1],
                                1e-8, None, ALU.max).then_inc(sems["vQX"], 1)
                v.wait_ge(sems["vQX"], 12 * t + 3)
                v.reciprocal(sb["p0col"][:, sl, 3:4],
                             sb["t2col"][:, t:t + 1]).then_inc(sems["vQX"], 1)
                v.wait_ge(sems["vQX"], 12 * t + 4)
                v.tensor_scalar(sb["ycols"][:, t:t + 1], sb["t2col"][:, t:t + 1],
                                sb["wfsc"][:, 4:5], 1.0 / 127, ALU.mult,
                                ALU.mult).then_inc(sems["vQX"], 1)
                v.tensor_scalar(sb["xtile"][:, 0, :], sb["xqf"][:],
                                sb["p0col"][:, sl, 3:4], 127.0, ALU.mult,
                                ALU.mult).then_inc(sems["vQX"], 1)
                v.tensor_copy(sb["xqi"][:], sb["xtile"][:, 0, :]).then_inc(sems["vQX"], 1)
                v.tensor_copy(sb["xqd2"][:], sb["xqi"][:]).then_inc(sems["vQX"], 1)
                v.tensor_tensor(sb["xtile"][:, 0, :], sb["xtile"][:, 0, :],
                                sb["xqd2"][:], ALU.subtract).then_inc(sems["vQX"], 1)
                v.scalar_tensor_tensor(sb["xtile"][:, 1, :], sb["xtile"][:, 0, :], 0.5,
                                       sb["xqd2"][:], ALU.is_ge,
                                       ALU.add).then_inc(sems["vQX"], 1)
                v.scalar_tensor_tensor(sb["xqf"][:], sb["xtile"][:, 0, :], -0.5,
                                       sb["xtile"][:, 1, :], ALU.is_le,
                                       ALU.subtract).then_inc(sems["vQX"], 1)
                if t >= 2:
                    v.wait_ge(sems["stX"], 16 * DX * (t - 1))
                v.tensor_scalar(sb["wfq"][:, sl, :], sb["xqf"][:], 127.0, -127.0,
                                ALU.min, ALU.max).then_inc(sems["vQX"], 1)
        cnt["vT"] += 5 * 8
        cnt["vQX"] += 12 * 8
        em.emit("vector", t2_vec)

        def t2_act(s):
            for t in range(8):
                sl = t % 2
                s.wait_ge(sems["vT"], 5 * (t + 1))
                s.activation(sb["p0col"][:, sl, 3:4], sb["p0col"][:, sl, 1:2], AF.Ln,
                             bias=CS[:, C_EPSL:C_EPSL + 1]).then_inc(sems["sT"], 1)
                s.wait_ge(sems["sT"], 2 * t + 1)
                s.activation(sb["p0col"][:, sl, 2:3], sb["p0col"][:, sl, 3:4], AF.Exp,
                             scale=-0.5).then_inc(sems["sT"], 1)
                s.wait_ge(sems["vQX"], 12 * t + 1)
                s.activation(sb["xqd2"][:], sb["xqf"][:], AF.Abs).then_inc(sems["sT2"], 1)
                s.wait_ge(sems["vQX"], 12 * (t + 1))
                s.dma_start(out=XQD[t], in_=sb["wfq"][:, sl, :]).then_inc(sems["stX"], 16)
                if debug:
                    s.dma_start(out=dbg["dbg_xq2"][t], in_=sb["wfq"][:, sl, :]).then_inc(sems["stX"], 16)
        cnt["sT"] += 2 * 8
        cnt["sT2"] += 8
        cnt["stX"] += 16 * 8 * DX
        em.emit("scalar", t2_act)

        def t3_pe(t):
            t.wait_ge(sems["sX2"], 128)
            for idx2 in range(128):
                th, nq, k = idx2 // 64, (idx2 // 16) % 4, idx2 % 16
                fg = idx2 // 16
                if k == 0:
                    t.wait_ge(sems["ldff"], 16 * 4 * (fg + 1))
                    t.wait_ge(sems["ldq"], 16 * 4 * (th + 1))
                    if fg >= 1:
                        t.wait_ge(sems["vFF"], 4 * fg)
                for ti in range(4):
                    mm = t.matmul(psW[ti][:, 0:512],
                                  sb["xqt_t"][:, ti, k, :],
                                  sb["wqt4"][:, k, :, :].rearrange("p a f -> p (a f)"),
                                  start=(k == 0), stop=(k == KC - 1))
                mm.then_inc(sems["tFF"], 1)
        cnt["tFF"] += 128
        em.emit("tensor", t3_pe)

        def t3_vec(v):
            for fg in range(8):
                th, nq = fg // 4, fg % 4
                v.wait_ge(sems["tFF"], 16 * (fg + 1))
                v.wait_ge(sems["vQX"], 12 * 4 * (th + 1))
                for ti in range(4):
                    tile = 4 * th + ti
                    j = 4 * fg + ti
                    if j >= 2:
                        v.wait_ge(sems["stO"], 16 * (j - 1))
                    v.scalar_tensor_tensor(
                        sb["outsb"][:, j % 2, :], psW[ti][:, 0:512],
                        sb["ycols"][:, tile:tile + 1],
                        x2f[:, 2048 * tile + 512 * nq:2048 * tile + 512 * (nq + 1)],
                        ALU.mult, ALU.add).then_inc(sems["vFF"], 1)
        cnt["vFF"] += 32
        em.emit("vector", t3_vec)

        def t3_st(s):
            for j in range(32):
                fg, ti = j // 4, j % 4
                th, nq = fg // 4, fg % 4
                tile = 4 * th + ti
                s.wait_ge(sems["vFF"], j + 1)
                s.dma_start(out=OUT[128 * tile:128 * (tile + 1), 512 * nq:512 * (nq + 1)],
                            in_=sb["outsb"][:, j % 2, :]).then_inc(sems["stO"], 16)
            if debug:
                s.dma_start(out=dbg["dbg_x2"], in_=x2f).then_inc(sems["stDG"], 16)
                s.dma_start(out=dbg["dbg_tc"][:, 0:8], in_=sb["t2col"][:]).then_inc(sems["stDG"], 16)
                s.dma_start(out=dbg["dbg_tc"][:, 8:16], in_=sb["ycols"][:]).then_inc(sems["stDG"], 16)
        cnt["stO"] += 16 * 32
        if debug:
            cnt["stDG"] += 16 * 3
        em.emit("scalar", t3_st)

    finals = {k: v for k, v in cnt.items()}

    def final_sync(sync):
        for k, v in finals.items():
            if v > 0:
                sync.wait_ge(sems[k], v)
    em.emit("sync", final_sync)

    em.build()
    ctx.close()
    return nc



def build_consts():
    c = np.zeros((128, 640), np.float32)
    c[:, C_ONE] = 1.0
    c[:, C_I128] = 1.0 / 128
    c[:, C_EPSI] = EPS_IN
    c[:, C_EPSL] = EPS_LN
    c[:, C_HALF] = 0.5
    c[:, C_NEG1] = -1.0
    c[:, C_127] = 127.0
    t = np.arange(128) % 32
    c[:, C_MSKE] = (t < 16).astype(np.float32)
    c[:, C_MSKO] = (t >= 16).astype(np.float32)
    for ci in range(32):
        even = (t < 16) & (ci < 16) & (t <= ci)
        odd = (t >= 16) & (ci >= 16) & ((t - 16) <= (ci - 16))
        c[:, C_AME + ci] = even.astype(np.float32)
        c[:, C_AMO + ci] = odd.astype(np.float32)
    c[:, 208:336] = 1.0
    m = np.arange(128)
    for j in range(4):
        c[j, 80:208] = (m // 32 == j).astype(np.float32)
    return c


def host_prep(inputs):
    x = np.asarray(inputs["L0_attn_x"])
    wq = np.asarray(inputs["L0_attn_Wq"])
    wk = np.asarray(inputs["L0_attn_Wk"])
    wv = np.asarray(inputs["L0_attn_Wv"])
    wo = np.asarray(inputs["L0_attn_Wo"])
    wbeta = np.asarray(inputs["L0_attn_Wbeta"])
    lnw = np.asarray(inputs["L0_attn_ln_w"])
    lnb = np.asarray(inputs["L0_attn_ln_b"])
    wg1 = np.asarray(inputs["L0_attn_Wg1"])
    wg2 = np.asarray(inputs["L0_attn_Wg2"])
    nrm = np.asarray(inputs["L0_attn_norm_w"])
    wf = np.asarray(inputs["L0_ffn_weight"])
    consts = build_consts()

    in_maps = []
    for c in range(8):
        b = c // 4
        hg = c % 4
        hsl = slice(4 * hg * Dh, 4 * (hg + 1) * Dh)
        heads = [4 * hg + j for j in range(HPC)]
        wkvb = np.concatenate(
            [wk[:, hsl], wv[:, hsl], wbeta[:, 4 * hg:4 * hg + 4]], axis=1)
        m = {
            "x": np.ascontiguousarray(x[b]),
            "xq": np.ascontiguousarray(x[b, QTOK * hg:QTOK * (hg + 1)]),
            "wq": np.ascontiguousarray(wq[:, hsl]),
            "wkvb": np.ascontiguousarray(wkvb),
            "wg1": np.ascontiguousarray(wg1),
            "wg2": np.ascontiguousarray(wg2[:, hsl]),
            "wos": np.ascontiguousarray(wo[hsl, :]),
            "wf": np.ascontiguousarray(wf),
            "lnw": np.ascontiguousarray(lnw[heads]),
            "lnb": np.ascontiguousarray(lnb[heads]),
            "nrmw": np.ascontiguousarray(nrm[hsl]).reshape(HPC, Dh),
            "consts": consts,
        }
        in_maps.append(m)
    return in_maps


def kernel(**inputs):
    in_maps = host_prep(inputs)
    nc = build_program(debug=DEBUG)
    res = run_bass_kernel_spmd(nc, in_maps, core_ids=list(range(8)))
    out = np.zeros((B, T, D), np.float32)
    for ci in range(8):
        b = ci // 4
        hg = ci % 4
        out[b, QTOK * hg:QTOK * (hg + 1)] = res.results[ci]["out"]
    return out
